# version 02
# baseline (speedup 1.0000x reference)
import contextlib
import numpy as np
import ml_dtypes
import concourse.bass as bass
import concourse.mybir as mybir
from concourse.bass_utils import run_bass_kernel_spmd

F32 = mybir.dt.float32
BF16 = mybir.dt.bfloat16
U32 = mybir.dt.uint32
AF = mybir.ActivationFunctionType
OP = mybir.AluOpType
AX = mybir.AxisListType

NCORE = 8
B, S, D = 2, 8192, 1024
NH, DH, NHP = 16, 64, 8
TPC, NT, GT = 2048, 16, 64
FF0 = 2816
E, CAP, HMOE = 8, 512, 5464
EPS = 1e-5
SQD = 1.0 / 8.0
SC1 = 1024 ** -0.5

_BASS_CACHE = {}


def _pos_tiles(p):
    return [4 * t + p for t in range(NT)]


def make_in_maps_l0(inputs):
    x = np.asarray(inputs["L0_attn_x"], np.float32)
    wq = np.ascontiguousarray(np.asarray(inputs["L0_attn_W_q"], np.float32))
    wk = np.ascontiguousarray(np.asarray(inputs["L0_attn_W_k"], np.float32))
    wv = np.ascontiguousarray(np.asarray(inputs["L0_attn_W_v"], np.float32))
    wo = np.ascontiguousarray(np.asarray(inputs["L0_attn_W_o"], np.float32))
    wg = np.asarray(inputs["L0_ffn_w_gate"], np.float32)
    wu = np.asarray(inputs["L0_ffn_w_up"], np.float32)
    wd = np.asarray(inputs["L0_ffn_w_down"], np.float32)
    ident = np.eye(128, dtype=np.float32)
    wgp = np.zeros((3072, 1024), np.float32); wgp[:2816] = wg
    wup = np.zeros((3072, 1024), np.float32); wup[:2816] = wu
    maps = []
    for c in range(NCORE):
        b, p = c // 4, c % 4
        xt = x[b].reshape(GT, 128, D)[_pos_tiles(p)].reshape(TPC, D).copy()
        bm = np.zeros((4, 128, 128), np.float32)
        for r in range(4):
            if r < p:
                bm[r] = 1.0
            elif r == p:
                bm[r] = (np.arange(128)[:, None] <= np.arange(128)[None, :]).astype(np.float32)
        maps.append(dict(
            x0=xt, wq=wq, wk=wk, wv=wv, wo=wo,
            wgs=np.ascontiguousarray(wgp[384 * c:384 * (c + 1)]),
            wus=np.ascontiguousarray(wup[384 * c:384 * (c + 1)]),
            wds=np.ascontiguousarray(wd[128 * c:128 * (c + 1)]),
            ident=ident, bmask=bm))
    return maps


def assemble_output(res, key):
    out = np.zeros((B, S, D), np.float32)
    for c in range(NCORE):
        b, p = c // 4, c % 4
        y = np.asarray(res[c][key]).reshape(NT, 128, D)
        out[b].reshape(GT, 128, D)[_pos_tiles(p)] = y
    return out


class Ctx:
    def __init__(self, nc, ex):
        self.nc = nc
        self.ex = ex
        self.sem_ex = ex
        self.sem = {}
        self._cnt = {}
        self.alias = {}

    def canon(self, name):
        return self.alias.get(name, name)

    @property
    def cnt(self):
        return _CntView(self)

    def S(self, name):
        name = self.canon(name)
        if name not in self.sem:
            self.sem[name] = self.sem_ex.enter_context(self.nc.semaphore(name))
            self._cnt[name] = 0
        return self.sem[name]

    def inc(self, instr, name, n=1):
        cn = self.canon(name)
        instr.then_inc(self.S(name), n)
        self._cnt[cn] += n
        return self._cnt[cn]

    def w(self, eng, name, val=None):
        cn = self.canon(name)
        self.S(name)
        eng.wait_ge(self.sem[cn], self._cnt[cn] if val is None else val)

    def sb(self, name, shape, dt):
        return self.ex.enter_context(self.nc.sbuf_tensor(name, shape, dt))

    def ps(self, name, shape, dt=F32):
        return self.ex.enter_context(self.nc.psum_tensor(name, shape, dt))


class _CntView:
    def __init__(self, ctx):
        self._ctx = ctx

    def __getitem__(self, k):
        return self._ctx._cnt[self._ctx.canon(k)]

    def __contains__(self, k):
        return self._ctx.canon(k) in self._ctx._cnt

    def get(self, k, d=None):
        return self._ctx._cnt.get(self._ctx.canon(k), d)

def cs(t, w):
    return slice(t * w, (t + 1) * w)


def emit_ln_tile(ctx, v, s, src_ap, xnf_ap, xnb_ap, st, t):
    v.tensor_scalar(out=st["scr"][:, 0:1024], in0=src_ap, scalar1=1.0, scalar2=None,
                    op0=OP.mult, op1=OP.add, accum_out=st["sumx"][:, t:t + 1])
    s.activation(out=st["scr2"][:, 0:1024], in_=src_ap, func=AF.Square,
                 accum_out=st["sumsq"][:, t:t + 1])


def emit_ln_stats(ctx, v, s, st, nt=NT):
    iv = 1.0 / 1024.0
    v.tensor_scalar(out=st["m"][:, 0:nt], in0=st["sumx"][:, 0:nt], scalar1=iv, scalar2=None, op0=OP.mult)
    v.tensor_scalar(out=st["msq"][:, 0:nt], in0=st["sumsq"][:, 0:nt], scalar1=iv, scalar2=None, op0=OP.mult)
    v.tensor_tensor(out=st["var"][:, 0:nt], in0=st["m"][:, 0:nt], in1=st["m"][:, 0:nt], op=OP.mult)
    v.tensor_tensor(out=st["var"][:, 0:nt], in0=st["msq"][:, 0:nt], in1=st["var"][:, 0:nt], op=OP.subtract)
    s.activation(out=st["lnv"][:, 0:nt], in_=st["var"][:, 0:nt], func=AF.Ln, bias=EPS)
    s.activation(out=st["rstd"][:, 0:nt], in_=st["lnv"][:, 0:nt], func=AF.Exp, scale=-0.5)
    v.scalar_tensor_tensor(out=st["mrn"][:, 0:nt], in0=st["m"][:, 0:nt], scalar=-1.0,
                           in1=st["rstd"][:, 0:nt], op0=OP.mult, op1=OP.mult)


def emit_fp4_tile(ctx, v, s, src_ap, out_ap, q, width, parts=128):
    nb = width // 16
    P = slice(0, parts)
    src3 = src_ap.rearrange("p (b s) -> p b s", s=16)

    def hs(instr):
        ctx.inc(instr, "fqh")
        ctx.w(v, "fqh")

    hs(v.tensor_reduce(out=q["blk"][P, 0:nb], in_=src3, axis=AX.X, op=OP.max,
                       apply_absolute_value=True))
    hs(v.tensor_scalar(out=q["blk"][P, 0:nb], in0=q["blk"][P, 0:nb], scalar1=1.0 / 6.0,
                       scalar2=1e-12, op0=OP.mult, op1=OP.max))
    hs(v.reciprocal(out=q["rs"][P, 0:nb], in_=q["blk"][P, 0:nb]))
    rse = q["rs"][P, 0:nb].unsqueeze(-1).broadcast_to([parts, nb, 16])
    ctx.inc(v.tensor_tensor(out=q["t"][P, 0:width], in0=src3, in1=rse, op=OP.mult), "fq_v")
    ctx.w(s, "fq_v")
    s.activation(out=q["u"][P, 0:width], in_=q["t"][P, 0:width], func=AF.Abs)
    ctx.inc(s.activation(out=q["sg"][P, 0:width], in_=q["t"][P, 0:width], func=AF.Sign), "fq_s")
    ctx.w(v, "fq_s")
    u = q["u"][P, 0:width]
    hs(v.tensor_scalar(out=q["a"][P, 0:width], in0=u, scalar1=0.25, scalar2=None, op0=OP.is_gt))
    for thr in (0.75, 1.25, 1.75):
        hs(v.scalar_tensor_tensor(out=q["a"][P, 0:width], in0=u, scalar=thr,
                                  in1=q["a"][P, 0:width], op0=OP.is_gt, op1=OP.add))
    hs(v.tensor_scalar(out=q["acc"][P, 0:width], in0=u, scalar1=2.5, scalar2=None, op0=OP.is_gt))
    hs(v.scalar_tensor_tensor(out=q["acc"][P, 0:width], in0=u, scalar=3.5,
                              in1=q["acc"][P, 0:width], op0=OP.is_gt, op1=OP.add))
    hs(v.scalar_tensor_tensor(out=q["acc"][P, 0:width], in0=q["a"][P, 0:width], scalar=0.5,
                              in1=q["acc"][P, 0:width], op0=OP.mult, op1=OP.add))
    for _ in range(2):
        hs(v.scalar_tensor_tensor(out=q["acc"][P, 0:width], in0=u, scalar=5.0,
                                  in1=q["acc"][P, 0:width], op0=OP.is_gt, op1=OP.add))
    hs(v.tensor_tensor(out=q["acc"][P, 0:width], in0=q["acc"][P, 0:width],
                       in1=q["sg"][P, 0:width], op=OP.mult))
    sce = q["blk"][P, 0:nb].unsqueeze(-1).broadcast_to([parts, nb, 16])
    return v.tensor_tensor(out=out_ap, in0=q["acc"][P, 0:width].rearrange("p (b s) -> p b s", s=16),
                           in1=sce, op=OP.mult)


def emit_t128(ctx, src_ap, dst_ap, pst, id_bf):
    nc = ctx.nc
    t, v = nc.tensor, nc.vector
    if not isinstance(id_bf, bass.AP):
        id_bf = id_bf[:]
    ctx.S("tr_t"); ctx.S("tr_v")
    j = ctx.cnt["tr_t"]
    slot = (j % 2) * 1024
    if j >= 2:
        ctx.w(t, "tr_v", j - 1)
    ctx.inc(t.transpose(pst[:, slot:slot + 128], src_ap, id_bf), "tr_t")
    ctx.w(v, "tr_t", j + 1)
    ctx.inc(v.tensor_copy(dst_ap, pst[:, slot:slot + 128]), "tr_v")


def emit_bulk_transpose(ctx, src, dst, ntile, pst, id_bf, src_stride=1024, dst_stride=2048,
                        dst_off=0, src_off=0, nptile=8):
    for tt in range(ntile):
        for j in range(nptile):
            emit_t128(ctx, src[:, src_off + tt * src_stride + 128 * j: src_off + tt * src_stride + 128 * j + 128],
                      dst[:, dst_off + j * dst_stride + 128 * tt: dst_off + j * dst_stride + 128 * tt + 128],
                      pst, id_bf)




def ap3(dram_ap, ngrp, width, off_elems=0):
    return bass.AP(dram_ap.tensor, dram_ap.offset + off_elems,
                   [[width, 128], [128 * width, ngrp], [1, width]])


def sb3(sbuf_ap, ngrp, width):
    return sbuf_ap.rearrange("p (g w) -> p g w", w=width)

def emit_full_barrier(ctx):
    nc = ctx.nc
    bs = ctx.barscr
    s = ctx.S("bar")
    ctx.inc(nc.gpsimd.tensor_copy(bs[:, 0:1], bs[:, 1:2]), "bar")
    ctx.inc(nc.sync.nop(), "bar")
    ctx.inc(nc.tensor.nop(), "bar")
    ctx.inc(nc.vector.tensor_copy(bs[:, 2:3], bs[:, 3:4]), "bar")
    ctx.inc(nc.scalar.activation(out=bs[:, 4:5], in_=bs[:, 5:6], func=AF.Copy), "bar")
    tgt = ctx.cnt["bar"]
    for e in [nc.gpsimd, nc.sync, nc.tensor, nc.vector, nc.scalar]:
        e.wait_ge(s, tgt)


def emit_proj(ctx, w_sb, xnT, psA, psB, evac, sem, nhp=8):
    nc = ctx.nc
    t = nc.tensor
    tb = ctx.cnt.get(sem + "_t", 0)
    for hp in range(nhp):
        ps = psA if hp % 2 == 0 else psB
        ev = ctx.cnt.get(sem + "_v", 0)
        if ev >= 2:
            ctx.w(t, sem + "_v", ev - 1)
        ins = None
        for mp in range(8):
            for ch in range(4):
                ins = t.matmul(ps[:, cs(ch, 512)],
                               w_sb[:, 1024 * mp + 128 * hp:1024 * mp + 128 * hp + 128],
                               xnT[:, 2048 * mp + 512 * ch:2048 * mp + 512 * (ch + 1)],
                               start=(mp == 0), stop=(mp == 7))
        ctx.inc(ins, sem + "_t")
        evac(hp, ps)


def l0_granules(qh):
    out = []
    for kt in range(32 * qh + 32):
        tmin = kt // 4
        qfrom = max(0, 128 * (tmin - 8 * qh))
        off, first = qfrom, True
        while off < 1024:
            wdt = min(512, 1024 - off)
            out.append((kt, off, wdt, first and (8 * qh <= tmin)))
            off += wdt
            first = False
    return out


def build_l0(dump=(), phases=9, p1sub=9, p4sub=127, F=None):
    nc = F["nc"] if F is not None else bass.Bass("TRN2", target_bir_lowering=False, debug=False)
    g, y, t, v, s = nc.gpsimd, nc.sync, nc.tensor, nc.vector, nc.scalar
    x0 = nc.dram_tensor("x0", [TPC, D], F32, kind="ExternalInput").ap()
    wq_d = nc.dram_tensor("wq", [D, D], F32, kind="ExternalInput").ap()
    wk_d = nc.dram_tensor("wk", [D, D], F32, kind="ExternalInput").ap()
    wv_d = nc.dram_tensor("wv", [D, D], F32, kind="ExternalInput").ap()
    wo_d = nc.dram_tensor("wo", [D, D], F32, kind="ExternalInput").ap()
    wgs_d = nc.dram_tensor("wgs", [384, 1024], F32, kind="ExternalInput").ap()
    wus_d = nc.dram_tensor("wus", [384, 1024], F32, kind="ExternalInput").ap()
    wds_d = nc.dram_tensor("wds", [128, 2816], F32, kind="ExternalInput").ap()
    ident_d = nc.dram_tensor("ident", [128, 128], F32, kind="ExternalInput").ap()
    bmask_d = nc.dram_tensor("bmask", [4, 128, 128], F32, kind="ExternalInput").ap()
    x1_d = (nc.dram_tensor("x1", [TPC, D], F32).ap() if F is not None
            else nc.dram_tensor("x1", [TPC, D], F32, kind="ExternalOutput").ap())
    dumps = {}
    _dsh = {"xnf": ([TPC, D], F32), "xqt": ([128, 8 * TPC], F32), "qt": ([128, NHP * TPC], F32),
            "kv": ([2048, 2048], F32), "aot": ([128, 8 * TPC], F32), "wqa": ([8960, 1024], F32),
            "ffg": ([TPC, 1408], F32), "ffh": ([TPC, 1408], F32), "fft": ([TPC, 1408], F32),
            "ffw": ([128, 11264], F32)}
    for name in dump:
        dumps[name] = nc.dram_tensor("d_" + name, _dsh[name][0], _dsh[name][1],
                                     kind="ExternalOutput").ap()

    kvin = nc.dram_tensor("kvin", [2048, 2048], BF16).ap()
    kvag = nc.dram_tensor("kvag", [NCORE * 2048, 2048], BF16, addr_space="Shared").ap()
    kvloc = nc.dram_tensor("kvloc", [4 * 2048, 2048], BF16).ap()
    wqin = nc.dram_tensor("wqin", [1120, 1024], BF16).ap()
    wqag = nc.dram_tensor("wqag", [NCORE * 1120, 1024], BF16, addr_space="Shared").ap()

    with contextlib.ExitStack() as ex:
        if F is not None:
            ctx = F["ctx"]
            ctx.ex = ex
        else:
            ctx = Ctx(nc, ex)
        R = ctx.sb("R", [128, NT * 1024], F32)
        xqT = ctx.sb("xqT", [128, 8 * TPC], BF16)
        id_bf = ctx.sb("id_bf", [128, 128], BF16)
        bm_bf = ctx.sb("bm_bf", [128, 4 * 128], BF16)
        ones_f = ctx.sb("ones_f", [128, 64], F32)
        eps_c = ctx.sb("eps_c", [128, 1], F32)
        ctx.barscr = ctx.sb("barscr", [128, 8], F32)
        exQ = ctx.ex.enter_context(contextlib.ExitStack())
        QT = exQ.enter_context(nc.sbuf_tensor("QT", [128, NHP * TPC], BF16))
        ex.enter_context(nc.Block())

        ctx.inc(g.dma_start(out=sb3(R[:], NT, 1024), in_=ap3(x0, NT, 1024)), "ld", 16)
        ctx.inc(g.dma_start(out=id_bf[:], in_=ident_d), "ld", 16)
        ctx.inc(g.dma_start(out=sb3(bm_bf[:], 4, 128), in_=bass.AP(bmask_d.tensor, 0, [[128, 128], [128 * 128, 4], [1, 128]])), "ld", 16)
        g.memset(ones_f[:], 1.0)
        g.memset(eps_c[:], EPS)
        ctx.inc(g.nop(), "ld")

        if phases >= 1:
            exA = ctx.ex.enter_context(contextlib.ExitStack())
            xnT = exA.enter_context(nc.sbuf_tensor("xnT", [128, 8 * TPC], BF16))
            with nc.sbuf_tensor("xnf", [128, 2 * 1024], F32) as xnf, \
                 nc.sbuf_tensor("xnb", [128, 2 * 1024], BF16) as xnb, \
                 nc.sbuf_tensor("lnt", [128, 8 * NT], F32) as lnt, \
                 nc.sbuf_tensor("fq1", [128, 5 * 1024], F32) as fq1, \
                 nc.sbuf_tensor("fq2", [128, 2 * 64], F32) as fq2, \
                 nc.sbuf_tensor("wstg", [128, 2 * 1024], F32) as wstg, \
                 nc.sbuf_tensor("wstq", [128, 2 * 1024], BF16) as wstq, \
                 nc.psum_tensor("pst", [128, 2048], BF16) as pst:
                qd = dict(blk=fq2[:, 0:64], rs=fq2[:, 64:128],
                          t=fq1[:, cs(0, 1024)], u=fq1[:, cs(1, 1024)], sg=fq1[:, cs(2, 1024)],
                          a=fq1[:, cs(3, 1024)], acc=fq1[:, cs(4, 1024)])
                st = dict(sumx=lnt[:, 0:NT], sumsq=lnt[:, NT:2 * NT], m=lnt[:, 2 * NT:3 * NT],
                          msq=lnt[:, 3 * NT:4 * NT], var=lnt[:, 4 * NT:5 * NT],
                          lnv=lnt[:, 5 * NT:6 * NT], rstd=lnt[:, 6 * NT:7 * NT],
                          mrn=lnt[:, 7 * NT:8 * NT])
                v.wait_ge(ctx.S("ld"), ctx.cnt["ld"])
                s.wait_ge(ctx.S("ld"), ctx.cnt["ld"])
                for tt in range(NT):
                    ctx.inc(v.reduce_sum(out=st["sumx"][:, tt:tt + 1], in_=R[:, cs(tt, 1024)],
                                         axis=AX.X), "lsv")
                    ctx.inc(s.activation(out=fq1[:, 1024:2048], in_=R[:, cs(tt, 1024)],
                                         func=AF.Square), "lsq")
                    ctx.w(v, "lsq")
                    ctx.inc(v.reduce_sum(out=st["sumsq"][:, tt:tt + 1], in_=fq1[:, 1024:2048],
                                         axis=AX.X), "lsv")
                    ctx.w(s, "lsv")
                ctx.w(v, "lsv")
                iv = 1.0 / 1024.0
                ctx.inc(v.tensor_scalar(out=st["m"], in0=st["sumx"], scalar1=iv, scalar2=None, op0=OP.mult), "lns")
                ctx.w(v, "lns")
                ctx.inc(v.tensor_scalar(out=st["msq"], in0=st["sumsq"], scalar1=iv, scalar2=None, op0=OP.mult), "lns")
                ctx.w(v, "lns")
                ctx.inc(v.tensor_tensor(out=st["var"], in0=st["m"], in1=st["m"], op=OP.mult), "lns")
                ctx.w(v, "lns")
                ctx.inc(v.tensor_tensor(out=st["var"], in0=st["msq"], in1=st["var"], op=OP.subtract), "ln1")
                ctx.w(s, "ln1")
                ctx.inc(s.activation(out=st["lnv"], in_=st["var"], func=AF.Ln, bias=eps_c[:, 0:1]), "lnx")
                ctx.w(s, "lnx")
                ctx.inc(s.activation(out=st["rstd"], in_=st["lnv"], func=AF.Exp, scale=-0.5), "ln2")
                ctx.w(v, "ln2")
                ctx.inc(v.scalar_tensor_tensor(out=st["mrn"], in0=st["m"], scalar=-1.0,
                                               in1=st["rstd"], op0=OP.mult, op1=OP.mult), "ln3")
                ctx.w(v, "ln3")
                for tt in range(NT if p1sub >= 2 else 0):
                    sl = tt % 2
                    rst = st["rstd"][:, tt:tt + 1]
                    mrn = st["mrn"][:, tt:tt + 1]
                    if tt >= 2:
                        ctx.w(v, "fq_s", ctx.cnt.get("fq_s", 0))
                        if "xnf" in dumps:
                            ctx.w(v, "dxn", ctx.cnt.get("dxn", 0))
                    v.tensor_scalar(out=xnf[:, cs(sl, 1024)], in0=R[:, cs(tt, 1024)],
                                    scalar1=rst, scalar2=mrn, op0=OP.mult, op1=OP.add)
                    if tt >= 2:
                        ctx.w(v, "tr_v", ctx.cnt.get("tr_v", 0) - 8)
                    ctx.inc(v.tensor_scalar(out=xnb[:, cs(sl, 1024)], in0=R[:, cs(tt, 1024)],
                                            scalar1=rst, scalar2=mrn, op0=OP.mult, op1=OP.add), "xb")
                    ctx.w(t, "xb", ctx.cnt["xb"])
                    for j in range(8):
                        emit_t128(ctx, xnb[:, cs(sl, 1024)][:, cs(j, 128)],
                                  xnT[:, 2048 * j + 128 * tt:2048 * j + 128 * tt + 128], pst, id_bf)
                    if "xnf" in dumps:
                        ctx.w(y, "xb", ctx.cnt["xb"])
                        ctx.inc(y.dma_start(out=ap3(dumps["xnf"], 1, 1024, off_elems=1024 * 128 * tt),
                                            in_=xnf[:, cs(sl, 1024)].unsqueeze(1)), "dxn", 16)
                    if p1sub >= 3:
                        qi = emit_fp4_tile(ctx, v, s, xnf[:, cs(sl, 1024)], wstq[:, cs(sl, 1024)], qd, 1024)
                        ctx.inc(qi, "xqb")
                        ctx.w(t, "xqb", ctx.cnt["xqb"])
                        for j in range(8):
                            emit_t128(ctx, wstq[:, cs(sl, 1024)][:, cs(j, 128)],
                                      xqT[:, 2048 * j + 128 * tt:2048 * j + 128 * tt + 128], pst, id_bf)
                if "xqt" in dumps:
                    ctx.w(g, "tr_v")
                    ctx.inc(g.dma_start(out=dumps["xqt"], in_=xqT[:]), "dmp", 16)
                plist = []
                for j in range(3):
                    plist.append((wgs_d[128 * j:128 * (j + 1), :], 1024, 128 * j))
                for j in range(3):
                    plist.append((wus_d[128 * j:128 * (j + 1), :], 1024, 384 + 128 * j))
                for hh in range(4):
                    src = bass.AP(wds_d.tensor, 704 * hh, [[2816, 128], [1, 704]])
                    plist.append((src, 704, 1000 + hh))
                ctx.w(v, "tr_v")
                if p1sub < 4:
                    plist = []
                for pi, (srcp, wdt, rowoff) in enumerate(plist):
                    sl = pi % 2
                    so = sl * 1024
                    ctx.inc(g.dma_start(out=wstg[:, so:so + wdt], in_=srcp), "wl", 16)
                    if pi >= 2:
                        ctx.w(v, "wst", ctx.cnt.get("wst", 0) - 16)
                    v.wait_ge(ctx.S("wl"), ctx.cnt["wl"])
                    s.wait_ge(ctx.S("wl"), ctx.cnt["wl"])
                    qi = emit_fp4_tile(ctx, v, s, wstg[:, so:so + wdt], wstq[:, so:so + wdt], qd, wdt)
                    ctx.inc(qi, "wqp")
                    ctx.w(g, "wqp", ctx.cnt["wqp"])
                    if wdt == 1024:
                        dst = wqin[rowoff:rowoff + 128, :]
                    else:
                        hh = rowoff - 1000
                        dst = bass.AP(wqin.tensor, 768 * 1024 + 704 * hh, [[2816, 128], [1, 704]])
                    ctx.inc(g.dma_start(out=dst, in_=wstq[:, so:so + wdt]), "wst", 16)
                g.wait_ge(ctx.S("wst"), ctx.cnt["wst"])
                if p1sub >= 5:
                    ctx.inc(g.collective_compute("AllGather", OP.bypass, replica_groups=[list(range(NCORE))],
                                                 ins=[wqin], outs=[wqag]), "wqc")
                if "wqa" in dumps:
                    ctx.w(g, "wqc")
                    ctx.inc(g.dma_start(out=dumps["wqa"], in_=wqag[:]), "dmp", 16)
                emit_full_barrier(ctx)

        if phases >= 2:
            with nc.sbuf_tensor("wA", [128, 8 * 1024], BF16) as wA, \
                 nc.sbuf_tensor("kst", [128, 2048], BF16) as kst, \
                 nc.psum_tensor("psqA", [128, 2048], F32) as psqA, \
                 nc.psum_tensor("psqB", [128, 2048], F32) as psqB:
                ctx.inc(g.dma_start(out=sb3(wA[:], 8, 1024), in_=ap3(wq_d, 8, 1024)), "wab", 16)
                t.wait_ge(ctx.S("wab"), ctx.cnt["wab"])
                ctx.w(t, "ln3")
                ctx.w(t, "tr_v", 256)

                def evac_qt(hp, ps):
                    ctx.w(v, "pj_t", ctx.cnt.get("pj_t", 0))
                    ctx.inc(v.tensor_copy(QT[:, cs(hp, 2048)], ps[:, 0:2048]), "pj_v")

                emit_proj(ctx, wA, xnT, psqA, psqB, evac_qt, "pj")

                def evac_kv(hp, ps, base_row):
                    ctx.w(v, "pj_t", ctx.cnt.get("pj_t", 0))
                    nst = ctx.cnt.get("kvst", 0)
                    if nst >= 16:
                        ctx.w(v, "kvst", nst)
                    ctx.inc(v.tensor_copy(kst[:], ps[:, 0:2048]), "pj_v")
                    ctx.w(g, "pj_v", ctx.cnt["pj_v"])
                    ctx.inc(g.dma_start(out=kvin[base_row + 128 * hp:base_row + 128 * (hp + 1), :],
                                        in_=kst[:]), "kvst", 16)

                ctx.w(g, "pj_t", 8)
                ctx.inc(g.dma_start(out=sb3(wA[:], 8, 1024), in_=ap3(wk_d, 8, 1024)), "wab", 16)
                t.wait_ge(ctx.S("wab"), ctx.cnt["wab"])
                emit_proj(ctx, wA, xnT, psqA, psqB, lambda hp, ps: evac_kv(hp, ps, 0), "pj")
                ctx.w(g, "pj_t", 16)
                ctx.inc(g.dma_start(out=sb3(wA[:], 8, 1024), in_=ap3(wv_d, 8, 1024)), "wab", 16)
                t.wait_ge(ctx.S("wab"), ctx.cnt["wab"])
                emit_proj(ctx, wA, xnT, psqA, psqB, lambda hp, ps: evac_kv(hp, ps, 1024), "pj")
                g.wait_ge(ctx.S("kvst"), ctx.cnt["kvst"])
                ctx.inc(g.collective_compute("AllGather", OP.bypass, replica_groups=[list(range(NCORE))],
                                             ins=[kvin], outs=[kvag]), "kvc")
                ctx.w(g, "kvc")
                with g.register("roff") as roff:
                    pid = g.partition_id()
                    g.reg_mov(roff, pid)
                    g.reg_div(roff, roff, 4)
                    g.reg_mul(roff, roff, 4 * 2048 * 2048)
                    src = bass.AP(kvag.tensor, roff, [[2048, 8192], [1, 2048]])
                    ctx.inc(g.dma_start(out=kvloc, in_=src), "kvl", 16)
                g.wait_ge(ctx.S("kvl"), ctx.cnt["kvl"])
                if "kv" in dumps:
                    ctx.inc(g.dma_start(out=dumps["kv"], in_=kvloc[0:2048, :]), "dmp", 16)
                if "qt" in dumps:
                    ctx.inc(g.dma_start(out=dumps["qt"], in_=QT[:]), "dmp", 16)
                emit_full_barrier(ctx)
            exA.close()

        if phases >= 3:
            with nc.sbuf_tensor("AOT", [128, 8 * TPC], BF16) as AOT, \
                 nc.sbuf_tensor("KTl", [128, 8192], BF16) as KTl, \
                 nc.sbuf_tensor("Vl", [128, 64 * 130], BF16) as Vl, \
                 nc.sbuf_tensor("vst", [128, 256], BF16) as vst, \
                 nc.sbuf_tensor("PT", [128, 2048], BF16) as PT, \
                 nc.sbuf_tensor("rzb", [65, 2048], F32) as rzb, \
                 nc.psum_tensor("psS", [128, 2048], F32) as psS, \
                 nc.psum_tensor("psO1", [128, 1024], F32) as psO1, \
                 nc.psum_tensor("psO2", [128, 1024], F32) as psO2:
                ov = Vl[:].rearrange("p (k s) -> p k s", s=65)[:, :, 64:65]
                ctx.inc(v.memset(ov, 1.0), "vone")
                for hp in range(NHP):
                    if hp == 0:
                        ctx.w(g, "kvl")
                    else:
                        ctx.w(g, "at", ctx.cnt.get("at", 0))
                    for r in range(4):
                        ctx.inc(g.dma_start(out=KTl[:, cs(r, 2048)],
                                            in_=kvloc[2048 * r + 128 * hp:2048 * r + 128 * (hp + 1), :]),
                                "ktl", 16)
                    vt_b = ctx.cnt.get("vt_v", 0)
                    for kt in range(64):
                        r, lt = kt % 4, kt // 4
                        if kt >= 2:
                            ctx.w(y, "vt_v", vt_b + kt - 1)
                        if hp == 0 and kt == 0:
                            ctx.w(y, "kvl")
                            ctx.w(y, "vone")
                        if hp > 0 and kt == 0:
                            ctx.w(y, "hp_done", hp)
                        ctx.inc(y.dma_start_transpose(vst[:, cs(kt % 2, 128)],
                                                      kvloc[2048 * r + 1024 + 128 * hp:2048 * r + 1024 + 128 * (hp + 1),
                                                            cs(lt, 128)]), "vt_y", 16)
                        ctx.w(v, "vt_y")
                        v.tensor_copy(Vl[:, 130 * kt:130 * kt + 64], vst[:, cs(kt % 2, 128)][:, 0:64])
                        ctx.inc(v.tensor_copy(Vl[:, 130 * kt + 65:130 * kt + 129],
                                              vst[:, cs(kt % 2, 128)][:, 64:128]), "vt_v")
                    for qh in range(2):
                        grans = l0_granules(qh)
                        ng = len(grans)
                        exb = ctx.cnt.get("ex", 0)
                        atb = ctx.cnt.get("at", 0)
                        avb = ctx.cnt.get("av", 0)
                        for gi, (kt, off, wdt, msk) in enumerate(grans):
                            slot = gi % 2
                            kcol = 2048 * (kt % 4) + 128 * (kt // 4)
                            qcol = 2048 * hp + 1024 * qh + off
                            if gi >= 2:
                                ctx.w(t, "ex", exb + gi - 1)
                            if gi == 0:
                                t.wait_ge(ctx.S("ktl"), ctx.cnt["ktl"])
                                ctx.w(t, "ev", ctx.cnt.get("ev", 0))
                            t.matmul(psS[:, 1024 * slot:1024 * slot + wdt],
                                     KTl[0:64, kcol:kcol + 128], QT[0:64, qcol:qcol + wdt],
                                     start=True, stop=True, tile_position=(0, 0))
                            ctx.inc(t.matmul(psS[:, 1024 * slot + 512:1024 * slot + 512 + wdt],
                                             KTl[64:128, kcol:kcol + 128], QT[64:128, qcol:qcol + wdt],
                                             start=True, stop=True, tile_position=(64, 0)), "at")
                            ctx.w(s, "at", atb + gi + 1)
                            if gi >= 2:
                                ctx.w(s, "av", avb + gi - 1)
                            ctx.inc(s.activation(out=PT[:, 1024 * slot:1024 * slot + 512 + wdt],
                                                 in_=psS[:, 1024 * slot:1024 * slot + 512 + wdt],
                                                 func=AF.Exp, scale=SQD), "ex")
                            if msk:
                                ctx.w(v, "ex", ctx.cnt["ex"])
                                bms = bm_bf[:, cs(kt % 4, 128)]
                                v.tensor_tensor(out=PT[:, 1024 * slot:1024 * slot + 128],
                                                in0=PT[:, 1024 * slot:1024 * slot + 128], in1=bms, op=OP.mult)
                                ctx.inc(v.tensor_tensor(out=PT[:, 1024 * slot + 512:1024 * slot + 640],
                                                        in0=PT[:, 1024 * slot + 512:1024 * slot + 640],
                                                        in1=bms, op=OP.mult), "mk")
                            if msk:
                                ctx.w(t, "mk", ctx.cnt["mk"])
                            else:
                                ctx.w(t, "ex", ctx.cnt["ex"])
                            ctx.w(t, "vt_v", vt_b + kt + 1)
                            t.matmul(psO1[0:65, off:off + wdt], Vl[:, 130 * kt:130 * kt + 65],
                                     PT[:, 1024 * slot:1024 * slot + wdt],
                                     start=(kt == 0), stop=False)
                            ctx.inc(t.matmul(psO2[0:65, off:off + wdt], Vl[:, 130 * kt + 65:130 * kt + 130],
                                             PT[:, 1024 * slot + 512:1024 * slot + 512 + wdt],
                                             start=(kt == 0), stop=False), "av")
                        ctx.w(v, "av", ctx.cnt["av"])
                        v.reciprocal(out=rzb[64:65, 0:1024], in_=psO1[64:65, 0:1024])
                        ctx.inc(v.reciprocal(out=rzb[64:65, 1024:2048], in_=psO2[64:65, 0:1024]), "rzv")
                        ctx.w(t, "rzv", ctx.cnt["rzv"])
                        ctx.w(t, "ex", ctx.cnt["ex"])
                        ins = None
                        for h in range(2):
                            for c2 in range(2):
                                ins = t.matmul(psS[0:64, 1024 * h + 512 * c2:1024 * h + 512 * (c2 + 1)],
                                               ones_f[64:65, 0:64],
                                               rzb[64:65, 1024 * h + 512 * c2:1024 * h + 512 * (c2 + 1)],
                                               start=True, stop=True)
                        ctx.inc(ins, "rbt")
                        ctx.w(v, "rbt", ctx.cnt["rbt"])
                        ctx.inc(v.tensor_copy(rzb[0:64, 0:2048], psS[0:64, 0:2048]), "rzc")
                        ctx.w(v, "rzc")
                        acol = 2048 * hp + 1024 * qh
                        v.tensor_tensor(out=AOT[0:64, acol:acol + 1024], in0=psO1[0:64, 0:1024],
                                        in1=rzb[0:64, 0:1024], op=OP.mult)
                        ctx.inc(v.tensor_tensor(out=AOT[64:128, acol:acol + 1024], in0=psO2[0:64, 0:1024],
                                                in1=rzb[0:64, 1024:2048], op=OP.mult), "ev")
                    ctx.w(y, "ev", ctx.cnt.get("ev", 0))
                    ctx.inc(y.nop(), "hp_done")
                if "aot" in dumps:
                    ctx.w(g, "ev", ctx.cnt.get("ev", 0))
                    ctx.inc(g.dma_start(out=dumps["aot"], in_=AOT[:]), "dmp", 16)
                ctx.w(g, "ev", ctx.cnt.get("ev", 0))
                ctx.inc(g.dma_start(out=sb3(KTl[:], 8, 1024), in_=ap3(wo_d, 8, 1024)), "wol", 16)
                t.wait_ge(ctx.S("wol"), ctx.cnt["wol"])
                ctx.w(t, "ev", ctx.cnt.get("ev", 0))
                for tc in range(NT):
                    ps = psO1 if tc % 2 == 0 else psO2
                    nyv = ctx.cnt.get("yv", 0)
                    if nyv >= 2:
                        ctx.w(t, "yv", nyv - 1)
                    ins = None
                    for dp in range(8):
                        for c2 in range(2):
                            ins = t.matmul(ps[:, cs(c2, 512)],
                                           AOT[:, 2048 * dp + 128 * tc:2048 * dp + 128 * (tc + 1)],
                                           KTl[:, 1024 * dp + 512 * c2:1024 * dp + 512 * (c2 + 1)],
                                           start=(dp == 0), stop=(dp == 7))
                    ctx.inc(ins, "yt")
                    ctx.w(v, "yt", ctx.cnt["yt"])
                    ctx.inc(v.tensor_tensor(out=R[:, cs(tc, 1024)], in0=R[:, cs(tc, 1024)],
                                            in1=ps[:, 0:1024], op=OP.add), "yv")
                emit_full_barrier(ctx)
            exQ.close()

        if phases >= 4:
            FH = 1408
            with nc.sbuf_tensor("wgT", [128, 8 * FH], BF16) as wgT, \
                 nc.sbuf_tensor("wuT", [128, 8 * FH], BF16) as wuT, \
                 nc.sbuf_tensor("wdT", [128, 11 * 1024], BF16) as wdT, \
                 nc.sbuf_tensor("wtg", [128, 2 * 1408], BF16) as wtg, \
                 nc.sbuf_tensor("sgat", [128, FH], F32) as sgat, \
                 nc.sbuf_tensor("hbuf", [128, FH], F32) as hbuf, \
                 nc.sbuf_tensor("hq", [128, FH], BF16) as hq, \
                 nc.sbuf_tensor("hqT", [128, 11 * 128], BF16) as hqT, \
                 nc.sbuf_tensor("fq1b", [128, 3 * FH], F32) as fq1b, \
                 nc.sbuf_tensor("fq2b", [128, 2 * 88], F32) as fq2b:
                qd2 = dict(blk=fq2b[:, 0:88], rs=fq2b[:, 88:176],
                           t=sgat[:, 0:FH], u=fq1b[:, cs(0, FH)], sg=fq1b[:, cs(1, FH)],
                           a=hbuf[:, 0:FH], acc=fq1b[:, cs(2, FH)])
                g.wait_ge(ctx.S("wqc"), ctx.cnt["wqc"])
                for hf in range(2 if p4sub else 0):
                    with nc.psum_tensor("pstf%d" % hf, [128, 2048], BF16) as pstf:
                        for wi, (dstT, base) in enumerate(((wgT, 0), (wuT, 384))):
                            for jj in range(11):
                                jn = 11 * hf + jj
                                rrow = 1120 * (jn // 3) + base + 128 * (jn % 3)
                                slot = (jj + wi * 11) % 2
                                so = slot * 1408
                                if ctx.cnt.get("wtl", 0) >= 32:
                                    ctx.w(g, "wtt", ctx.cnt.get("wtt", 0) - 1)
                                ctx.inc(g.dma_start(out=wtg[:, so:so + 1024],
                                                    in_=wqag[rrow:rrow + 128, :]), "wtl", 16)
                                t.wait_ge(ctx.S("wtl"), ctx.cnt["wtl"])
                                for kp in range(8):
                                    ctx.S("tr_t"); ctx.S("tr_v")
                                    jt = ctx.cnt["tr_t"]
                                    sl2 = (jt % 2) * 1024
                                    if jt >= 2:
                                        ctx.w(t, "tr_v", jt - 1)
                                    ctx.inc(t.transpose(pstf[:, sl2:sl2 + 128],
                                                        wtg[:, so + 128 * kp:so + 128 * (kp + 1)], id_bf[:]), "tr_t")
                                    ctx.w(v, "tr_t", jt + 1)
                                    ctx.inc(v.tensor_copy(dstT[:, FH * kp + 128 * jj:FH * kp + 128 * (jj + 1)],
                                                          pstf[:, sl2:sl2 + 128]), "tr_v")
                                ctx.inc(t.nop(), "wtt")
                        for j in range(8):
                            so = (j % 2) * 1408
                            if ctx.cnt.get("wtl", 0) >= 32:
                                ctx.w(g, "wtt", ctx.cnt.get("wtt", 0) - 1)
                            src = bass.AP(wqag.tensor, (1120 * j + 768) * 1024 + FH * hf, [[2816, 128], [1, FH]])
                            ctx.inc(g.dma_start(out=wtg[:, so:so + FH], in_=src), "wtl", 16)
                            t.wait_ge(ctx.S("wtl"), ctx.cnt["wtl"])
                            for ip in range(11):
                                jt = ctx.cnt["tr_t"]
                                sl2 = (jt % 2) * 1024
                                if jt >= 2:
                                    ctx.w(t, "tr_v", jt - 1)
                                srcb = wtg[:, so + 128 * ip:so + 128 * (ip + 1)]
                                ctx.inc(t.transpose(pstf[:, sl2:sl2 + 128], srcb, id_bf[:]), "tr_t")
                                ctx.w(v, "tr_t", jt + 1)
                                ctx.inc(v.tensor_copy(wdT[:, 1024 * ip + 128 * j:1024 * ip + 128 * (j + 1)],
                                                      pstf[:, sl2:sl2 + 128]), "tr_v")
                            ctx.inc(t.nop(), "wtt")
                        ctx.w(t, "tr_v")
                    emit_full_barrier(ctx)
                    if "ffw" in dumps and hf == 0:
                        ctx.inc(g.dma_start(out=dumps["ffw"], in_=wdT[:]), "dfw", 16)
                        g.wait_ge(ctx.S("dfw"), ctx.cnt["dfw"])
                    with nc.psum_tensor("pgf%d" % hf, [128, 1536], F32) as pgf, \
                         nc.psum_tensor("puf%d" % hf, [128, 1536], F32) as puf, \
                         nc.psum_tensor("pdf%d" % hf, [128, 1024], F32) as pdf:
                        for tc in range(NT if (p4sub & 8) else 0):
                            ins = None
                            if tc > 0:
                                ctx.w(t, "sil", ctx.cnt.get("sil", 0))
                            for kp in range(8):
                                for c3, (o3, w3) in enumerate(((0, 512), (512, 512), (1024, 384))):
                                    ins = t.matmul(pgf[:, o3:o3 + w3],
                                                   xqT[:, 2048 * kp + 128 * tc:2048 * kp + 128 * (tc + 1)],
                                                   wgT[:, FH * kp + o3:FH * kp + o3 + w3],
                                                   start=(kp == 0), stop=(kp == 7))
                            ctx.inc(ins, "gt")
                            ctx.w(s, "gt", ctx.cnt["gt"])
                            if "ffg" in dumps and ctx.cnt.get("dfg", 0):
                                ctx.w(s, "dfg", ctx.cnt["dfg"])
                            ctx.inc(s.activation(out=sgat[:, 0:FH], in_=pgf[:, 0:FH], func=AF.Silu), "sil")
                            if "ffg" in dumps and hf == 0:
                                ctx.w(y, "sil", ctx.cnt["sil"])
                                ctx.inc(y.dma_start(out=dumps["ffg"][128 * tc:128 * (tc + 1), :],
                                                    in_=sgat[:, 0:FH]), "dfg", 16)
                            if not (p4sub & 16):
                                continue
                            ins = None
                            if tc > 0:
                                ctx.w(t, "hv", ctx.cnt.get("hv", 0))
                            for kp in range(8):
                                for c3, (o3, w3) in enumerate(((0, 512), (512, 512), (1024, 384))):
                                    ins = t.matmul(puf[:, o3:o3 + w3],
                                                   xqT[:, 2048 * kp + 128 * tc:2048 * kp + 128 * (tc + 1)],
                                                   wuT[:, FH * kp + o3:FH * kp + o3 + w3],
                                                   start=(kp == 0), stop=(kp == 7))
                            ctx.inc(ins, "ut")
                            ctx.w(v, "ut", ctx.cnt["ut"])
                            ctx.w(v, "sil", ctx.cnt["sil"])
                            ctx.inc(v.tensor_tensor(out=hbuf[:, 0:FH], in0=puf[:, 0:FH],
                                                    in1=sgat[:, 0:FH], op=OP.mult), "hv")
                            ctx.w(v, "hv")
                            if not (p4sub & 32):
                                continue
                            if ctx.cnt.get("hqt", 0):
                                ctx.w(v, "hqt", ctx.cnt["hqt"])
                            if "ffh" in dumps and ctx.cnt.get("dfh", 0):
                                ctx.w(v, "dfh", ctx.cnt["dfh"])
                            qi = emit_fp4_tile(ctx, v, s, hbuf[:, 0:FH], hq[:, 0:FH], qd2, FH)
                            ctx.inc(qi, "hqd")
                            if "ffh" in dumps and hf == 0:
                                ctx.w(g, "hqd", ctx.cnt["hqd"])
                                ctx.inc(g.dma_start(out=dumps["ffh"][128 * tc:128 * (tc + 1), :],
                                                    in_=hq[:, 0:FH]), "dfh", 16)
                            if not (p4sub & 64):
                                continue
                            ctx.w(y, "hqd", ctx.cnt["hqd"])
                            if tc > 0:
                                ctx.w(y, "dnt", ctx.cnt.get("dnt", 0))
                                if "fft" in dumps and ctx.cnt.get("dft", 0):
                                    ctx.w(y, "dft", ctx.cnt["dft"])
                            for ip in range(11):
                                ctx.inc(y.dma_start_transpose(hqT[:, cs(ip, 128)], hq[:, cs(ip, 128)]), "hqt", 16)
                            if "fft" in dumps and hf == 0:
                                g.wait_ge(ctx.S("hqt"), ctx.cnt["hqt"])
                                ctx.inc(g.dma_start(out=dumps["fft"][128 * tc:128 * (tc + 1), :],
                                                    in_=hqT[:, 0:1408]), "dft", 16)
                            t.wait_ge(ctx.S("hqt"), ctx.cnt["hqt"])
                            if tc > 0:
                                ctx.w(t, "rv", ctx.cnt.get("rv", 0))
                            ins = None
                            for ip in range(11):
                                for c2 in range(2):
                                    ins = t.matmul(pdf[:, cs(c2, 512)], hqT[:, cs(ip, 128)],
                                                   wdT[:, 1024 * ip + 512 * c2:1024 * ip + 512 * (c2 + 1)],
                                                   start=(ip == 0), stop=(ip == 10))
                            ctx.inc(ins, "dnt")
                            ctx.w(v, "dnt", ctx.cnt["dnt"])
                            ctx.inc(v.tensor_tensor(out=R[:, cs(tc, 1024)], in0=R[:, cs(tc, 1024)],
                                                    in1=pdf[:, 0:1024], op=OP.add), "rv")
                    emit_full_barrier(ctx)
        ctx.inc(y.dma_start(out=ap3(x1_d, NT, 1024), in_=sb3(R[:], NT, 1024)), "sto", 16)
        y.wait_ge(ctx.S("sto"), ctx.cnt["sto"])
        if dumps:
            y.wait_ge(ctx.S("dmp"), ctx.cnt.get("dmp", 0))
            if "dxn" in ctx.cnt:
                y.wait_ge(ctx.S("dxn"), ctx.cnt["dxn"])
        if F is not None:
            x1ag = nc.dram_tensor("x1ag", [NCORE * TPC, D], F32, addr_space="Shared").ap()
            x1loc = nc.dram_tensor("x1loc", [TPC, D], F32).ap()
            F["x1loc"] = x1loc
            F["ident_d"] = ident_d
            g.wait_ge(ctx.S("sto"), ctx.cnt["sto"])
            ctx.inc(g.collective_compute("AllGather", OP.bypass, replica_groups=[list(range(NCORE))],
                                         ins=[x1_d], outs=[x1ag]), "x1c")
            ctx.w(g, "x1c")
            with g.register("xoff") as xoff, g.register("xq4") as xq4:
                pid = g.partition_id()
                g.reg_mov(xq4, pid)
                g.reg_div(xq4, xq4, 4)
                g.reg_mul(xq4, xq4, 6144 * 1024)
                g.reg_mov(xoff, pid)
                g.reg_mul(xoff, xoff, 512 * 1024)
                g.reg_add(xoff, xoff, xq4)
                for p4 in range(4):
                    g.reg_add(xq4, xoff, 2048 * 1024 * p4)
                    src = bass.AP(x1ag.tensor, xq4, [[128 * 1024, 4], [16384, 8], [1, 16384]])
                    dst = bass.AP(x1loc.tensor, 128 * 1024 * p4, [[512 * 1024, 4], [16384, 8], [1, 16384]])
                    ctx.inc(g.dma_start(out=dst, in_=src), "x1l", 16)
            g.wait_ge(ctx.S("x1l"), ctx.cnt["x1l"])
            emit_full_barrier(ctx)
    return nc



def make_in_maps_l1(inputs, x1_list):
    wq1 = np.ascontiguousarray(np.asarray(inputs["L1_attn_Wq"], np.float32).T)
    wk1 = np.ascontiguousarray(np.asarray(inputs["L1_attn_Wk"], np.float32).T)
    wv1 = np.ascontiguousarray(np.asarray(inputs["L1_attn_Wv"], np.float32).T)
    table = np.asarray(inputs["L1_attn_rel_bias_table"], np.float32)
    gam = np.ascontiguousarray(np.asarray(inputs["L1_attn_ln_gamma"], np.float32).reshape(8, 128).T)
    bet = np.ascontiguousarray(np.asarray(inputs["L1_attn_ln_beta"], np.float32).reshape(8, 128).T)
    router = np.ascontiguousarray(np.asarray(inputs["L1_ffn_router_w"], np.float32))
    w0 = np.asarray(inputs["L1_ffn_w0_kernel"], np.float32)
    w1 = np.asarray(inputs["L1_ffn_w1_kernel"], np.float32)
    wo_ = np.asarray(inputs["L1_ffn_wo_kernel"], np.float32)
    ident = np.eye(128, dtype=np.float32)
    ltri = (np.arange(128)[:, None] < np.arange(128)[None, :]).astype(np.float32)
    chead = np.broadcast_to(table[4094][None, :], (128, 16)).copy()
    ii = np.arange(128)[None, None, :]
    jj = np.arange(128)[None, :, None]
    dd = np.arange(32)[:, None, None]
    if x1_list is not None:
        x1f = np.zeros((B, S, D), np.float32)
        for c in range(NCORE):
            b, p = c // 4, c % 4
            x1f[b].reshape(GT, 128, D)[_pos_tiles(p)] = np.asarray(x1_list[c]).reshape(NT, 128, D)
        x1flat = x1f.reshape(B * S, D)
    maps = []
    for c in range(NCORE):
        p = c % 4
        delta = dd + 32 - 16 * p
        idx = np.minimum(8191 - 128 * delta + ii - jj, 4094)
        blk = table[np.maximum(idx, 0), :] * (delta >= 32)[..., None]
        blk = np.ascontiguousarray(blk.transpose(3, 0, 1, 2)).astype(np.float32)
        um = (np.arange(64)[:, None] <= np.minimum(16 * p + np.arange(16)[None, :] + 31, 63)).astype(np.float32)
        mp = dict(
            wq1=wq1, wk1=wk1, wv1=wv1,
            lngam=gam, lnbet=bet, router=router,
            w0e=np.ascontiguousarray(w0[c]), w1e=np.ascontiguousarray(w1[c]),
            woe=np.ascontiguousarray(wo_[c]),
            biasblk=blk, chead=chead, umat=um, ltri=ltri, ident=ident)
        if x1_list is not None:
            mp["x1"] = np.ascontiguousarray(x1flat[2048 * c:2048 * (c + 1)])
        maps.append(mp)
    return maps


def assemble_contig(res, key):
    out = np.zeros((B * S, D), np.float32)
    for c in range(NCORE):
        out[2048 * c:2048 * (c + 1)] = np.asarray(res[c][key])
    return out.reshape(B, S, D)


class _NcSuffix:
    def __init__(self, nc, suf):
        self._nc = nc
        self._suf = suf

    def sbuf_tensor(self, name, shape, dt, **kw):
        return self._nc.sbuf_tensor(name + self._suf, shape, dt, **kw)

    def psum_tensor(self, name, shape, dt=F32, **kw):
        return self._nc.psum_tensor(name + self._suf, shape, dt, **kw)

    def __getattr__(self, a):
        return getattr(self._nc, a)


def build_l1(dump=(), phases=9, p3sub=9, F=None):
    nc = F["nc"] if F is not None else bass.Bass("TRN2", target_bir_lowering=False, debug=False)
    if F is not None:
        nc = _NcSuffix(nc, "_1")
        F["ctx"].nc = nc
    g, y, t, v, s = nc.gpsimd, nc.sync, nc.tensor, nc.vector, nc.scalar
    x1_d = F["x1loc"] if F is not None else nc.dram_tensor("x1", [TPC, D], F32, kind="ExternalInput").ap()
    wq_d = nc.dram_tensor("wq1", [D, D], F32, kind="ExternalInput").ap()
    wk_d = nc.dram_tensor("wk1", [D, D], F32, kind="ExternalInput").ap()
    wv_d = nc.dram_tensor("wv1", [D, D], F32, kind="ExternalInput").ap()
    gam_d = nc.dram_tensor("lngam", [128, 8], F32, kind="ExternalInput").ap()
    bet_d = nc.dram_tensor("lnbet", [128, 8], F32, kind="ExternalInput").ap()
    rt_d = nc.dram_tensor("router", [D, E], F32, kind="ExternalInput").ap()
    w0_d = nc.dram_tensor("w0e", [D, HMOE], F32, kind="ExternalInput").ap()
    w1_d = nc.dram_tensor("w1e", [D, HMOE], F32, kind="ExternalInput").ap()
    wo_d = nc.dram_tensor("woe", [HMOE, D], F32, kind="ExternalInput").ap()
    bb_d = nc.dram_tensor("biasblk", [NH, 32, 128, 128], F32, kind="ExternalInput").ap()
    ch_d = nc.dram_tensor("chead", [128, NH], F32, kind="ExternalInput").ap()
    um_d = nc.dram_tensor("umat", [64, 16], F32, kind="ExternalInput").ap()
    lt_d = nc.dram_tensor("ltri", [128, 128], F32, kind="ExternalInput").ap()
    ident_d = F["ident_d"] if F is not None else nc.dram_tensor("ident", [128, 128], F32, kind="ExternalInput").ap()
    out_d = nc.dram_tensor("out", [TPC, D], F32, kind="ExternalOutput").ap()
    dumps = {}
    _dsh = {"xn1": ([TPC, D], F32), "aot1": ([128, 8 * TPC], F32), "y1": ([TPC, D], F32),
            "lgt": ([TPC, E], F32), "gat": ([TPC, 16], F32), "ei": ([4096, D], F32),
            "eo": ([4096, D], F32), "ffo": ([TPC, D], F32), "elo": ([4096, D], F32)}
    for name in dump:
        dumps[name] = nc.dram_tensor("d_" + name, _dsh[name][0], _dsh[name][1],
                                     kind="ExternalOutput").ap()

    kvin = nc.dram_tensor("kvin1", [2048, 2048], BF16).ap()
    kvag = nc.dram_tensor("kvag1", [NCORE * 2048, 2048], BF16, addr_space="Shared").ap()
    kvloc = nc.dram_tensor("kvloc1", [4 * 2048, 2048], BF16).ap()
    xnin = nc.dram_tensor("xnin", [TPC, D], F32).ap()
    y1d = nc.dram_tensor("y1d", [TPC, D], F32).ap()
    eiin = nc.dram_tensor("eiin", [4096, D], BF16).ap()
    eiag = nc.dram_tensor("eiag", [NCORE * 4096, D], BF16, addr_space="Shared").ap()
    eloc = nc.dram_tensor("eloc", [4096, D], BF16).ap()
    eoin = nc.dram_tensor("eoin", [4096, D], BF16).ap()
    eoag = nc.dram_tensor("eoag", [NCORE * 4096, D], BF16, addr_space="Shared").ap()
    eocd = nc.dram_tensor("eocd", [4096, D], BF16).ap()
    ffout = nc.dram_tensor("ffout", [TPC, D], F32).ap()

    with contextlib.ExitStack() as ex:
        if F is not None:
            ctx = F["ctx"]
            ctx.ex = ex
        else:
            ctx = Ctx(nc, ex)
        ph1 = ["lsv", "lsq", "lns", "ln1", "lnx", "ln2", "ln3", "xnw", "xb", "xld",
               "wab", "pj_t", "pj_v", "kvst", "kvc", "kvl", "dxn"]
        attn = ["at", "ktl", "bbl", "bt", "vt_y", "vt_v", "hp_done", "ex", "av", "ev",
                "ev1", "ev2", "rzv", "rbt", "rzc", "bev", "tsm", "tse", "tst", "tsv",
                "cum", "csc", "vone"]
        mixer = ["mst", "mse", "msq", "msm", "msr", "msl", "mbc", "mbe", "ym", "ytt", "ytr", "y1w"]
        moe = ["mio", "mld", "mxf", "xgc", "mgh", "mge", "rlt", "rle", "rtt", "rtv",
               "c1m", "c1me", "c2m", "c2me", "m1c", "m1ce", "m1cb", "m1cbe", "dbd", "dsp",
               "dse", "eiw", "eic", "elc", "els", "xet", "wst2", "h0m", "h1m", "gsq",
               "gs2", "gth", "sgx", "wol2", "eom", "eoe", "eow", "eoc", "eor", "ctt",
               "ctr", "cb2", "cbd", "ffm", "ffe", "ffw", "ffc"]
        final = ["fld", "fadd", "fst"]
        for i, n in enumerate(mixer):
            ctx.alias[n] = ph1[i]
        pool_moe = attn + [p for p in ph1 if p not in ("dxn",)] + ["xp%d" % i for i in range(10)]
        for i, n in enumerate(moe):
            ctx.alias[n] = pool_moe[i]
        for i, n in enumerate(final):
            ctx.alias[n] = mixer[i]
        ctx.alias = {k: ctx.alias.get(vv, vv) for k, vv in ctx.alias.items()}
        id_bf = ctx.sb("id_bf", [128, 128], BF16)
        id_f = ctx.sb("id_f", [128, 128], F32)
        ones_f = ctx.sb("ones_f", [128, 64], F32)
        onesc_bf = ctx.sb("onesc_bf", [128, 1], BF16)
        onesr_f = ctx.sb("onesr_f", [1, 128], F32)
        eps_c = ctx.sb("eps_c", [128, 1], F32)
        ctx.barscr = ctx.sb("barscr", [128, 8], F32)
        um_bf = ctx.sb("um_bf", [64, 16], BF16)
        lt_bf = ctx.sb("lt_bf", [128, 128], BF16)
        allon_bf = ctx.sb("allon_bf", [128, 128], BF16)
        ch_sb = ctx.sb("ch_sb", [128, NH], F32)
        gam_sb = ctx.sb("gam_sb", [128, 8], F32)
        bet_sb = ctx.sb("bet_sb", [128, 8], F32)

        ctx.inc(g.dma_start(out=id_bf[:], in_=ident_d), "ld", 16)
        ctx.inc(g.dma_start(out=id_f[:], in_=ident_d), "ld", 16)
        ctx.inc(g.dma_start(out=um_bf[:], in_=um_d), "ld", 16)
        ctx.inc(g.dma_start(out=lt_bf[:], in_=lt_d), "ld", 16)
        ctx.inc(g.dma_start(out=ch_sb[:], in_=ch_d), "ld", 16)
        ctx.inc(g.dma_start(out=gam_sb[:], in_=gam_d), "ld", 16)
        ctx.inc(g.dma_start(out=bet_sb[:], in_=bet_d), "ld", 16)
        g.memset(ones_f[:], 1.0)
        g.memset(onesc_bf[:], 1.0)
        g.memset(onesr_f[:], 1.0)
        g.memset(allon_bf[:], 1.0)
        ctx.inc(g.memset(eps_c[:], EPS), "ld", 1)

        exA1 = ctx.ex.enter_context(contextlib.ExitStack())
        AOT = exA1.enter_context(nc.sbuf_tensor("AOT1", [128, 8 * TPC], F32))
        exQ = ctx.ex.enter_context(contextlib.ExitStack())
        QT = exQ.enter_context(nc.sbuf_tensor("QT1", [128, NHP * TPC], BF16))
        exX = ctx.ex.enter_context(contextlib.ExitStack())
        xnT = exX.enter_context(nc.sbuf_tensor("xnT1", [128, 8 * TPC], BF16))
        with nc.sbuf_tensor("xt1", [128, 2 * 1024], F32) as R1t, \
             nc.sbuf_tensor("xnf", [128, 2 * 1024], F32) as xnf, \
             nc.sbuf_tensor("xnb", [128, 2 * 1024], BF16) as xnb, \
             nc.sbuf_tensor("lnt", [128, 8 * NT], F32) as lnt, \
             nc.sbuf_tensor("scr", [128, 2048], F32) as scr, \
             nc.sbuf_tensor("wA", [128, 8 * 1024], BF16) as wA, \
             nc.sbuf_tensor("kst", [128, 2048], BF16) as kst, \
             nc.psum_tensor("psqA", [128, 2048], F32) as psqA, \
             nc.psum_tensor("psqB", [128, 2048], F32) as psqB:
            pst = psqA[:].bitcast(BF16)
            st = dict(sumx=lnt[:, 0:NT], sumsq=lnt[:, NT:2 * NT], m=lnt[:, 2 * NT:3 * NT],
                      msq=lnt[:, 3 * NT:4 * NT], var=lnt[:, 4 * NT:5 * NT],
                      lnv=lnt[:, 5 * NT:6 * NT], rstd=lnt[:, 6 * NT:7 * NT],
                      mrn=lnt[:, 7 * NT:8 * NT])
            for tt in range(NT):
                sl = tt % 2
                if tt >= 2:
                    ctx.w(g, "lsv", ctx.cnt.get("lsv", 0) - 2)
                    ctx.w(g, "lsq", ctx.cnt.get("lsq", 0) - 1)
                ctx.inc(g.dma_start(out=sb3(R1t[:, cs(sl, 1024)], 1, 1024),
                                    in_=ap3(x1_d, 1, 1024, off_elems=1024 * 128 * tt)), "xld", 16)
                v.wait_ge(ctx.S("xld"), ctx.cnt["xld"])
                s.wait_ge(ctx.S("xld"), ctx.cnt["xld"])
                ctx.inc(v.reduce_sum(out=st["sumx"][:, tt:tt + 1], in_=R1t[:, cs(sl, 1024)],
                                     axis=AX.X), "lsv")
                ctx.inc(s.activation(out=scr[:, 0:1024], in_=R1t[:, cs(sl, 1024)],
                                     func=AF.Square), "lsq")
                ctx.w(v, "lsq")
                ctx.inc(v.reduce_sum(out=st["sumsq"][:, tt:tt + 1], in_=scr[:, 0:1024],
                                     axis=AX.X), "lsv")
                ctx.w(s, "lsv")
            ctx.w(v, "lsv")
            iv = 1.0 / 1024.0
            ctx.inc(v.tensor_scalar(out=st["m"], in0=st["sumx"], scalar1=iv, scalar2=None, op0=OP.mult), "lns")
            ctx.w(v, "lns")
            ctx.inc(v.tensor_scalar(out=st["msq"], in0=st["sumsq"], scalar1=iv, scalar2=None, op0=OP.mult), "lns")
            ctx.w(v, "lns")
            ctx.inc(v.tensor_tensor(out=st["var"], in0=st["m"], in1=st["m"], op=OP.mult), "lns")
            ctx.w(v, "lns")
            ctx.inc(v.tensor_tensor(out=st["var"], in0=st["msq"], in1=st["var"], op=OP.subtract), "ln1")
            ctx.w(s, "ln1")
            ctx.inc(s.activation(out=st["lnv"], in_=st["var"], func=AF.Ln, bias=eps_c[:, 0:1]), "lnx")
            ctx.w(s, "lnx")
            ctx.inc(s.activation(out=st["rstd"], in_=st["lnv"], func=AF.Exp, scale=-0.5), "ln2")
            ctx.w(v, "ln2")
            ctx.inc(v.scalar_tensor_tensor(out=st["mrn"], in0=st["m"], scalar=-1.0,
                                           in1=st["rstd"], op0=OP.mult, op1=OP.mult), "ln3")
            ctx.w(v, "ln3")
            for tt in range(NT):
                sl = tt % 2
                rst = st["rstd"][:, tt:tt + 1]
                mrn = st["mrn"][:, tt:tt + 1]
                if tt >= 2:
                    ctx.w(g, "xb", ctx.cnt.get("xb", 0) - 1)
                ctx.inc(g.dma_start(out=sb3(R1t[:, cs(sl, 1024)], 1, 1024),
                                    in_=ap3(x1_d, 1, 1024, off_elems=1024 * 128 * tt)), "xld", 16)
                v.wait_ge(ctx.S("xld"), ctx.cnt["xld"])
                if tt >= 2:
                    ctx.w(v, "dxn", ctx.cnt.get("dxn", 0))
                ctx.inc(v.tensor_scalar(out=xnf[:, cs(sl, 1024)], in0=R1t[:, cs(sl, 1024)],
                                        scalar1=rst, scalar2=mrn, op0=OP.mult, op1=OP.add), "xnw")
                if tt >= 2:
                    ctx.w(v, "tr_v", ctx.cnt.get("tr_v", 0) - 8)
                ctx.inc(v.tensor_scalar(out=xnb[:, cs(sl, 1024)], in0=R1t[:, cs(sl, 1024)],
                                        scalar1=rst, scalar2=mrn, op0=OP.mult, op1=OP.add), "xb")
                ctx.w(g, "xnw", ctx.cnt["xnw"])
                ctx.inc(g.dma_start(out=ap3(xnin, 1, 1024, off_elems=1024 * 128 * tt),
                                    in_=xnf[:, cs(sl, 1024)].unsqueeze(1)), "dxn", 16)
                ctx.w(t, "xb", ctx.cnt["xb"])
                for j in range(8):
                    emit_t128(ctx, xnb[:, cs(sl, 1024)][:, cs(j, 128)],
                              xnT[:, 2048 * j + 128 * tt:2048 * j + 128 * tt + 128], pst, id_bf)
            g.wait_ge(ctx.S("dxn"), ctx.cnt["dxn"])
            if "xn1" in dumps:
                ctx.inc(g.dma_start(out=dumps["xn1"], in_=xnin), "dmp", 16)
            ctx.inc(g.dma_start(out=sb3(wA[:], 8, 1024), in_=ap3(wq_d, 8, 1024)), "wab", 16)
            t.wait_ge(ctx.S("wab"), ctx.cnt["wab"])
            ctx.w(t, "tr_v", ctx.cnt["tr_v"])
            pjt0 = ctx.cnt.get("pj_t", 0)

            def evac_qt(hp, ps):
                ctx.w(v, "pj_t", ctx.cnt.get("pj_t", 0))
                ctx.inc(v.tensor_copy(QT[:, cs(hp, 2048)], ps[:, 0:2048]), "pj_v")

            emit_proj(ctx, wA, xnT, psqA, psqB, evac_qt, "pj")

            def evac_kv(hp, ps, base_row):
                ctx.w(v, "pj_t", ctx.cnt.get("pj_t", 0))
                nst = ctx.cnt.get("kvst", 0)
                if nst >= 16:
                    ctx.w(v, "kvst", nst)
                ctx.inc(v.tensor_copy(kst[:], ps[:, 0:2048]), "pj_v")
                ctx.w(g, "pj_v", ctx.cnt["pj_v"])
                ctx.inc(g.dma_start(out=kvin[base_row + 128 * hp:base_row + 128 * (hp + 1), :],
                                    in_=kst[:]), "kvst", 16)

            ctx.w(g, "pj_t", pjt0 + 8)
            ctx.inc(g.dma_start(out=sb3(wA[:], 8, 1024), in_=ap3(wk_d, 8, 1024)), "wab", 16)
            t.wait_ge(ctx.S("wab"), ctx.cnt["wab"])
            emit_proj(ctx, wA, xnT, psqA, psqB, lambda hp, ps: evac_kv(hp, ps, 0), "pj")
            ctx.w(g, "pj_t", pjt0 + 16)
            ctx.inc(g.dma_start(out=sb3(wA[:], 8, 1024), in_=ap3(wv_d, 8, 1024)), "wab", 16)
            t.wait_ge(ctx.S("wab"), ctx.cnt["wab"])
            emit_proj(ctx, wA, xnT, psqA, psqB, lambda hp, ps: evac_kv(hp, ps, 1024), "pj")
            g.wait_ge(ctx.S("kvst"), ctx.cnt["kvst"])
            ctx.inc(g.collective_compute("AllGather", OP.bypass, replica_groups=[list(range(NCORE))],
                                         ins=[kvin], outs=[kvag]), "kvc")
            ctx.w(g, "kvc")
            with g.register("roff1") as roff:
                pid = g.partition_id()
                g.reg_mov(roff, pid)
                g.reg_div(roff, roff, 4)
                g.reg_mul(roff, roff, 4 * 2048 * 2048)
                src = bass.AP(kvag.tensor, roff, [[2048, 8192], [1, 2048]])
                ctx.inc(g.dma_start(out=kvloc, in_=src), "kvl", 16)
            g.wait_ge(ctx.S("kvl"), ctx.cnt["kvl"])
            emit_full_barrier(ctx)
        exX.close()

        if phases >= 2:
            exAtt = ctx.ex.enter_context(contextlib.ExitStack())
            KTl = exAtt.enter_context(nc.sbuf_tensor("KTl", [128, 8192], BF16))
            Vl = exAtt.enter_context(nc.sbuf_tensor("Vl", [128, 64 * 130], BF16))
            vst = exAtt.enter_context(nc.sbuf_tensor("vst", [128, 256], BF16))
            PT = exAtt.enter_context(nc.sbuf_tensor("PT", [128, 2048], BF16))
            rzb = exAtt.enter_context(nc.sbuf_tensor("rzb", [65, 2048], F32))
            bb_bf = exAtt.enter_context(nc.sbuf_tensor("bb_bf", [128, 2 * 32 * 128], BF16))
            bias_sb = exAtt.enter_context(nc.sbuf_tensor("bias_sb", [64, 2048], F32))
            ts_sb = exAtt.enter_context(nc.sbuf_tensor("ts_sb", [64, 128], F32))
            tst_bf = exAtt.enter_context(nc.sbuf_tensor("tst_bf", [64, 128], BF16))
            cumnsc = exAtt.enter_context(nc.sbuf_tensor("cumnsc", [64, 32], F32))
            tmp_sb = exAtt.enter_context(nc.sbuf_tensor("tmp_sb", [64, 2048], F32))
            psS = exAtt.enter_context(nc.psum_tensor("psS", [128, 2048], F32))
            psO1 = exAtt.enter_context(nc.psum_tensor("psO1", [128, 1024], F32))
            psO2 = exAtt.enter_context(nc.psum_tensor("psO2", [128, 1024], F32))
            if True:
                ov = Vl[:].rearrange("p (k s) -> p k s", s=65)[:, :, 64:65]
                ctx.inc(v.memset(ov, 1.0), "vone")
                for hp in range(NHP):
                    if hp == 0:
                        ctx.w(g, "kvl")
                    else:
                        ctx.w(g, "at", ctx.cnt.get("at", 0))
                    for r in range(4):
                        ctx.inc(g.dma_start(out=KTl[:, cs(r, 2048)],
                                            in_=kvloc[2048 * r + 128 * hp:2048 * r + 128 * (hp + 1), :]),
                                "ktl", 16)
                    if hp > 0:
                        ctx.w(g, "bt", ctx.cnt.get("bt", 0))
                    for hh in range(2):
                        bbsrc = bass.AP(bb_d.tensor, (2 * hp + hh) * 32 * 128 * 128,
                                        [[128, 128], [128 * 128, 32], [1, 128]])
                        ctx.inc(g.dma_start(out=sb3(bb_bf[:, 4096 * hh:4096 * (hh + 1)], 32, 128),
                                            in_=bbsrc), "bbl", 16)
                    vt_b = ctx.cnt.get("vt_v", 0)
                    for kt in range(64):
                        r, lt = kt // 16, kt % 16
                        if kt >= 2:
                            ctx.w(y, "vt_v", vt_b + kt - 1)
                        if hp == 0 and kt == 0:
                            ctx.w(y, "kvl")
                            ctx.w(y, "vone")
                        if hp > 0 and kt == 0:
                            ctx.w(y, "hp_done", ctx.cnt.get("hp_done", 0))
                        ctx.inc(y.dma_start_transpose(vst[:, cs(kt % 2, 128)],
                                                      kvloc[2048 * r + 1024 + 128 * hp:2048 * r + 1024 + 128 * (hp + 1),
                                                            cs(lt, 128)]), "vt_y", 16)
                        ctx.w(v, "vt_y")
                        v.tensor_copy(Vl[:, 130 * kt:130 * kt + 64], vst[:, cs(kt % 2, 128)][:, 0:64])
                        ctx.inc(v.tensor_copy(Vl[:, 130 * kt + 65:130 * kt + 129],
                                              vst[:, cs(kt % 2, 128)][:, 64:128]), "vt_v")
                    for qh in range(2):
                        exb = ctx.cnt.get("ex", 0)
                        atb = ctx.cnt.get("at", 0)
                        avb = ctx.cnt.get("av", 0)
                        gi = 0
                        for kt in range(64):
                            kcol = 2048 * (kt // 16) + 128 * (kt % 16)
                            for qc in range(2):
                                off = 512 * qc
                                qcol = 2048 * hp + 1024 * qh + off
                                slot = gi % 2
                                if gi >= 2:
                                    ctx.w(t, "ex", exb + gi - 1)
                                if gi == 0:
                                    t.wait_ge(ctx.S("ktl"), ctx.cnt["ktl"])
                                    ctx.w(t, "ev", ctx.cnt.get("ev", 0))
                                t.matmul(psS[:, 1024 * slot:1024 * slot + 512],
                                         KTl[0:64, kcol:kcol + 128], QT[0:64, qcol:qcol + 512],
                                         start=True, stop=True, tile_position=(0, 0))
                                ctx.inc(t.matmul(psS[:, 1024 * slot + 512:1024 * slot + 1024],
                                                 KTl[64:128, kcol:kcol + 128], QT[64:128, qcol:qcol + 512],
                                                 start=True, stop=True, tile_position=(64, 0)), "at")
                                ctx.w(s, "at", atb + gi + 1)
                                if gi >= 2:
                                    ctx.w(s, "av", avb + gi - 1)
                                ctx.inc(s.activation(out=PT[:, 1024 * slot:1024 * slot + 1024],
                                                     in_=psS[:, 1024 * slot:1024 * slot + 1024],
                                                     func=AF.Exp, scale=SC1), "ex")
                                ctx.w(t, "ex", ctx.cnt["ex"])
                                ctx.w(t, "vt_v", vt_b + kt + 1)
                                t.matmul(psO1[0:65, off:off + 512], Vl[:, 130 * kt:130 * kt + 65],
                                         PT[:, 1024 * slot:1024 * slot + 512],
                                         start=(kt == 0), stop=False)
                                ctx.inc(t.matmul(psO2[0:65, off:off + 512], Vl[:, 130 * kt + 65:130 * kt + 130],
                                                 PT[:, 1024 * slot + 512:1024 * slot + 1024],
                                                 start=(kt == 0), stop=False), "av")
                                gi += 1
                        ctx.w(t, "ex", ctx.cnt["ex"])
                        ctx.w(t, "av", ctx.cnt["av"])
                        t.wait_ge(ctx.S("bbl"), ctx.cnt["bbl"])
                        if True:
                            tl0, tl1 = (0, 8) if qh == 0 else (8, 16)
                            for hh in range(2):
                                ins = None
                                for tt in range(tl0, tl1):
                                    for ss in range(tt + 32, 64):
                                        dpr = ss - tt - 32
                                        ins = t.matmul(psS[0:64, 1024 * hh + 128 * (tt - tl0):1024 * hh + 128 * (tt - tl0 + 1)],
                                                       Vl[:, 130 * ss + 65 * hh:130 * ss + 65 * hh + 64],
                                                       bb_bf[:, 4096 * hh + 128 * dpr:4096 * hh + 128 * (dpr + 1)],
                                                       start=(ss == tt + 32), stop=(ss == 63))
                                ctx.inc(ins, "bt")
                            ctx.w(v, "bt", ctx.cnt["bt"])
                            ctx.inc(v.tensor_copy(bias_sb[0:64, 0:2048], psS[0:64, 0:2048]), "bev")
                            ctx.w(t, "bev", ctx.cnt["bev"])
                            ins = None
                            for hh in range(2):
                                for kt in range(64):
                                    ins = t.matmul(psS[0:64, 1024 * hh + kt:1024 * hh + kt + 1],
                                                   Vl[:, 130 * kt + 65 * hh:130 * kt + 65 * hh + 64],
                                                   onesc_bf[:], start=True, stop=True)
                            ctx.inc(ins, "tsm")
                            ctx.w(v, "tsm", ctx.cnt["tsm"])
                            v.tensor_copy(ts_sb[0:64, 0:64], psS[0:64, 0:64])
                            ctx.inc(v.tensor_copy(ts_sb[0:64, 64:128], psS[0:64, 1024:1088]), "tse")
                            ctx.w(t, "tse", ctx.cnt["tse"])
                            for hh in range(2):
                                ctx.inc(t.transpose(psS[0:64, 0:64],
                                                    ts_sb[0:64, 64 * hh:64 * (hh + 1)], id_f[0:64, 0:64]), "tst")
                                ctx.w(v, "tst", ctx.cnt["tst"])
                                ctx.inc(v.tensor_copy(tst_bf[0:64, 64 * hh:64 * (hh + 1)], psS[0:64, 0:64]), "tsv")
                                ctx.w(t, "tsv", ctx.cnt["tsv"])
                            for hh in range(2):
                                ctx.inc(t.matmul(psS[0:64, 128 * hh:128 * hh + 16],
                                                 tst_bf[0:64, 64 * hh:64 * (hh + 1)], um_bf[:],
                                                 start=True, stop=True), "cum")
                            ctx.w(v, "cum", ctx.cnt["cum"])
                            for hh in range(2):
                                hgl = 2 * hp + hh
                                ctx.inc(v.tensor_scalar(out=cumnsc[0:64, 16 * hh:16 * (hh + 1)],
                                                        in0=psS[0:64, 128 * hh:128 * hh + 16],
                                                        scalar1=ch_sb[0:64, hgl:hgl + 1], scalar2=None,
                                                        op0=OP.mult), "csc")
                            ctx.w(v, "csc", ctx.cnt["csc"])
                        ctx.w(v, "av", ctx.cnt["av"])
                        v.reciprocal(out=rzb[64:65, 0:1024], in_=psO1[64:65, 0:1024])
                        ctx.inc(v.reciprocal(out=rzb[64:65, 1024:2048], in_=psO2[64:65, 0:1024]), "rzv")
                        ctx.w(t, "rzv", ctx.cnt["rzv"])
                        ins = None
                        for hh in range(2):
                            for c2 in range(2):
                                ins = t.matmul(psS[0:64, 1024 * hh + 512 * c2:1024 * hh + 512 * (c2 + 1)],
                                               ones_f[64:65, 0:64],
                                               rzb[64:65, 1024 * hh + 512 * c2:1024 * hh + 512 * (c2 + 1)],
                                               start=True, stop=True)
                        ctx.inc(ins, "rbt")
                        ctx.w(v, "rbt", ctx.cnt["rbt"])
                        ctx.inc(v.tensor_copy(rzb[0:64, 0:2048], psS[0:64, 0:2048]), "rzc")
                        ctx.w(v, "rzc")
                        acol = 2048 * hp + 1024 * qh
                        for hh, psO in ((0, psO1), (1, psO2)):
                            ctx.inc(v.tensor_tensor(out=tmp_sb[0:64, 1024 * hh:1024 * (hh + 1)],
                                                    in0=psO[0:64, 0:1024],
                                                    in1=rzb[0:64, 1024 * hh:1024 * (hh + 1)], op=OP.mult), "ev1")
                            ctx.w(v, "ev1")
                            ctx.inc(v.tensor_tensor(out=tmp_sb[0:64, 1024 * hh:1024 * (hh + 1)],
                                                    in0=tmp_sb[0:64, 1024 * hh:1024 * (hh + 1)],
                                                    in1=bias_sb[0:64, 1024 * hh:1024 * (hh + 1)],
                                                    op=OP.add), "ev1")
                            ctx.w(v, "ev1")
                            for tl in range(8):
                                tg = 8 * qh + tl
                                ctx.inc(v.tensor_scalar(
                                    out=AOT[64 * hh:64 * (hh + 1), acol + 128 * tl:acol + 128 * (tl + 1)],
                                    in0=tmp_sb[0:64, 1024 * hh + 128 * tl:1024 * hh + 128 * (tl + 1)],
                                    scalar1=cumnsc[0:64, 16 * hh + tg:16 * hh + tg + 1], scalar2=None,
                                    op0=OP.add), "ev2")
                        ctx.inc(v.tensor_copy(ctx.barscr[:, 6:7], ctx.barscr[:, 7:8]), "ev")
                    ctx.w(y, "ev", ctx.cnt.get("ev", 0))
                    ctx.inc(y.nop(), "hp_done")
                if "aot1" in dumps:
                    ctx.w(g, "ev", ctx.cnt.get("ev", 0))
                    ctx.inc(g.dma_start(out=dumps["aot1"], in_=AOT[:]), "dmp", 16)
            emit_full_barrier(ctx)
            exAtt.close()
            exQ.close()
        if phases >= 3:
            class _SkipRest(Exception):
                pass
            exMix = ctx.ex.enter_context(contextlib.ExitStack())
            sqs = exMix.enter_context(nc.sbuf_tensor("sqs", [128, 2048], F32))
            sm0 = exMix.enter_context(nc.sbuf_tensor("sm0", [1, 2048], F32))
            sm1 = exMix.enter_context(nc.sbuf_tensor("sm1", [1, 2048], F32))
            sm2 = exMix.enter_context(nc.sbuf_tensor("sm2", [1, 2048], F32))
            sm3 = exMix.enter_context(nc.sbuf_tensor("sm3", [1, 2048], F32))
            bcst = exMix.enter_context(nc.sbuf_tensor("bcst", [128, 2 * 2048], F32))
            y1T = exMix.enter_context(nc.sbuf_tensor("y1T", [128, 2048], BF16))
            y1f = exMix.enter_context(nc.sbuf_tensor("y1f", [128, NT * 1024], BF16))
            psm = exMix.enter_context(nc.psum_tensor("psm", [128, 2048], F32))
            psty = exMix.enter_context(nc.psum_tensor("psty", [128, 2048], BF16))
            if True:
                try:
                    if p3sub < 1:
                        raise _SkipRest
                    onescol = ones_f[0:128, 0:1]
                    ins = None
                    for dp in range(8):
                        for c4 in range(4):
                            ins = t.matmul(psm[0:1, cs(c4, 512)], onescol,
                                           AOT[:, 2048 * dp + 512 * c4:2048 * dp + 512 * (c4 + 1)],
                                           start=(dp == 0), stop=(dp == 7))
                    ctx.inc(ins, "mst")
                    ctx.w(v, "mst", ctx.cnt["mst"])
                    ctx.inc(v.tensor_copy(sm0[0:1, :], psm[0:1, 0:2048]), "mse")
                    if p3sub < 2:
                        raise _SkipRest
                    ctx.w(t, "mse", ctx.cnt["mse"])
                    for dp in range(8):
                        ctx.inc(s.activation(out=sqs[:, 0:2048], in_=AOT[:, cs(dp, 2048)], func=AF.Square), "msq")
                        t.wait_ge(ctx.S("msq"), ctx.cnt["msq"])
                        ins = None
                        for c4 in range(4):
                            ins = t.matmul(psm[0:1, cs(c4, 512)], onescol, sqs[:, cs(c4, 512)],
                                           start=(dp == 0), stop=(dp == 7))
                        ctx.inc(ins, "msm")
                        ctx.w(s, "msm", ctx.cnt["msm"])
                    ctx.w(v, "msm", ctx.cnt["msm"])
                    ctx.inc(v.tensor_copy(sm1[0:1, :], psm[0:1, 0:2048]), "mse")
                    ctx.w(v, "mse")
                    iv = 1.0 / 1024.0
                    ctx.inc(v.tensor_scalar(out=sm0[0:1, :], in0=sm0[0:1, :], scalar1=iv, scalar2=None, op0=OP.mult), "msr")
                    ctx.w(v, "msr")
                    ctx.inc(v.tensor_scalar(out=sm1[0:1, :], in0=sm1[0:1, :], scalar1=iv, scalar2=None, op0=OP.mult), "msr")
                    ctx.w(v, "msr")
                    ctx.inc(v.tensor_tensor(out=sm2[0:1, :], in0=sm0[0:1, :], in1=sm0[0:1, :], op=OP.mult), "msr")
                    ctx.w(v, "msr")
                    ctx.inc(v.tensor_tensor(out=sm1[0:1, :], in0=sm1[0:1, :], in1=sm2[0:1, :], op=OP.subtract), "msr")
                    ctx.w(s, "msr", ctx.cnt["msr"])
                    ctx.inc(s.activation(out=sm2[0:1, :], in_=sm1[0:1, :], func=AF.Ln, bias=eps_c[0:1, 0:1]), "msl")
                    ctx.w(s, "msl")
                    ctx.inc(s.activation(out=sm2[0:1, :], in_=sm2[0:1, :], func=AF.Exp, scale=-0.5), "msl")
                    ctx.w(v, "msl", ctx.cnt["msl"])
                    ctx.inc(v.tensor_tensor(out=sm3[0:1, :], in0=sm0[0:1, :], in1=sm2[0:1, :], op=OP.mult), "msr")
                    ctx.w(t, "msr", ctx.cnt["msr"])
                    ctx.w(t, "mse", ctx.cnt["mse"])
                    if p3sub < 3:
                        raise _SkipRest
                    for k, src_row in ((0, sm2), (1, sm3)):
                        ins = None
                        for c4 in range(4):
                            ins = t.matmul(psm[:, cs(c4, 512)], onesr_f[0:1, :],
                                           src_row[0:1, cs(c4, 512)], start=True, stop=True)
                        ctx.inc(ins, "mbc")
                        ctx.w(v, "mbc", ctx.cnt["mbc"])
                        ctx.inc(v.tensor_copy(bcst[:, cs(k, 2048)], psm[:, 0:2048]), "mbe")
                        ctx.w(t, "mbe", ctx.cnt["mbe"])
                    if p3sub < 4:
                        raise _SkipRest
                    ctx.S("ytt"); ctx.S("ytr")
                    ytt0 = ctx.cnt.get("ytt", 0)
                    ytr0 = ctx.cnt.get("ytr", 0)
                    for dp in range(8 if p3sub >= 7 else 1):
                        ctx.w(v, "mbe", ctx.cnt["mbe"])
                        if dp > 0:
                            ctx.w(v, "ytr", ctx.cnt.get("ytr", 0))
                        ctx.inc(v.tensor_tensor(out=sqs[:, 0:2048], in0=AOT[:, cs(dp, 2048)],
                                                in1=bcst[:, 0:2048], op=OP.mult), "ym")
                        ctx.w(v, "ym")
                        ctx.inc(v.tensor_tensor(out=sqs[:, 0:2048], in0=sqs[:, 0:2048],
                                                in1=bcst[:, 2048:4096], op=OP.subtract), "ym")
                        ctx.w(v, "ym")
                        ctx.inc(v.tensor_scalar(out=y1T[:, 0:2048], in0=sqs[:, 0:2048],
                                                scalar1=gam_sb[:, dp:dp + 1], scalar2=bet_sb[:, dp:dp + 1],
                                                op0=OP.mult, op1=OP.add), "ym")
                        ctx.w(t, "ym", ctx.cnt["ym"])
                        for tc in range(NT if p3sub >= 5 else 0):
                            jt = ctx.cnt["ytt"] - ytt0
                            sl2 = (jt % 2) * 1024
                            if jt >= 2:
                                ctx.w(t, "ytr", ytr0 + jt - 1)
                            ctx.inc(t.transpose(psty[:, sl2:sl2 + 128], y1T[:, cs(tc, 128)], id_bf[:]), "ytt")
                            ctx.w(v, "ytt", ytt0 + jt + 1)
                            ctx.inc(v.tensor_copy(y1f[:, 1024 * tc + 128 * dp:1024 * tc + 128 * (dp + 1)],
                                                  psty[:, sl2:sl2 + 128]), "ytr")
                    if p3sub < 6:
                        raise _SkipRest
                    ctx.w(g, "ytr", ctx.cnt.get("ytr", 0))
                    ctx.inc(g.dma_start(out=ap3(y1d, NT, 1024), in_=sb3(y1f[:], NT, 1024)), "y1w", 16)
                    g.wait_ge(ctx.S("y1w"), ctx.cnt["y1w"])
                    if "y1" in dumps:
                        ctx.inc(g.dma_start(out=dumps["y1"], in_=y1d), "dmp", 16)
                except _SkipRest:
                    pass
                finally:
                    emit_full_barrier(ctx)
            exMix.close()
            exA1.close()

        if phases >= 4:
            def hsv(instr, name="mgh"):
                ctx.inc(instr, name)
                ctx.w(v, name)
                return instr

            exMoe = ctx.ex.enter_context(contextlib.ExitStack())
            iotaq = exMoe.enter_context(nc.sbuf_tensor("iotaq", [128, 4096], F32))
            gq = exMoe.enter_context(nc.sbuf_tensor("gq", [128, 4 * 16], F32))
            m1f = exMoe.enter_context(nc.sbuf_tensor("m1f", [128, 16 * 8], F32))
            m2f = exMoe.enter_context(nc.sbuf_tensor("m2f", [128, 16 * 8], F32))
            m1b = exMoe.enter_context(nc.sbuf_tensor("m1b", [128, 16 * 8], BF16))
            m2b = exMoe.enter_context(nc.sbuf_tensor("m2b", [128, 16 * 8], BF16))
            m1mb = exMoe.enter_context(nc.sbuf_tensor("m1mb", [128, 16 * 8], BF16))
            gidxf2 = exMoe.enter_context(nc.sbuf_tensor("gidxf2", [128, 32], F32))
            gscu = exMoe.enter_context(nc.sbuf_tensor("gscu", [128, 8], U32))
            iota8 = exMoe.enter_context(nc.sbuf_tensor("iota8", [128, 8], F32))
            gsc = exMoe.enter_context(nc.sbuf_tensor("gsc", [128, 24], F32))
            if True:
                q1c, g1c = gq[:, 0:16], gq[:, 16:32]
                q2c, g2c = gq[:, 32:48], gq[:, 48:64]
                ctx.inc(g.iota(iotaq[:], [[1, 4096]], channel_multiplier=0,
                               allow_small_or_imprecise_dtypes=True), "mio")
                ctx.inc(g.iota(iota8[:], [[1, 8]], channel_multiplier=0,
                               allow_small_or_imprecise_dtypes=True), "mio")
                g.wait_ge(ctx.S("dxn"), ctx.cnt["dxn"])
                exD = ctx.ex.enter_context(contextlib.ExitStack())
                xng = exD.enter_context(nc.sbuf_tensor("xng", [128, NT * 1024], BF16))
                exDsp = ctx.ex.enter_context(contextlib.ExitStack())
                xtf = exDsp.enter_context(nc.sbuf_tensor("xtf", [128, 2 * 1024], F32))
                xngT = exDsp.enter_context(nc.sbuf_tensor("xngT", [128, 8 * 2048], F32))
                rt_sb = exDsp.enter_context(nc.sbuf_tensor("rt_sb", [128, 64], F32))
                lg_sb = exDsp.enter_context(nc.sbuf_tensor("lg_sb", [8, 2048], F32))
                lgt = exDsp.enter_context(nc.sbuf_tensor("lgt", [128, 16 * 8], F32))
                exs = exDsp.enter_context(nc.sbuf_tensor("exs", [128, 16 * 8], F32))
                p1f = exDsp.enter_context(nc.sbuf_tensor("p1f", [128, 16 * 8], F32))
                p2f = exDsp.enter_context(nc.sbuf_tensor("p2f", [128, 16 * 8], F32))
                m1cb = exDsp.enter_context(nc.sbuf_tensor("m1cb", [128, 8], F32))
                Dh = exDsp.enter_context(nc.sbuf_tensor("Dh", [128, 8 * 4096], BF16))
                eist = exDsp.enter_context(nc.sbuf_tensor("eist", [128, 2 * 1024], BF16))
                psg = exDsp.enter_context(nc.psum_tensor("psg", [128, 512], F32))
                psgf = exDsp.enter_context(nc.psum_tensor("psgf", [128, 2048], F32))
                psd1 = psgf[:, 0:1024]
                psd2 = psgf[:, 1024:2048]
                if True:
                    ctx.inc(g.dma_start(out=sb3(rt_sb[:], 8, 8), in_=ap3(rt_d, 8, 8)), "mld", 16)
                    for k in range(NT):
                        sl = k % 2
                        if k >= 2:
                            ctx.w(g, "tr_v", ctx.cnt.get("tr_v", 0) - 8)
                            ctx.w(g, "xgc", ctx.cnt.get("xgc", 0) - 1)
                        ctx.inc(g.dma_start(out=sb3(xtf[:, cs(sl, 1024)], 1, 1024),
                                            in_=ap3(xnin, 1, 1024, off_elems=1024 * 128 * k)), "mxf", 16)
                        t.wait_ge(ctx.S("mxf"), ctx.cnt["mxf"])
                        v.wait_ge(ctx.S("mxf"), ctx.cnt["mxf"])
                        ctx.inc(v.tensor_copy(xng[:, cs(k, 1024)], xtf[:, cs(sl, 1024)]), "xgc")
                        for j in range(8):
                            ctx.S("tr_t"); ctx.S("tr_v")
                            jt = ctx.cnt["tr_t"]
                            sl2 = (jt % 2) * 512
                            if jt >= 2:
                                ctx.w(t, "tr_v", jt - 1)
                            ctx.inc(t.transpose(psgf[:, sl2:sl2 + 128],
                                                xtf[:, cs(sl, 1024)][:, cs(j, 128)], id_f[:]), "tr_t")
                            ctx.w(v, "tr_t", jt + 1)
                            ctx.inc(v.tensor_copy(xngT[:, 2048 * j + 128 * k:2048 * j + 128 * (k + 1)],
                                                  psgf[:, sl2:sl2 + 128]), "tr_v")
                    ctx.w(t, "tr_v")
                    t.wait_ge(ctx.S("mld"), ctx.cnt["mld"])
                    ins = None
                    for mp in range(8):
                        for c4 in range(4):
                            ins = t.matmul(psgf[0:8, cs(c4, 512)], rt_sb[:, 8 * mp:8 * (mp + 1)],
                                           xngT[:, 2048 * mp + 512 * c4:2048 * mp + 512 * (c4 + 1)],
                                           start=(mp == 0), stop=(mp == 7))
                    ctx.inc(ins, "rlt")
                    ctx.w(v, "rlt", ctx.cnt["rlt"])
                    ctx.inc(v.tensor_copy(lg_sb[0:8, :], psgf[0:8, 0:2048]), "rle")
                    ctx.w(t, "rle", ctx.cnt["rle"])
                    for tt in range(NT):
                        ctx.inc(t.transpose(psg[0:128, 0:8], lg_sb[0:8, cs(tt, 128)], id_f[0:8, 0:8]), "rtt")
                        ctx.w(v, "rtt", ctx.cnt["rtt"])
                        ctx.inc(v.tensor_copy(lgt[:, cs(tt, 8)], psg[0:128, 0:8]), "rtv")
                        ctx.w(t, "rtv", ctx.cnt["rtv"])
                    if "lgt" in dumps:
                        ctx.w(g, "rtv", ctx.cnt["rtv"])
                        ctx.inc(g.dma_start(out=bass.AP(dumps["lgt"].tensor, 0, [[8, 128], [128 * 8, 16], [1, 8]]),
                                            in_=sb3(lgt[:], 16, 8)), "dmp", 16)
                    for tt in range(NT):
                        L = lgt[:, cs(tt, 8)]
                        hsv(v.max(gsc[:, 0:8], L))
                        hsv(v.tensor_scalar(out=gsc[:, 8:9], in0=gsc[:, 0:1], scalar1=-1.0, scalar2=None, op0=OP.mult))
                        ctx.w(s, "mgh", ctx.cnt["mgh"])
                        ctx.inc(s.activation(out=exs[:, cs(tt, 8)], in_=L, func=AF.Exp, bias=gsc[:, 8:9]), "mge")
                        ctx.w(v, "mge")
                        EXT = exs[:, cs(tt, 8)]
                        hsv(v.reduce_sum(out=gsc[:, 9:10], in_=EXT, axis=AX.X))
                        hsv(v.reciprocal(out=gsc[:, 10:11], in_=gsc[:, 9:10]))
                        hsv(v.max(gsc[:, 12:20], EXT))
                        hsv(v.max_index(gscu[:, 0:8], gsc[:, 12:20], EXT))
                        hsv(v.tensor_copy(gidxf2[:, 2 * tt:2 * tt + 2], gscu[:, 0:2]))
                        hsv(v.tensor_tensor(out=g1c[:, tt:tt + 1], in0=gsc[:, 12:13], in1=gsc[:, 10:11], op=OP.mult))
                        hsv(v.tensor_tensor(out=g2c[:, tt:tt + 1], in0=gsc[:, 13:14], in1=gsc[:, 10:11], op=OP.mult))
                        hsv(v.tensor_scalar(out=m1f[:, cs(tt, 8)], in0=iota8[:], scalar1=gidxf2[:, 2 * tt:2 * tt + 1],
                                            scalar2=None, op0=OP.is_equal))
                        hsv(v.tensor_scalar(out=m2f[:, cs(tt, 8)], in0=iota8[:], scalar1=gidxf2[:, 2 * tt + 1:2 * tt + 2],
                                            scalar2=None, op0=OP.is_equal))
                        hsv(v.tensor_copy(m1b[:, cs(tt, 8)], m1f[:, cs(tt, 8)]))
                        hsv(v.tensor_copy(m2b[:, cs(tt, 8)], m2f[:, cs(tt, 8)]))
                    ctx.w(t, "mgh", ctx.cnt["mgh"])
                    for (mb, pf, nm) in ((m1b, p1f, "c1m"), (m2b, p2f, "c2m")):
                        for tt in range(NT):
                            ins = None
                            for tp in range(tt + 1):
                                ins = t.matmul(psg[0:128, 0:8],
                                               allon_bf[:] if tp < tt else lt_bf[:],
                                               mb[:, cs(tp, 8)], start=(tp == 0), stop=(tp == tt))
                            ctx.inc(ins, nm)
                            ctx.w(v, nm, ctx.cnt[nm])
                            ctx.inc(v.tensor_copy(pf[:, cs(tt, 8)], psg[0:128, 0:8]), nm + "e")
                            ctx.w(t, nm + "e", ctx.cnt[nm + "e"])
                    ctx.w(v, "c2me", ctx.cnt["c2me"])
                    for tt in range(NT):
                        hsv(v.tensor_scalar(out=gsc[:, 0:8], in0=p1f[:, cs(tt, 8)], scalar1=float(CAP),
                                            scalar2=None, op0=OP.is_lt))
                        hsv(v.tensor_tensor(out=m1f[:, cs(tt, 8)], in0=m1f[:, cs(tt, 8)], in1=gsc[:, 0:8], op=OP.mult))
                        hsv(v.tensor_copy(m1mb[:, cs(tt, 8)], m1f[:, cs(tt, 8)]))
                        hsv(v.tensor_tensor(out=gsc[:, 0:8], in0=p1f[:, cs(tt, 8)], in1=m1f[:, cs(tt, 8)], op=OP.mult))
                        hsv(v.reduce_sum(out=gsc[:, 8:9], in_=gsc[:, 0:8], axis=AX.X))
                        hsv(v.reduce_sum(out=gsc[:, 9:10], in_=m1f[:, cs(tt, 8)], axis=AX.X))
                        hsv(v.tensor_scalar(out=gsc[:, 10:11], in0=gidxf2[:, 2 * tt:2 * tt + 1], scalar1=512.0,
                                            scalar2=None, op0=OP.mult))
                        hsv(v.tensor_tensor(out=gsc[:, 10:11], in0=gsc[:, 10:11], in1=gsc[:, 8:9], op=OP.add))
                        hsv(v.tensor_scalar(out=gsc[:, 11:12], in0=gsc[:, 9:10], scalar1=-1e6,
                                            scalar2=1e6, op0=OP.mult, op1=OP.add))
                        hsv(v.tensor_tensor(out=q1c[:, tt:tt + 1], in0=gsc[:, 10:11], in1=gsc[:, 11:12], op=OP.add))
                        hsv(v.tensor_tensor(out=g1c[:, tt:tt + 1], in0=g1c[:, tt:tt + 1], in1=gsc[:, 9:10], op=OP.mult))
                    ctx.w(t, "mgh", ctx.cnt["mgh"])
                    ins = None
                    for tt in range(NT):
                        ins = t.matmul(psg[0:1, 0:8], onesc_bf[:], m1mb[:, cs(tt, 8)],
                                       start=(tt == 0), stop=(tt == NT - 1))
                    ctx.inc(ins, "m1c")
                    ctx.w(v, "m1c", ctx.cnt["m1c"])
                    ctx.inc(v.tensor_copy(gsc[0:1, 16:24][0:1, 0:8], psg[0:1, 0:8]), "m1ce")
                    ctx.w(t, "m1ce", ctx.cnt["m1ce"])
                    ctx.inc(t.matmul(psg[0:128, 8:16], onesr_f[0:1, :], gsc[0:1, 16:24][0:1, 0:8],
                                     start=True, stop=True), "m1cb")
                    ctx.w(v, "m1cb", ctx.cnt["m1cb"])
                    ctx.inc(v.tensor_copy(m1cb[:], psg[0:128, 8:16]), "m1cbe")
                    ctx.w(v, "m1cbe")
                    for tt in range(NT):
                        hsv(v.tensor_tensor(out=p2f[:, cs(tt, 8)], in0=p2f[:, cs(tt, 8)], in1=m1cb[:], op=OP.add))
                        hsv(v.tensor_scalar(out=gsc[:, 0:8], in0=p2f[:, cs(tt, 8)], scalar1=float(CAP),
                                            scalar2=None, op0=OP.is_lt))
                        hsv(v.tensor_tensor(out=m2f[:, cs(tt, 8)], in0=m2f[:, cs(tt, 8)], in1=gsc[:, 0:8], op=OP.mult))
                        hsv(v.tensor_tensor(out=gsc[:, 0:8], in0=p2f[:, cs(tt, 8)], in1=m2f[:, cs(tt, 8)], op=OP.mult))
                        hsv(v.reduce_sum(out=gsc[:, 8:9], in_=gsc[:, 0:8], axis=AX.X))
                        hsv(v.reduce_sum(out=gsc[:, 9:10], in_=m2f[:, cs(tt, 8)], axis=AX.X))
                        hsv(v.tensor_scalar(out=gsc[:, 10:11], in0=gidxf2[:, 2 * tt + 1:2 * tt + 2], scalar1=512.0,
                                            scalar2=None, op0=OP.mult))
                        hsv(v.tensor_tensor(out=gsc[:, 10:11], in0=gsc[:, 10:11], in1=gsc[:, 8:9], op=OP.add))
                        hsv(v.tensor_scalar(out=gsc[:, 11:12], in0=gsc[:, 9:10], scalar1=-1e6,
                                            scalar2=1e6, op0=OP.mult, op1=OP.add))
                        hsv(v.tensor_tensor(out=q2c[:, tt:tt + 1], in0=gsc[:, 10:11], in1=gsc[:, 11:12], op=OP.add))
                        hsv(v.tensor_tensor(out=g2c[:, tt:tt + 1], in0=g2c[:, tt:tt + 1], in1=gsc[:, 9:10], op=OP.mult))
                    if "gat" in dumps:
                        ctx.w(g, "mgh", ctx.cnt["mgh"])
                        for kk in range(4):
                            ctx.inc(g.dma_start(out=bass.AP(dumps["gat"].tensor, kk * 128 * 16, [[16, 128], [1, 16]]),
                                                in_=gq[:, 16 * kk:16 * (kk + 1)]), "dmp", 16)
                    for half in range(2):
                        for k8 in range(8):
                            tt = 8 * half + k8
                            if ctx.cnt.get("dsp", 0):
                                ctx.w(v, "dsp", ctx.cnt["dsp"])
                            hsv(v.tensor_scalar(out=Dh[:, cs(k8, 4096)], in0=iotaq[:], scalar1=q1c[:, tt:tt + 1],
                                                scalar2=None, op0=OP.is_equal), "mgh")
                            ctx.inc(v.scalar_tensor_tensor(out=Dh[:, cs(k8, 4096)], in0=iotaq[:],
                                                           scalar=q2c[:, tt:tt + 1], in1=Dh[:, cs(k8, 4096)],
                                                           op0=OP.is_equal, op1=OP.add), "dbd")
                            ctx.w(v, "dbd")
                        ctx.w(t, "dbd", ctx.cnt["dbd"])
                        for qt in range(32):
                            ps = psd1 if qt % 2 == 0 else psd2
                            nev = ctx.cnt.get("dse", 0)
                            if nev >= 2:
                                ctx.w(t, "dse", nev - 1)
                            ins = None
                            for k8 in range(8):
                                for c2 in range(2):
                                    ins = t.matmul(ps[:, cs(c2, 512)],
                                                   Dh[:, 4096 * k8 + 128 * qt:4096 * k8 + 128 * (qt + 1)],
                                                   xng[:, 1024 * (8 * half + k8) + 512 * c2:1024 * (8 * half + k8) + 512 * (c2 + 1)],
                                                   start=(k8 == 0), stop=(k8 == 7))
                            ctx.inc(ins, "dsp")
                            ctx.w(v, "dsp", ctx.cnt["dsp"])
                            nst = ctx.cnt.get("eiw", 0)
                            if nst >= 32:
                                ctx.w(v, "eiw", nst - 16)
                            ctx.inc(v.tensor_copy(eist[:, cs(qt % 2, 1024)], ps[:, 0:1024]), "dse")
                            ctx.w(g, "dse", ctx.cnt["dse"])
                            if half == 0:
                                ctx.inc(g.dma_start(out=eiin[128 * qt:128 * (qt + 1), :],
                                                    in_=eist[:, cs(qt % 2, 1024)]), "eiw", 16)
                            else:
                                ctx.inc(g.dma_start(out=eiin[128 * qt:128 * (qt + 1), :],
                                                    in_=eist[:, cs(qt % 2, 1024)], accum_op=OP.add), "eiw", 16)
                    g.wait_ge(ctx.S("eiw"), ctx.cnt["eiw"])
                    ctx.inc(g.collective_compute("AllGather", OP.bypass, replica_groups=[list(range(NCORE))],
                                                 ins=[eiin], outs=[eiag]), "eic")
                    if "ei" in dumps:
                        ctx.w(g, "eic")
                        ctx.inc(g.dma_start(out=dumps["ei"], in_=eiin), "dmp", 16)
                    emit_full_barrier(ctx)
                    exDsp.close()
                exD.close()

                NHC = 43
                exExp = ctx.ex.enter_context(contextlib.ExitStack())
                XeT = exExp.enter_context(nc.sbuf_tensor("XeT", [128, 2 * 8 * 512], BF16))
                hsT = exExp.enter_context(nc.sbuf_tensor("hsT", [128, NHC * 512], BF16))
                w01 = exExp.enter_context(nc.sbuf_tensor("w01", [128, 4 * 1024], BF16))
                wos = exExp.enter_context(nc.sbuf_tensor("wos", [128, 43 * 1024], BF16))
                gsx = exExp.enter_context(nc.sbuf_tensor("gsx", [128, 5 * 512], F32))
                eos = exExp.enter_context(nc.sbuf_tensor("eos", [128, 2 * 1024], BF16))
                ph0 = exExp.enter_context(nc.psum_tensor("ph0", [128, 1024], F32))
                ph1 = exExp.enter_context(nc.psum_tensor("ph1", [128, 1024], F32))
                pso1 = exExp.enter_context(nc.psum_tensor("pso1", [128, 1024], F32))
                pso2 = exExp.enter_context(nc.psum_tensor("pso2", [128, 1024], F32))
                if True:
                    ctx.w(g, "eic")
                    with g.register("ebase") as ebase:
                        pid = g.partition_id()
                        g.reg_mov(ebase, pid)
                        g.reg_mul(ebase, ebase, 512 * 1024)
                        src = bass.AP(eiag.tensor, ebase,
                                      [[4096 * 1024, NCORE], [16384, 32], [1, 16384]])
                        dst = bass.AP(eloc.tensor, 0,
                                      [[512 * 1024, NCORE], [16384, 32], [1, 16384]])
                        ctx.inc(g.dma_start(out=dst, in_=src), "elc", 16)
                    g.wait_ge(ctx.S("elc"), ctx.cnt["elc"])
                    y.wait_ge(ctx.S("elc"), ctx.cnt["elc"])
                    if "elo" in dumps:
                        with nc.sbuf_tensor("elstg", [128, 1024], BF16) as elstg:
                            for j in range(32):
                                ctx.w(g, "dmp", ctx.cnt.get("dmp", 0))
                                ctx.inc(g.dma_start(out=elstg[:], in_=eloc[128 * j:128 * (j + 1), :]), "dmp", 16)
                                g.wait_ge(ctx.S("dmp"), ctx.cnt["dmp"])
                                ctx.inc(g.dma_start(out=dumps["elo"][128 * j:128 * (j + 1), :], in_=elstg[:]), "dmp", 16)
                            g.wait_ge(ctx.S("dmp"), ctx.cnt["dmp"])
                    for sc in range(8):
                        ssl = sc % 2
                        if sc >= 2:
                            ctx.w(y, "h1m", ctx.cnt.get("h1m", 0) - NHC)
                        for sb in range(4):
                            for mb in range(8):
                                ctx.inc(y.dma_start_transpose(
                                    XeT[:, 4096 * ssl + 512 * mb + 128 * sb:4096 * ssl + 512 * mb + 128 * (sb + 1)],
                                    eloc[512 * sc + 128 * sb:512 * sc + 128 * (sb + 1), 128 * mb:128 * (mb + 1)]),
                                    "xet", 16)
                        t.wait_ge(ctx.S("xet"), ctx.cnt["xet"])
                        if sc > 0:
                            ctx.w(g, "eom", ctx.cnt.get("eom", 0))
                        for hc in range(NHC):
                            hw = 88 if hc == NHC - 1 else 128
                            ctx.inc(g.dma_start(out=wos[:, cs(hc, 1024)][0:hw, :] if hw < 128 else wos[:, cs(hc, 1024)],
                                                in_=wo_d[128 * hc:128 * hc + hw, :]), "wol2", 16)
                        for hc in range(NHC):
                            hw = 88 if hc == NHC - 1 else 128
                            wsl = hc % 2
                            nwl = ctx.cnt.get("wst2", 0)
                            if nwl >= 64:
                                ctx.w(g, "h1m", ctx.cnt.get("h1m", 0) - 1)
                            w0src = bass.AP(w0_d.tensor, 128 * hc, [[HMOE, 128], [128 * HMOE, 8], [1, hw]])
                            w1src = bass.AP(w1_d.tensor, 128 * hc, [[HMOE, 128], [128 * HMOE, 8], [1, hw]])
                            ctx.inc(g.dma_start(out=sb3(w01[:, cs(2 * wsl, 1024)], 8, 128)[:, :, 0:hw], in_=w0src), "wst2", 16)
                            ctx.inc(g.dma_start(out=sb3(w01[:, cs(2 * wsl + 1, 1024)], 8, 128)[:, :, 0:hw], in_=w1src), "wst2", 16)
                            t.wait_ge(ctx.S("wst2"), ctx.cnt["wst2"])
                            phA = ph0 if hc % 2 == 0 else ph1
                            nsg = ctx.cnt.get("sgx", 0)
                            if nsg >= 2:
                                ctx.w(t, "sgx", nsg - 1)
                            ins0, ins1 = None, None
                            for mp in range(8):
                                ins0 = t.matmul(phA[0:hw, 0:512],
                                                w01[:, 1024 * (2 * wsl) + 128 * mp:1024 * (2 * wsl) + 128 * mp + hw],
                                                XeT[:, 4096 * (sc % 2) + 512 * mp:4096 * (sc % 2) + 512 * (mp + 1)],
                                                start=(mp == 0), stop=(mp == 7))
                            ctx.inc(ins0, "h0m")
                            for mp in range(8):
                                ins1 = t.matmul(phA[0:hw, 512:1024],
                                                w01[:, 1024 * (2 * wsl + 1) + 128 * mp:1024 * (2 * wsl + 1) + 128 * mp + hw],
                                                XeT[:, 4096 * (sc % 2) + 512 * mp:4096 * (sc % 2) + 512 * (mp + 1)],
                                                start=(mp == 0), stop=(mp == 7))
                            ctx.inc(ins1, "h1m")
                            ctx.w(s, "h1m", ctx.cnt["h1m"])
                            ctx.inc(s.activation(out=gsx[0:hw, 0:512], in_=phA[0:hw, 0:512], func=AF.Square), "gsq")
                            ctx.w(v, "gsq")
                            hsv(v.tensor_scalar(out=gsx[0:hw, 512:1024], in0=gsx[0:hw, 0:512],
                                                scalar1=0.044715 * 0.7978845608028654, scalar2=0.7978845608028654,
                                                op0=OP.mult, op1=OP.add))
                            hsv(v.tensor_tensor(out=gsx[0:hw, 1024:1536], in0=gsx[0:hw, 512:1024],
                                                in1=phA[0:hw, 0:512], op=OP.mult))
                            ctx.inc(v.tensor_copy(ctx.barscr[:, 6:7], ctx.barscr[:, 7:8]), "gs2")
                            ctx.w(s, "gs2", ctx.cnt["gs2"])
                            ctx.inc(s.activation(out=gsx[0:hw, 1536:2048], in_=gsx[0:hw, 1024:1536], func=AF.Tanh), "gth")
                            ctx.w(v, "gth")
                            hsv(v.tensor_scalar(out=gsx[0:hw, 1536:2048], in0=gsx[0:hw, 1536:2048],
                                                scalar1=1.0, scalar2=0.5, op0=OP.add, op1=OP.mult))
                            hsv(v.tensor_tensor(out=gsx[0:hw, 2048:2560], in0=gsx[0:hw, 1536:2048],
                                                in1=phA[0:hw, 0:512], op=OP.mult))
                            ctx.inc(v.tensor_tensor(out=hsT[0:hw, cs(hc, 512)], in0=gsx[0:hw, 2048:2560],
                                                    in1=phA[0:hw, 512:1024], op=OP.mult), "sgx")
                            ctx.w(v, "sgx")
                        t.wait_ge(ctx.S("wol2"), ctx.cnt["wol2"])
                        ctx.w(t, "sgx", ctx.cnt["sgx"])
                        for st4 in range(4):
                            pso = pso1 if st4 % 2 == 0 else pso2
                            nev = ctx.cnt.get("eoe", 0)
                            if nev >= 2:
                                ctx.w(t, "eoe", nev - 1)
                            ins = None
                            for hc in range(NHC):
                                hw = 88 if hc == NHC - 1 else 128
                                for c2 in range(2):
                                    ins = t.matmul(pso[:, cs(c2, 512)],
                                                   hsT[0:hw, 512 * hc + 128 * st4:512 * hc + 128 * (st4 + 1)],
                                                   wos[0:hw, 1024 * hc + 512 * c2:1024 * hc + 512 * (c2 + 1)],
                                                   start=(hc == 0), stop=(hc == NHC - 1))
                            ctx.inc(ins, "eom")
                            ctx.w(v, "eom", ctx.cnt["eom"])
                            nei = ctx.cnt.get("eow", 0)
                            if nei >= 32:
                                ctx.w(v, "eow", nei - 16)
                            ctx.inc(v.tensor_copy(eos[:, cs(st4 % 2, 1024)], pso[:, 0:1024]), "eoe")
                            ctx.w(g, "eoe", ctx.cnt["eoe"])
                            ctx.inc(g.dma_start(out=eoin[512 * sc + 128 * st4:512 * sc + 128 * (st4 + 1), :],
                                                in_=eos[:, cs(st4 % 2, 1024)]), "eow", 16)
                    g.wait_ge(ctx.S("eow"), ctx.cnt["eow"])
                    ctx.inc(g.collective_compute("AllGather", OP.bypass, replica_groups=[list(range(NCORE))],
                                                 ins=[eoin], outs=[eoag]), "eoc")
                    if "eo" in dumps:
                        ctx.w(g, "eoc")
                        with nc.sbuf_tensor("eostg", [128, 1024], BF16) as eostg:
                            for j in range(32):
                                ctx.w(g, "dmp", ctx.cnt.get("dmp", 0))
                                ctx.inc(g.dma_start(out=eostg[:], in_=eoin[128 * j:128 * (j + 1), :]), "dmp", 16)
                                g.wait_ge(ctx.S("dmp"), ctx.cnt["dmp"])
                                ctx.inc(g.dma_start(out=dumps["eo"][128 * j:128 * (j + 1), :], in_=eostg[:]), "dmp", 16)
                            g.wait_ge(ctx.S("dmp"), ctx.cnt["dmp"])
                    emit_full_barrier(ctx)

                    exExp.close()
                exCmb = ctx.ex.enter_context(contextlib.ExitStack())
                eosb = exCmb.enter_context(nc.sbuf_tensor("eosb", [128, 32 * 1024], BF16))
                cmbr = exCmb.enter_context(nc.sbuf_tensor("cmb", [128, 4096], BF16))
                cmbT = exCmb.enter_context(nc.sbuf_tensor("cmbT", [128, 4096], BF16))
                ffst = exCmb.enter_context(nc.sbuf_tensor("ffst", [128, 2 * 1024], F32))
                psct = exCmb.enter_context(nc.psum_tensor("psct", [128, 2048], BF16))
                psc1 = exCmb.enter_context(nc.psum_tensor("psc1", [128, 1024], F32))
                psc2 = exCmb.enter_context(nc.psum_tensor("psc2", [128, 1024], F32))
                if True:
                    ctx.w(g, "eoc")
                    with g.register("obase") as obase:
                        pid = g.partition_id()
                        g.reg_mov(obase, pid)
                        g.reg_mul(obase, obase, 512 * 1024)
                        src = bass.AP(eoag.tensor, obase,
                                      [[4096 * 1024, NCORE], [16384, 32], [1, 16384]])
                        dst = bass.AP(eocd.tensor, 0,
                                      [[512 * 1024, NCORE], [16384, 32], [1, 16384]])
                        ctx.inc(g.dma_start(out=dst, in_=src), "eor", 16)
                    g.wait_ge(ctx.S("eor"), ctx.cnt["eor"])
                    for ee in range(NCORE):
                        src = bass.AP(eocd.tensor, 512 * 1024 * ee,
                                      [[1024, 128], [128 * 1024, 4], [1, 1024]])
                        ctx.inc(g.dma_start(out=sb3(eosb[:, 1024 * 4 * ee:1024 * 4 * (ee + 1)], 4, 1024),
                                            in_=src), "eor", 16)
                    t.wait_ge(ctx.S("eor"), ctx.cnt["eor"])
                    for st in range(NT):
                        nts = ctx.cnt.get("ctt", 0)
                        if nts:
                            ctx.w(v, "ctt", nts)
                        hsv(v.tensor_scalar(out=cmbr[:, 0:4096], in0=iotaq[:], scalar1=q1c[:, st:st + 1],
                                            scalar2=g1c[:, st:st + 1], op0=OP.is_equal, op1=OP.mult))
                        ctx.inc(v.tensor_scalar(out=cmbT[:, 0:4096], in0=iotaq[:], scalar1=q2c[:, st:st + 1],
                                                scalar2=g2c[:, st:st + 1], op0=OP.is_equal, op1=OP.mult), "cb2")
                        ctx.w(v, "cb2")
                        ctx.inc(v.tensor_tensor(out=cmbr[:, 0:4096], in0=cmbr[:, 0:4096], in1=cmbT[:, 0:4096],
                                                op=OP.add), "cbd")
                        ctx.w(t, "cbd", ctx.cnt["cbd"])
                        for qt in range(32):
                            jt = ctx.cnt.get("ctr", 0)
                            sl2 = (jt % 2) * 1024
                            if jt >= 2:
                                ctx.w(t, "ctt", jt - 1)
                            ctx.inc(t.transpose(psct[:, sl2:sl2 + 128], cmbr[:, cs(qt, 128)], id_bf[:]), "ctr")
                            ctx.w(v, "ctr", jt + 1)
                            ctx.inc(v.tensor_copy(cmbT[:, cs(qt, 128)], psct[:, sl2:sl2 + 128]), "ctt")
                        ctx.w(t, "ctt", ctx.cnt["ctt"])
                        ps = psc1 if st % 2 == 0 else psc2
                        nev = ctx.cnt.get("ffe", 0)
                        if nev >= 2:
                            ctx.w(t, "ffe", nev - 1)
                        ins = None
                        for qt in range(32):
                            for c2 in range(2):
                                ins = t.matmul(ps[:, cs(c2, 512)], cmbT[:, cs(qt, 128)],
                                               eosb[:, 1024 * qt + 512 * c2:1024 * qt + 512 * (c2 + 1)],
                                               start=(qt == 0), stop=(qt == 31))
                        ctx.inc(ins, "ffm")
                        ctx.w(v, "ffm", ctx.cnt["ffm"])
                        nfw = ctx.cnt.get("ffw", 0)
                        if nfw >= 32:
                            ctx.w(v, "ffw", nfw - 16)
                        ctx.inc(v.tensor_copy(ffst[:, cs(st % 2, 1024)], ps[:, 0:1024]), "ffe")
                        ctx.w(g, "ffe", ctx.cnt["ffe"])
                        ctx.inc(g.dma_start(out=ap3(ffout, 1, 1024, off_elems=1024 * 128 * st),
                                            in_=ffst[:, cs(st % 2, 1024)].unsqueeze(1)), "ffw", 16)
                    g.wait_ge(ctx.S("ffw"), ctx.cnt["ffw"])
                    ctx.inc(g.nop(), "ffc")
                    if "ffo" in dumps:
                        ctx.inc(g.dma_start(out=dumps["ffo"], in_=ffout), "dmp", 16)
                    emit_full_barrier(ctx)

                    exCmb.close()
            exMoe.close()
        with nc.sbuf_tensor("fr", [128, 3 * 1024], F32) as fr, \
             nc.sbuf_tensor("fo", [128, 2 * 1024], F32) as fo:
            ctx.w(g, "ffc")
            for tc in range(NT):
                if tc >= 1:
                    ctx.w(g, "fadd", ctx.cnt.get("fadd", 0))
                if tc >= 2:
                    ctx.w(g, "fst", ctx.cnt.get("fst", 0) - 16)
                ctx.inc(g.dma_start(out=sb3(fr[:, 0:1024], 1, 1024),
                                    in_=ap3(x1_d, 1, 1024, off_elems=1024 * 128 * tc)), "fld", 16)
                ctx.inc(g.dma_start(out=sb3(fr[:, 1024:2048], 1, 1024),
                                    in_=ap3(y1d, 1, 1024, off_elems=1024 * 128 * tc)), "fld", 16)
                ctx.inc(g.dma_start(out=sb3(fr[:, 2048:3072], 1, 1024),
                                    in_=ap3(ffout, 1, 1024, off_elems=1024 * 128 * tc)), "fld", 16)
                v.wait_ge(ctx.S("fld"), ctx.cnt["fld"])
                sl = tc % 2
                ctx.inc(v.tensor_tensor(out=fo[:, cs(sl, 1024)], in0=fr[:, 0:1024],
                                        in1=fr[:, 1024:2048], op=OP.add), "fadd")
                ctx.w(v, "fadd")
                ctx.inc(v.tensor_tensor(out=fo[:, cs(sl, 1024)], in0=fo[:, cs(sl, 1024)],
                                        in1=fr[:, 2048:3072], op=OP.add), "fadd")
                ctx.w(g, "fadd", ctx.cnt["fadd"])
                ctx.inc(g.dma_start(out=ap3(out_d, 1, 1024, off_elems=1024 * 128 * tc),
                                    in_=fo[:, cs(sl, 1024)].unsqueeze(1)), "fst", 16)
            g.wait_ge(ctx.S("fst"), ctx.cnt["fst"])
            if dumps:
                g.wait_ge(ctx.S("dmp"), ctx.cnt.get("dmp", 0))
    return nc

def build_fused():
    nc = bass.Bass("TRN2", target_bir_lowering=False, debug=False)
    with contextlib.ExitStack() as ex0:
        ctx = Ctx(nc, ex0)
        F = dict(nc=nc, ctx=ctx)
        build_l0(F=F)
        build_l1(F=F)
    return nc


def make_in_maps_fused(inputs):
    m0 = make_in_maps_l0(inputs)
    m1 = make_in_maps_l1(inputs, None)
    return [dict(m0[c], **m1[c]) for c in range(NCORE)]


def kernel(**inputs):
    maps = make_in_maps_fused(inputs)
    if "f" not in _BASS_CACHE:
        _BASS_CACHE["f"] = build_fused()
    res = run_bass_kernel_spmd(_BASS_CACHE["f"], maps, core_ids=list(range(NCORE)))
    return assemble_contig(res.results, "out")


def kernel_2launch(**inputs):
    maps0 = make_in_maps_l0(inputs)
    if "l0" not in _BASS_CACHE:
        _BASS_CACHE["l0"] = build_l0()
    res0 = run_bass_kernel_spmd(_BASS_CACHE["l0"], maps0, core_ids=list(range(NCORE)))
    x1_list = [np.asarray(r["x1"]) for r in res0.results]
    maps1 = make_in_maps_l1(inputs, x1_list)
    if "l1" not in _BASS_CACHE:
        _BASS_CACHE["l1"] = build_l1()
    res1 = run_bass_kernel_spmd(_BASS_CACHE["l1"], maps1, core_ids=list(range(NCORE)))
    return assemble_contig(res1.results, "out")


# version 03
# speedup vs baseline: 1.0203x; 1.0203x over previous
import contextlib
import numpy as np
import ml_dtypes
import concourse.bass as bass
import concourse.mybir as mybir
from concourse.bass_utils import run_bass_kernel_spmd

F32 = mybir.dt.float32
BF16 = mybir.dt.bfloat16
U32 = mybir.dt.uint32
AF = mybir.ActivationFunctionType
OP = mybir.AluOpType
AX = mybir.AxisListType

NCORE = 8
B, S, D = 2, 8192, 1024
NH, DH, NHP = 16, 64, 8
TPC, NT, GT = 2048, 16, 64
FF0 = 2816
E, CAP, HMOE = 8, 512, 5464
EPS = 1e-5
SQD = 1.0 / 8.0
SC1 = 1024 ** -0.5

_BASS_CACHE = {}


def _pos_tiles(p):
    return [4 * t + p for t in range(NT)]


def make_in_maps_l0(inputs):
    x = np.asarray(inputs["L0_attn_x"], np.float32)
    wq = np.ascontiguousarray(np.asarray(inputs["L0_attn_W_q"], np.float32))
    wk = np.ascontiguousarray(np.asarray(inputs["L0_attn_W_k"], np.float32))
    wv = np.ascontiguousarray(np.asarray(inputs["L0_attn_W_v"], np.float32))
    wo = np.ascontiguousarray(np.asarray(inputs["L0_attn_W_o"], np.float32))
    wg = np.asarray(inputs["L0_ffn_w_gate"], np.float32)
    wu = np.asarray(inputs["L0_ffn_w_up"], np.float32)
    wd = np.asarray(inputs["L0_ffn_w_down"], np.float32)
    ident = np.eye(128, dtype=np.float32)
    wgp = np.zeros((3072, 1024), np.float32); wgp[:2816] = wg
    wup = np.zeros((3072, 1024), np.float32); wup[:2816] = wu
    maps = []
    for c in range(NCORE):
        b, p = c // 4, c % 4
        xt = x[b].reshape(GT, 128, D)[_pos_tiles(p)].reshape(TPC, D).copy()
        bm = np.zeros((4, 128, 128), np.float32)
        for r in range(4):
            if r < p:
                bm[r] = 1.0
            elif r == p:
                bm[r] = (np.arange(128)[:, None] <= np.arange(128)[None, :]).astype(np.float32)
        maps.append(dict(
            x0=xt, wq=wq, wk=wk, wv=wv, wo=wo,
            wgs=np.ascontiguousarray(wgp[384 * c:384 * (c + 1)]),
            wus=np.ascontiguousarray(wup[384 * c:384 * (c + 1)]),
            wds=np.ascontiguousarray(wd[128 * c:128 * (c + 1)]),
            ident=ident, bmask=bm))
    return maps


def assemble_output(res, key):
    out = np.zeros((B, S, D), np.float32)
    for c in range(NCORE):
        b, p = c // 4, c % 4
        y = np.asarray(res[c][key]).reshape(NT, 128, D)
        out[b].reshape(GT, 128, D)[_pos_tiles(p)] = y
    return out


class Ctx:
    def __init__(self, nc, ex):
        self.nc = nc
        self.ex = ex
        self.sem_ex = ex
        self.sem = {}
        self._cnt = {}
        self.alias = {}

    def canon(self, name):
        return self.alias.get(name, name)

    @property
    def cnt(self):
        return _CntView(self)

    def S(self, name):
        name = self.canon(name)
        if name not in self.sem:
            self.sem[name] = self.sem_ex.enter_context(self.nc.semaphore(name))
            self._cnt[name] = 0
        return self.sem[name]

    def inc(self, instr, name, n=1):
        cn = self.canon(name)
        instr.then_inc(self.S(name), n)
        self._cnt[cn] += n
        return self._cnt[cn]

    def w(self, eng, name, val=None):
        cn = self.canon(name)
        self.S(name)
        eng.wait_ge(self.sem[cn], self._cnt[cn] if val is None else val)

    def sb(self, name, shape, dt):
        return self.ex.enter_context(self.nc.sbuf_tensor(name, shape, dt))

    def ps(self, name, shape, dt=F32):
        return self.ex.enter_context(self.nc.psum_tensor(name, shape, dt))


class _CntView:
    def __init__(self, ctx):
        self._ctx = ctx

    def __getitem__(self, k):
        return self._ctx._cnt[self._ctx.canon(k)]

    def __contains__(self, k):
        return self._ctx.canon(k) in self._ctx._cnt

    def get(self, k, d=None):
        return self._ctx._cnt.get(self._ctx.canon(k), d)

def cs(t, w):
    return slice(t * w, (t + 1) * w)


def emit_ln_tile(ctx, v, s, src_ap, xnf_ap, xnb_ap, st, t):
    v.tensor_scalar(out=st["scr"][:, 0:1024], in0=src_ap, scalar1=1.0, scalar2=None,
                    op0=OP.mult, op1=OP.add, accum_out=st["sumx"][:, t:t + 1])
    s.activation(out=st["scr2"][:, 0:1024], in_=src_ap, func=AF.Square,
                 accum_out=st["sumsq"][:, t:t + 1])


def emit_ln_stats(ctx, v, s, st, nt=NT):
    iv = 1.0 / 1024.0
    v.tensor_scalar(out=st["m"][:, 0:nt], in0=st["sumx"][:, 0:nt], scalar1=iv, scalar2=None, op0=OP.mult)
    v.tensor_scalar(out=st["msq"][:, 0:nt], in0=st["sumsq"][:, 0:nt], scalar1=iv, scalar2=None, op0=OP.mult)
    v.tensor_tensor(out=st["var"][:, 0:nt], in0=st["m"][:, 0:nt], in1=st["m"][:, 0:nt], op=OP.mult)
    v.tensor_tensor(out=st["var"][:, 0:nt], in0=st["msq"][:, 0:nt], in1=st["var"][:, 0:nt], op=OP.subtract)
    s.activation(out=st["lnv"][:, 0:nt], in_=st["var"][:, 0:nt], func=AF.Ln, bias=EPS)
    s.activation(out=st["rstd"][:, 0:nt], in_=st["lnv"][:, 0:nt], func=AF.Exp, scale=-0.5)
    v.scalar_tensor_tensor(out=st["mrn"][:, 0:nt], in0=st["m"][:, 0:nt], scalar=-1.0,
                           in1=st["rstd"][:, 0:nt], op0=OP.mult, op1=OP.mult)


def emit_fp4_tile(ctx, v, s, src_ap, out_ap, q, width, parts=128):
    nb = width // 16
    P = slice(0, parts)
    src3 = src_ap.rearrange("p (b s) -> p b s", s=16)

    def hs(instr):
        ctx.inc(instr, "fqh")
        ctx.w(v, "fqh")

    hs(v.tensor_reduce(out=q["blk"][P, 0:nb], in_=src3, axis=AX.X, op=OP.max,
                       apply_absolute_value=True))
    hs(v.tensor_scalar(out=q["blk"][P, 0:nb], in0=q["blk"][P, 0:nb], scalar1=1.0 / 6.0,
                       scalar2=1e-12, op0=OP.mult, op1=OP.max))
    hs(v.reciprocal(out=q["rs"][P, 0:nb], in_=q["blk"][P, 0:nb]))
    rse = q["rs"][P, 0:nb].unsqueeze(-1).broadcast_to([parts, nb, 16])
    ctx.inc(v.tensor_tensor(out=q["t"][P, 0:width], in0=src3, in1=rse, op=OP.mult), "fq_v")
    ctx.w(s, "fq_v")
    s.activation(out=q["u"][P, 0:width], in_=q["t"][P, 0:width], func=AF.Abs)
    ctx.inc(s.activation(out=q["sg"][P, 0:width], in_=q["t"][P, 0:width], func=AF.Sign), "fq_s")
    ctx.w(v, "fq_s")
    u = q["u"][P, 0:width]
    hs(v.tensor_scalar(out=q["a"][P, 0:width], in0=u, scalar1=0.25, scalar2=None, op0=OP.is_gt))
    for thr in (0.75, 1.25, 1.75):
        hs(v.scalar_tensor_tensor(out=q["a"][P, 0:width], in0=u, scalar=thr,
                                  in1=q["a"][P, 0:width], op0=OP.is_gt, op1=OP.add))
    hs(v.tensor_scalar(out=q["acc"][P, 0:width], in0=u, scalar1=2.5, scalar2=None, op0=OP.is_gt))
    hs(v.scalar_tensor_tensor(out=q["acc"][P, 0:width], in0=u, scalar=3.5,
                              in1=q["acc"][P, 0:width], op0=OP.is_gt, op1=OP.add))
    hs(v.scalar_tensor_tensor(out=q["acc"][P, 0:width], in0=q["a"][P, 0:width], scalar=0.5,
                              in1=q["acc"][P, 0:width], op0=OP.mult, op1=OP.add))
    for _ in range(2):
        hs(v.scalar_tensor_tensor(out=q["acc"][P, 0:width], in0=u, scalar=5.0,
                                  in1=q["acc"][P, 0:width], op0=OP.is_gt, op1=OP.add))
    hs(v.tensor_tensor(out=q["acc"][P, 0:width], in0=q["acc"][P, 0:width],
                       in1=q["sg"][P, 0:width], op=OP.mult))
    sce = q["blk"][P, 0:nb].unsqueeze(-1).broadcast_to([parts, nb, 16])
    return v.tensor_tensor(out=out_ap, in0=q["acc"][P, 0:width].rearrange("p (b s) -> p b s", s=16),
                           in1=sce, op=OP.mult)


def emit_t128(ctx, src_ap, dst_ap, pst, id_bf):
    nc = ctx.nc
    t, v = nc.tensor, nc.vector
    if not isinstance(id_bf, bass.AP):
        id_bf = id_bf[:]
    ctx.S("tr_t"); ctx.S("tr_v")
    j = ctx.cnt["tr_t"]
    slot = (j % 2) * 1024
    if j >= 2:
        ctx.w(t, "tr_v", j - 1)
    ctx.inc(t.transpose(pst[:, slot:slot + 128], src_ap, id_bf), "tr_t")
    ctx.w(v, "tr_t", j + 1)
    ctx.inc(v.tensor_copy(dst_ap, pst[:, slot:slot + 128]), "tr_v")


def emit_bulk_transpose(ctx, src, dst, ntile, pst, id_bf, src_stride=1024, dst_stride=2048,
                        dst_off=0, src_off=0, nptile=8):
    for tt in range(ntile):
        for j in range(nptile):
            emit_t128(ctx, src[:, src_off + tt * src_stride + 128 * j: src_off + tt * src_stride + 128 * j + 128],
                      dst[:, dst_off + j * dst_stride + 128 * tt: dst_off + j * dst_stride + 128 * tt + 128],
                      pst, id_bf)




def ap3(dram_ap, ngrp, width, off_elems=0):
    return bass.AP(dram_ap.tensor, dram_ap.offset + off_elems,
                   [[width, 128], [128 * width, ngrp], [1, width]])


def sb3(sbuf_ap, ngrp, width):
    return sbuf_ap.rearrange("p (g w) -> p g w", w=width)

def emit_full_barrier(ctx):
    nc = ctx.nc
    bs = ctx.barscr
    s = ctx.S("bar")
    ctx.inc(nc.gpsimd.tensor_copy(bs[:, 0:1], bs[:, 1:2]), "bar")
    ctx.inc(nc.sync.nop(), "bar")
    ctx.inc(nc.tensor.nop(), "bar")
    ctx.inc(nc.vector.tensor_copy(bs[:, 2:3], bs[:, 3:4]), "bar")
    ctx.inc(nc.scalar.activation(out=bs[:, 4:5], in_=bs[:, 5:6], func=AF.Copy), "bar")
    tgt = ctx.cnt["bar"]
    for e in [nc.gpsimd, nc.sync, nc.tensor, nc.vector, nc.scalar]:
        e.wait_ge(s, tgt)


def emit_proj(ctx, w_sb, xnT, psA, psB, evac, sem, nhp=8):
    nc = ctx.nc
    t = nc.tensor
    tb = ctx.cnt.get(sem + "_t", 0)
    for hp in range(nhp):
        ps = psA if hp % 2 == 0 else psB
        ev = ctx.cnt.get(sem + "_v", 0)
        if ev >= 2:
            ctx.w(t, sem + "_v", ev - 1)
        ins = None
        for mp in range(8):
            for ch in range(4):
                ins = t.matmul(ps[:, cs(ch, 512)],
                               w_sb[:, 1024 * mp + 128 * hp:1024 * mp + 128 * hp + 128],
                               xnT[:, 2048 * mp + 512 * ch:2048 * mp + 512 * (ch + 1)],
                               start=(mp == 0), stop=(mp == 7))
        ctx.inc(ins, sem + "_t")
        evac(hp, ps)


def l0_granules(qh):
    out = []
    for kt in range(32 * qh + 32):
        tmin = kt // 4
        qfrom = max(0, 128 * (tmin - 8 * qh))
        off, first = qfrom, True
        while off < 1024:
            wdt = min(512, 1024 - off)
            out.append((kt, off, wdt, first and (8 * qh <= tmin)))
            off += wdt
            first = False
    return out


def build_l0(dump=(), phases=9, p1sub=9, p4sub=127, F=None):
    nc = F["nc"] if F is not None else bass.Bass("TRN2", target_bir_lowering=False, debug=False)
    g, y, t, v, s = nc.gpsimd, nc.sync, nc.tensor, nc.vector, nc.scalar
    x0 = nc.dram_tensor("x0", [TPC, D], F32, kind="ExternalInput").ap()
    wq_d = nc.dram_tensor("wq", [D, D], F32, kind="ExternalInput").ap()
    wk_d = nc.dram_tensor("wk", [D, D], F32, kind="ExternalInput").ap()
    wv_d = nc.dram_tensor("wv", [D, D], F32, kind="ExternalInput").ap()
    wo_d = nc.dram_tensor("wo", [D, D], F32, kind="ExternalInput").ap()
    wgs_d = nc.dram_tensor("wgs", [384, 1024], F32, kind="ExternalInput").ap()
    wus_d = nc.dram_tensor("wus", [384, 1024], F32, kind="ExternalInput").ap()
    wds_d = nc.dram_tensor("wds", [128, 2816], F32, kind="ExternalInput").ap()
    ident_d = nc.dram_tensor("ident", [128, 128], F32, kind="ExternalInput").ap()
    bmask_d = nc.dram_tensor("bmask", [4, 128, 128], F32, kind="ExternalInput").ap()
    x1_d = (nc.dram_tensor("x1", [TPC, D], F32).ap() if F is not None
            else nc.dram_tensor("x1", [TPC, D], F32, kind="ExternalOutput").ap())
    dumps = {}
    _dsh = {"xnf": ([TPC, D], F32), "xqt": ([128, 8 * TPC], F32), "qt": ([128, NHP * TPC], F32),
            "kv": ([2048, 2048], F32), "aot": ([128, 8 * TPC], F32), "wqa": ([8960, 1024], F32),
            "ffg": ([TPC, 1408], F32), "ffh": ([TPC, 1408], F32), "fft": ([TPC, 1408], F32),
            "ffw": ([128, 11264], F32)}
    for name in dump:
        dumps[name] = nc.dram_tensor("d_" + name, _dsh[name][0], _dsh[name][1],
                                     kind="ExternalOutput").ap()

    kvin = nc.dram_tensor("kvin", [2048, 2048], BF16).ap()
    kvag = nc.dram_tensor("kvag", [NCORE * 2048, 2048], BF16, addr_space="Shared").ap()
    kvloc = nc.dram_tensor("kvloc", [4 * 2048, 2048], BF16).ap()
    wqin = nc.dram_tensor("wqin", [1120, 1024], BF16).ap()
    wqag = nc.dram_tensor("wqag", [NCORE * 1120, 1024], BF16, addr_space="Shared").ap()

    with contextlib.ExitStack() as ex:
        if F is not None:
            ctx = F["ctx"]
            ctx.ex = ex
        else:
            ctx = Ctx(nc, ex)
        R = ctx.sb("R", [128, NT * 1024], F32)
        xqT = ctx.sb("xqT", [128, 8 * TPC], BF16)
        id_bf = ctx.sb("id_bf", [128, 128], BF16)
        bm_bf = ctx.sb("bm_bf", [128, 4 * 128], BF16)
        ones_f = ctx.sb("ones_f", [128, 64], F32)
        eps_c = ctx.sb("eps_c", [128, 1], F32)
        ctx.barscr = ctx.sb("barscr", [128, 8], F32)
        exQ = ctx.ex.enter_context(contextlib.ExitStack())
        QT = exQ.enter_context(nc.sbuf_tensor("QT", [128, NHP * TPC], BF16))
        ex.enter_context(nc.Block())

        ctx.inc(g.dma_start(out=sb3(R[:], NT, 1024), in_=ap3(x0, NT, 1024)), "ld", 16)
        ctx.inc(g.dma_start(out=id_bf[:], in_=ident_d), "ld", 16)
        ctx.inc(g.dma_start(out=sb3(bm_bf[:], 4, 128), in_=bass.AP(bmask_d.tensor, 0, [[128, 128], [128 * 128, 4], [1, 128]])), "ld", 16)
        g.memset(ones_f[:], 1.0)
        g.memset(eps_c[:], EPS)
        ctx.inc(g.nop(), "ld")

        if phases >= 1:
            exA = ctx.ex.enter_context(contextlib.ExitStack())
            xnT = exA.enter_context(nc.sbuf_tensor("xnT", [128, 8 * TPC], BF16))
            with nc.sbuf_tensor("xnf", [128, 2 * 1024], F32) as xnf, \
                 nc.sbuf_tensor("xnb", [128, 2 * 1024], BF16) as xnb, \
                 nc.sbuf_tensor("lnt", [128, 8 * NT], F32) as lnt, \
                 nc.sbuf_tensor("fq1", [128, 5 * 1024], F32) as fq1, \
                 nc.sbuf_tensor("fq2", [128, 2 * 64], F32) as fq2, \
                 nc.sbuf_tensor("wstg", [128, 2 * 1024], F32) as wstg, \
                 nc.sbuf_tensor("wstq", [128, 2 * 1024], BF16) as wstq, \
                 nc.psum_tensor("pst", [128, 2048], BF16) as pst:
                qd = dict(blk=fq2[:, 0:64], rs=fq2[:, 64:128],
                          t=fq1[:, cs(0, 1024)], u=fq1[:, cs(1, 1024)], sg=fq1[:, cs(2, 1024)],
                          a=fq1[:, cs(3, 1024)], acc=fq1[:, cs(4, 1024)])
                st = dict(sumx=lnt[:, 0:NT], sumsq=lnt[:, NT:2 * NT], m=lnt[:, 2 * NT:3 * NT],
                          msq=lnt[:, 3 * NT:4 * NT], var=lnt[:, 4 * NT:5 * NT],
                          lnv=lnt[:, 5 * NT:6 * NT], rstd=lnt[:, 6 * NT:7 * NT],
                          mrn=lnt[:, 7 * NT:8 * NT])
                v.wait_ge(ctx.S("ld"), ctx.cnt["ld"])
                s.wait_ge(ctx.S("ld"), ctx.cnt["ld"])
                for tt in range(NT):
                    ctx.inc(v.reduce_sum(out=st["sumx"][:, tt:tt + 1], in_=R[:, cs(tt, 1024)],
                                         axis=AX.X), "lsv")
                    ctx.inc(s.activation(out=fq1[:, 1024:2048], in_=R[:, cs(tt, 1024)],
                                         func=AF.Square), "lsq")
                    ctx.w(v, "lsq")
                    ctx.inc(v.reduce_sum(out=st["sumsq"][:, tt:tt + 1], in_=fq1[:, 1024:2048],
                                         axis=AX.X), "lsv")
                    ctx.w(s, "lsv")
                ctx.w(v, "lsv")
                iv = 1.0 / 1024.0
                ctx.inc(v.tensor_scalar(out=st["m"], in0=st["sumx"], scalar1=iv, scalar2=None, op0=OP.mult), "lns")
                ctx.w(v, "lns")
                ctx.inc(v.tensor_scalar(out=st["msq"], in0=st["sumsq"], scalar1=iv, scalar2=None, op0=OP.mult), "lns")
                ctx.w(v, "lns")
                ctx.inc(v.tensor_tensor(out=st["var"], in0=st["m"], in1=st["m"], op=OP.mult), "lns")
                ctx.w(v, "lns")
                ctx.inc(v.tensor_tensor(out=st["var"], in0=st["msq"], in1=st["var"], op=OP.subtract), "ln1")
                ctx.w(s, "ln1")
                ctx.inc(s.activation(out=st["lnv"], in_=st["var"], func=AF.Ln, bias=eps_c[:, 0:1]), "lnx")
                ctx.w(s, "lnx")
                ctx.inc(s.activation(out=st["rstd"], in_=st["lnv"], func=AF.Exp, scale=-0.5), "ln2")
                ctx.w(v, "ln2")
                ctx.inc(v.scalar_tensor_tensor(out=st["mrn"], in0=st["m"], scalar=-1.0,
                                               in1=st["rstd"], op0=OP.mult, op1=OP.mult), "ln3")
                ctx.w(v, "ln3")
                for tt in range(NT if p1sub >= 2 else 0):
                    sl = tt % 2
                    rst = st["rstd"][:, tt:tt + 1]
                    mrn = st["mrn"][:, tt:tt + 1]
                    if tt >= 2:
                        ctx.w(v, "fq_s", ctx.cnt.get("fq_s", 0))
                        if "xnf" in dumps:
                            ctx.w(v, "dxn", ctx.cnt.get("dxn", 0))
                    v.tensor_scalar(out=xnf[:, cs(sl, 1024)], in0=R[:, cs(tt, 1024)],
                                    scalar1=rst, scalar2=mrn, op0=OP.mult, op1=OP.add)
                    if tt >= 2:
                        ctx.w(v, "tr_v", ctx.cnt.get("tr_v", 0) - 8)
                    ctx.inc(v.tensor_scalar(out=xnb[:, cs(sl, 1024)], in0=R[:, cs(tt, 1024)],
                                            scalar1=rst, scalar2=mrn, op0=OP.mult, op1=OP.add), "xb")
                    ctx.w(t, "xb", ctx.cnt["xb"])
                    for j in range(8):
                        emit_t128(ctx, xnb[:, cs(sl, 1024)][:, cs(j, 128)],
                                  xnT[:, 2048 * j + 128 * tt:2048 * j + 128 * tt + 128], pst, id_bf)
                    if "xnf" in dumps:
                        ctx.w(y, "xb", ctx.cnt["xb"])
                        ctx.inc(y.dma_start(out=ap3(dumps["xnf"], 1, 1024, off_elems=1024 * 128 * tt),
                                            in_=xnf[:, cs(sl, 1024)].unsqueeze(1)), "dxn", 16)
                    if p1sub >= 3:
                        qi = emit_fp4_tile(ctx, v, s, xnf[:, cs(sl, 1024)], wstq[:, cs(sl, 1024)], qd, 1024)
                        ctx.inc(qi, "xqb")
                        ctx.w(t, "xqb", ctx.cnt["xqb"])
                        for j in range(8):
                            emit_t128(ctx, wstq[:, cs(sl, 1024)][:, cs(j, 128)],
                                      xqT[:, 2048 * j + 128 * tt:2048 * j + 128 * tt + 128], pst, id_bf)
                if "xqt" in dumps:
                    ctx.w(g, "tr_v")
                    ctx.inc(g.dma_start(out=dumps["xqt"], in_=xqT[:]), "dmp", 16)
                plist = []
                for j in range(3):
                    plist.append((wgs_d[128 * j:128 * (j + 1), :], 1024, 128 * j))
                for j in range(3):
                    plist.append((wus_d[128 * j:128 * (j + 1), :], 1024, 384 + 128 * j))
                for hh in range(4):
                    src = bass.AP(wds_d.tensor, 704 * hh, [[2816, 128], [1, 704]])
                    plist.append((src, 704, 1000 + hh))
                ctx.w(v, "tr_v")
                if p1sub < 4:
                    plist = []
                for pi, (srcp, wdt, rowoff) in enumerate(plist):
                    sl = pi % 2
                    so = sl * 1024
                    ctx.inc(g.dma_start(out=wstg[:, so:so + wdt], in_=srcp), "wl", 16)
                    if pi >= 2:
                        ctx.w(v, "wst", ctx.cnt.get("wst", 0) - 16)
                    v.wait_ge(ctx.S("wl"), ctx.cnt["wl"])
                    s.wait_ge(ctx.S("wl"), ctx.cnt["wl"])
                    qi = emit_fp4_tile(ctx, v, s, wstg[:, so:so + wdt], wstq[:, so:so + wdt], qd, wdt)
                    ctx.inc(qi, "wqp")
                    ctx.w(g, "wqp", ctx.cnt["wqp"])
                    if wdt == 1024:
                        dst = wqin[rowoff:rowoff + 128, :]
                    else:
                        hh = rowoff - 1000
                        dst = bass.AP(wqin.tensor, 768 * 1024 + 704 * hh, [[2816, 128], [1, 704]])
                    ctx.inc(g.dma_start(out=dst, in_=wstq[:, so:so + wdt]), "wst", 16)
                g.wait_ge(ctx.S("wst"), ctx.cnt["wst"])
                if p1sub >= 5:
                    ctx.inc(g.collective_compute("AllGather", OP.bypass, replica_groups=[list(range(NCORE))],
                                                 ins=[wqin], outs=[wqag]), "wqc")
                if "wqa" in dumps:
                    ctx.w(g, "wqc")
                    ctx.inc(g.dma_start(out=dumps["wqa"], in_=wqag[:]), "dmp", 16)
                emit_full_barrier(ctx)

        if phases >= 2:
            with nc.sbuf_tensor("wA", [128, 8 * 1024], BF16) as wA, \
                 nc.sbuf_tensor("kst", [128, 2048], BF16) as kst, \
                 nc.psum_tensor("psqA", [128, 2048], F32) as psqA, \
                 nc.psum_tensor("psqB", [128, 2048], F32) as psqB:
                ctx.inc(g.dma_start(out=sb3(wA[:], 8, 1024), in_=ap3(wq_d, 8, 1024)), "wab", 16)
                t.wait_ge(ctx.S("wab"), ctx.cnt["wab"])
                ctx.w(t, "ln3")
                ctx.w(t, "tr_v", 256)

                def evac_qt(hp, ps):
                    ctx.w(v, "pj_t", ctx.cnt.get("pj_t", 0))
                    ctx.inc(v.tensor_copy(QT[:, cs(hp, 2048)], ps[:, 0:2048]), "pj_v")

                emit_proj(ctx, wA, xnT, psqA, psqB, evac_qt, "pj")

                def evac_kv(hp, ps, base_row):
                    ctx.w(v, "pj_t", ctx.cnt.get("pj_t", 0))
                    nst = ctx.cnt.get("kvst", 0)
                    if nst >= 16:
                        ctx.w(v, "kvst", nst)
                    ctx.inc(v.tensor_copy(kst[:], ps[:, 0:2048]), "pj_v")
                    ctx.w(g, "pj_v", ctx.cnt["pj_v"])
                    ctx.inc(g.dma_start(out=kvin[base_row + 128 * hp:base_row + 128 * (hp + 1), :],
                                        in_=kst[:]), "kvst", 16)

                ctx.w(g, "pj_t", 8)
                ctx.inc(g.dma_start(out=sb3(wA[:], 8, 1024), in_=ap3(wk_d, 8, 1024)), "wab", 16)
                t.wait_ge(ctx.S("wab"), ctx.cnt["wab"])
                emit_proj(ctx, wA, xnT, psqA, psqB, lambda hp, ps: evac_kv(hp, ps, 0), "pj")
                ctx.w(g, "pj_t", 16)
                ctx.inc(g.dma_start(out=sb3(wA[:], 8, 1024), in_=ap3(wv_d, 8, 1024)), "wab", 16)
                t.wait_ge(ctx.S("wab"), ctx.cnt["wab"])
                emit_proj(ctx, wA, xnT, psqA, psqB, lambda hp, ps: evac_kv(hp, ps, 1024), "pj")
                g.wait_ge(ctx.S("kvst"), ctx.cnt["kvst"])
                ctx.inc(g.collective_compute("AllGather", OP.bypass, replica_groups=[list(range(NCORE))],
                                             ins=[kvin], outs=[kvag]), "kvc")
                ctx.w(g, "kvc")
                with g.register("roff") as roff:
                    pid = g.partition_id()
                    g.reg_mov(roff, pid)
                    g.reg_div(roff, roff, 4)
                    g.reg_mul(roff, roff, 4 * 2048 * 2048)
                    src = bass.AP(kvag.tensor, roff, [[2048, 8192], [1, 2048]])
                    ctx.inc(g.dma_start(out=kvloc, in_=src), "kvl", 16)
                g.wait_ge(ctx.S("kvl"), ctx.cnt["kvl"])
                if "kv" in dumps:
                    ctx.inc(g.dma_start(out=dumps["kv"], in_=kvloc[0:2048, :]), "dmp", 16)
                if "qt" in dumps:
                    ctx.inc(g.dma_start(out=dumps["qt"], in_=QT[:]), "dmp", 16)
                emit_full_barrier(ctx)
            exA.close()

        if phases >= 3:
            with nc.sbuf_tensor("AOT", [128, 8 * TPC], BF16) as AOT, \
                 nc.sbuf_tensor("KTl", [128, 8192], BF16) as KTl, \
                 nc.sbuf_tensor("Vl", [128, 64 * 130], BF16) as Vl, \
                 nc.sbuf_tensor("vst", [128, 256], BF16) as vst, \
                 nc.sbuf_tensor("PT", [128, 2048], BF16) as PT, \
                 nc.sbuf_tensor("rzb", [65, 2048], F32) as rzb, \
                 nc.psum_tensor("psS", [128, 2048], F32) as psS, \
                 nc.psum_tensor("psO1", [128, 1024], F32) as psO1, \
                 nc.psum_tensor("psO2", [128, 1024], F32) as psO2:
                ov = Vl[:].rearrange("p (k s) -> p k s", s=65)[:, :, 64:65]
                ctx.inc(v.memset(ov, 1.0), "vone")
                for hp in range(NHP):
                    if hp == 0:
                        ctx.w(g, "kvl")
                    else:
                        ctx.w(g, "at", ctx.cnt.get("at", 0))
                    for r in range(4):
                        ctx.inc(g.dma_start(out=KTl[:, cs(r, 2048)],
                                            in_=kvloc[2048 * r + 128 * hp:2048 * r + 128 * (hp + 1), :]),
                                "ktl", 16)
                    vt_b = ctx.cnt.get("vt_v", 0)
                    for kt in range(64):
                        r, lt = kt % 4, kt // 4
                        if kt >= 2:
                            ctx.w(y, "vt_v", vt_b + kt - 1)
                        if hp == 0 and kt == 0:
                            ctx.w(y, "kvl")
                            ctx.w(y, "vone")
                        if hp > 0 and kt == 0:
                            ctx.w(y, "hp_done", hp)
                        ctx.inc(y.dma_start_transpose(vst[:, cs(kt % 2, 128)],
                                                      kvloc[2048 * r + 1024 + 128 * hp:2048 * r + 1024 + 128 * (hp + 1),
                                                            cs(lt, 128)]), "vt_y", 16)
                        ctx.w(v, "vt_y")
                        v.tensor_copy(Vl[:, 130 * kt:130 * kt + 64], vst[:, cs(kt % 2, 128)][:, 0:64])
                        ctx.inc(v.tensor_copy(Vl[:, 130 * kt + 65:130 * kt + 129],
                                              vst[:, cs(kt % 2, 128)][:, 64:128]), "vt_v")
                    for qh in range(2):
                        grans = l0_granules(qh)
                        ng = len(grans)
                        exb = ctx.cnt.get("ex", 0)
                        atb = ctx.cnt.get("at", 0)
                        avb = ctx.cnt.get("av", 0)
                        for gi, (kt, off, wdt, msk) in enumerate(grans):
                            slot = gi % 2
                            kcol = 2048 * (kt % 4) + 128 * (kt // 4)
                            qcol = 2048 * hp + 1024 * qh + off
                            if gi >= 2:
                                ctx.w(t, "ex", exb + gi - 1)
                            if gi == 0:
                                t.wait_ge(ctx.S("ktl"), ctx.cnt["ktl"])
                                ctx.w(t, "ev", ctx.cnt.get("ev", 0))
                            t.matmul(psS[:, 1024 * slot:1024 * slot + wdt],
                                     KTl[0:64, kcol:kcol + 128], QT[0:64, qcol:qcol + wdt],
                                     start=True, stop=True, tile_position=(0, 0))
                            ctx.inc(t.matmul(psS[:, 1024 * slot + 512:1024 * slot + 512 + wdt],
                                             KTl[64:128, kcol:kcol + 128], QT[64:128, qcol:qcol + wdt],
                                             start=True, stop=True, tile_position=(64, 0)), "at")
                            ctx.w(s, "at", atb + gi + 1)
                            if gi >= 2:
                                ctx.w(s, "av", avb + gi - 1)
                            ctx.inc(s.activation(out=PT[:, 1024 * slot:1024 * slot + 512 + wdt],
                                                 in_=psS[:, 1024 * slot:1024 * slot + 512 + wdt],
                                                 func=AF.Exp, scale=SQD), "ex")
                            if msk:
                                ctx.w(v, "ex", ctx.cnt["ex"])
                                bms = bm_bf[:, cs(kt % 4, 128)]
                                v.tensor_tensor(out=PT[:, 1024 * slot:1024 * slot + 128],
                                                in0=PT[:, 1024 * slot:1024 * slot + 128], in1=bms, op=OP.mult)
                                ctx.inc(v.tensor_tensor(out=PT[:, 1024 * slot + 512:1024 * slot + 640],
                                                        in0=PT[:, 1024 * slot + 512:1024 * slot + 640],
                                                        in1=bms, op=OP.mult), "mk")
                            if msk:
                                ctx.w(t, "mk", ctx.cnt["mk"])
                            else:
                                ctx.w(t, "ex", ctx.cnt["ex"])
                            ctx.w(t, "vt_v", vt_b + kt + 1)
                            t.matmul(psO1[0:65, off:off + wdt], Vl[:, 130 * kt:130 * kt + 65],
                                     PT[:, 1024 * slot:1024 * slot + wdt],
                                     start=(kt == 0), stop=False)
                            ctx.inc(t.matmul(psO2[0:65, off:off + wdt], Vl[:, 130 * kt + 65:130 * kt + 130],
                                             PT[:, 1024 * slot + 512:1024 * slot + 512 + wdt],
                                             start=(kt == 0), stop=False), "av")
                        ctx.w(v, "av", ctx.cnt["av"])
                        v.reciprocal(out=rzb[64:65, 0:1024], in_=psO1[64:65, 0:1024])
                        ctx.inc(v.reciprocal(out=rzb[64:65, 1024:2048], in_=psO2[64:65, 0:1024]), "rzv")
                        ctx.w(t, "rzv", ctx.cnt["rzv"])
                        ctx.w(t, "ex", ctx.cnt["ex"])
                        ins = None
                        for h in range(2):
                            for c2 in range(2):
                                ins = t.matmul(psS[0:64, 1024 * h + 512 * c2:1024 * h + 512 * (c2 + 1)],
                                               ones_f[64:65, 0:64],
                                               rzb[64:65, 1024 * h + 512 * c2:1024 * h + 512 * (c2 + 1)],
                                               start=True, stop=True)
                        ctx.inc(ins, "rbt")
                        ctx.w(v, "rbt", ctx.cnt["rbt"])
                        ctx.inc(v.tensor_copy(rzb[0:64, 0:2048], psS[0:64, 0:2048]), "rzc")
                        ctx.w(v, "rzc")
                        acol = 2048 * hp + 1024 * qh
                        v.tensor_tensor(out=AOT[0:64, acol:acol + 1024], in0=psO1[0:64, 0:1024],
                                        in1=rzb[0:64, 0:1024], op=OP.mult)
                        ctx.inc(v.tensor_tensor(out=AOT[64:128, acol:acol + 1024], in0=psO2[0:64, 0:1024],
                                                in1=rzb[0:64, 1024:2048], op=OP.mult), "ev")
                    ctx.w(y, "ev", ctx.cnt.get("ev", 0))
                    ctx.inc(y.nop(), "hp_done")
                if "aot" in dumps:
                    ctx.w(g, "ev", ctx.cnt.get("ev", 0))
                    ctx.inc(g.dma_start(out=dumps["aot"], in_=AOT[:]), "dmp", 16)
                ctx.w(g, "ev", ctx.cnt.get("ev", 0))
                ctx.inc(g.dma_start(out=sb3(KTl[:], 8, 1024), in_=ap3(wo_d, 8, 1024)), "wol", 16)
                t.wait_ge(ctx.S("wol"), ctx.cnt["wol"])
                ctx.w(t, "ev", ctx.cnt.get("ev", 0))
                for tc in range(NT):
                    ps = psO1 if tc % 2 == 0 else psO2
                    nyv = ctx.cnt.get("yv", 0)
                    if nyv >= 2:
                        ctx.w(t, "yv", nyv - 1)
                    ins = None
                    for dp in range(8):
                        for c2 in range(2):
                            ins = t.matmul(ps[:, cs(c2, 512)],
                                           AOT[:, 2048 * dp + 128 * tc:2048 * dp + 128 * (tc + 1)],
                                           KTl[:, 1024 * dp + 512 * c2:1024 * dp + 512 * (c2 + 1)],
                                           start=(dp == 0), stop=(dp == 7))
                    ctx.inc(ins, "yt")
                    ctx.w(v, "yt", ctx.cnt["yt"])
                    ctx.inc(v.tensor_tensor(out=R[:, cs(tc, 1024)], in0=R[:, cs(tc, 1024)],
                                            in1=ps[:, 0:1024], op=OP.add), "yv")
                emit_full_barrier(ctx)
            exQ.close()

        if phases >= 4:
            FH = 1408
            with nc.sbuf_tensor("wgT", [128, 8 * FH], BF16) as wgT, \
                 nc.sbuf_tensor("wuT", [128, 8 * FH], BF16) as wuT, \
                 nc.sbuf_tensor("wdT", [128, 11 * 1024], BF16) as wdT, \
                 nc.sbuf_tensor("wtg", [128, 2 * 1408], BF16) as wtg, \
                 nc.sbuf_tensor("sgat", [128, FH], F32) as sgat, \
                 nc.sbuf_tensor("hbuf", [128, FH], F32) as hbuf, \
                 nc.sbuf_tensor("hq", [128, FH], BF16) as hq, \
                 nc.sbuf_tensor("hqT", [128, 11 * 128], BF16) as hqT, \
                 nc.sbuf_tensor("fq1b", [128, 3 * FH], F32) as fq1b, \
                 nc.sbuf_tensor("fq2b", [128, 2 * 88], F32) as fq2b:
                qd2 = dict(blk=fq2b[:, 0:88], rs=fq2b[:, 88:176],
                           t=sgat[:, 0:FH], u=fq1b[:, cs(0, FH)], sg=fq1b[:, cs(1, FH)],
                           a=hbuf[:, 0:FH], acc=fq1b[:, cs(2, FH)])
                g.wait_ge(ctx.S("wqc"), ctx.cnt["wqc"])
                for hf in range(2 if p4sub else 0):
                    with nc.psum_tensor("pstf%d" % hf, [128, 2048], BF16) as pstf:
                        for wi, (dstT, base) in enumerate(((wgT, 0), (wuT, 384))):
                            for jj in range(11):
                                jn = 11 * hf + jj
                                rrow = 1120 * (jn // 3) + base + 128 * (jn % 3)
                                slot = (jj + wi * 11) % 2
                                so = slot * 1408
                                if ctx.cnt.get("wtl", 0) >= 32:
                                    ctx.w(g, "wtt", ctx.cnt.get("wtt", 0) - 1)
                                ctx.inc(g.dma_start(out=wtg[:, so:so + 1024],
                                                    in_=wqag[rrow:rrow + 128, :]), "wtl", 16)
                                t.wait_ge(ctx.S("wtl"), ctx.cnt["wtl"])
                                for kp in range(8):
                                    ctx.S("tr_t"); ctx.S("tr_v")
                                    jt = ctx.cnt["tr_t"]
                                    sl2 = (jt % 2) * 1024
                                    if jt >= 2:
                                        ctx.w(t, "tr_v", jt - 1)
                                    ctx.inc(t.transpose(pstf[:, sl2:sl2 + 128],
                                                        wtg[:, so + 128 * kp:so + 128 * (kp + 1)], id_bf[:]), "tr_t")
                                    ctx.w(v, "tr_t", jt + 1)
                                    ctx.inc(v.tensor_copy(dstT[:, FH * kp + 128 * jj:FH * kp + 128 * (jj + 1)],
                                                          pstf[:, sl2:sl2 + 128]), "tr_v")
                                ctx.inc(t.nop(), "wtt")
                        for j in range(8):
                            so = (j % 2) * 1408
                            if ctx.cnt.get("wtl", 0) >= 32:
                                ctx.w(g, "wtt", ctx.cnt.get("wtt", 0) - 1)
                            src = bass.AP(wqag.tensor, (1120 * j + 768) * 1024 + FH * hf, [[2816, 128], [1, FH]])
                            ctx.inc(g.dma_start(out=wtg[:, so:so + FH], in_=src), "wtl", 16)
                            t.wait_ge(ctx.S("wtl"), ctx.cnt["wtl"])
                            for ip in range(11):
                                jt = ctx.cnt["tr_t"]
                                sl2 = (jt % 2) * 1024
                                if jt >= 2:
                                    ctx.w(t, "tr_v", jt - 1)
                                srcb = wtg[:, so + 128 * ip:so + 128 * (ip + 1)]
                                ctx.inc(t.transpose(pstf[:, sl2:sl2 + 128], srcb, id_bf[:]), "tr_t")
                                ctx.w(v, "tr_t", jt + 1)
                                ctx.inc(v.tensor_copy(wdT[:, 1024 * ip + 128 * j:1024 * ip + 128 * (j + 1)],
                                                      pstf[:, sl2:sl2 + 128]), "tr_v")
                            ctx.inc(t.nop(), "wtt")
                        ctx.w(t, "tr_v")
                    emit_full_barrier(ctx)
                    if "ffw" in dumps and hf == 0:
                        ctx.inc(g.dma_start(out=dumps["ffw"], in_=wdT[:]), "dfw", 16)
                        g.wait_ge(ctx.S("dfw"), ctx.cnt["dfw"])
                    with nc.psum_tensor("pgf%d" % hf, [128, 1536], F32) as pgf, \
                         nc.psum_tensor("puf%d" % hf, [128, 1536], F32) as puf, \
                         nc.psum_tensor("pdf%d" % hf, [128, 1024], F32) as pdf:
                        for tc in range(NT if (p4sub & 8) else 0):
                            ins = None
                            if tc > 0:
                                ctx.w(t, "sil", ctx.cnt.get("sil", 0))
                            for kp in range(8):
                                for c3, (o3, w3) in enumerate(((0, 512), (512, 512), (1024, 384))):
                                    ins = t.matmul(pgf[:, o3:o3 + w3],
                                                   xqT[:, 2048 * kp + 128 * tc:2048 * kp + 128 * (tc + 1)],
                                                   wgT[:, FH * kp + o3:FH * kp + o3 + w3],
                                                   start=(kp == 0), stop=(kp == 7))
                            ctx.inc(ins, "gt")
                            ctx.w(s, "gt", ctx.cnt["gt"])
                            if "ffg" in dumps and ctx.cnt.get("dfg", 0):
                                ctx.w(s, "dfg", ctx.cnt["dfg"])
                            ctx.inc(s.activation(out=sgat[:, 0:FH], in_=pgf[:, 0:FH], func=AF.Silu), "sil")
                            if "ffg" in dumps and hf == 0:
                                ctx.w(y, "sil", ctx.cnt["sil"])
                                ctx.inc(y.dma_start(out=dumps["ffg"][128 * tc:128 * (tc + 1), :],
                                                    in_=sgat[:, 0:FH]), "dfg", 16)
                            if not (p4sub & 16):
                                continue
                            ins = None
                            if tc > 0:
                                ctx.w(t, "hv", ctx.cnt.get("hv", 0))
                            for kp in range(8):
                                for c3, (o3, w3) in enumerate(((0, 512), (512, 512), (1024, 384))):
                                    ins = t.matmul(puf[:, o3:o3 + w3],
                                                   xqT[:, 2048 * kp + 128 * tc:2048 * kp + 128 * (tc + 1)],
                                                   wuT[:, FH * kp + o3:FH * kp + o3 + w3],
                                                   start=(kp == 0), stop=(kp == 7))
                            ctx.inc(ins, "ut")
                            ctx.w(v, "ut", ctx.cnt["ut"])
                            ctx.w(v, "sil", ctx.cnt["sil"])
                            ctx.inc(v.tensor_tensor(out=hbuf[:, 0:FH], in0=puf[:, 0:FH],
                                                    in1=sgat[:, 0:FH], op=OP.mult), "hv")
                            ctx.w(v, "hv")
                            if not (p4sub & 32):
                                continue
                            if ctx.cnt.get("hqt", 0):
                                ctx.w(v, "hqt", ctx.cnt["hqt"])
                            if "ffh" in dumps and ctx.cnt.get("dfh", 0):
                                ctx.w(v, "dfh", ctx.cnt["dfh"])
                            qi = emit_fp4_tile(ctx, v, s, hbuf[:, 0:FH], hq[:, 0:FH], qd2, FH)
                            ctx.inc(qi, "hqd")
                            if "ffh" in dumps and hf == 0:
                                ctx.w(g, "hqd", ctx.cnt["hqd"])
                                ctx.inc(g.dma_start(out=dumps["ffh"][128 * tc:128 * (tc + 1), :],
                                                    in_=hq[:, 0:FH]), "dfh", 16)
                            if not (p4sub & 64):
                                continue
                            ctx.w(y, "hqd", ctx.cnt["hqd"])
                            if tc > 0:
                                ctx.w(y, "dnt", ctx.cnt.get("dnt", 0))
                                if "fft" in dumps and ctx.cnt.get("dft", 0):
                                    ctx.w(y, "dft", ctx.cnt["dft"])
                            for ip in range(11):
                                ctx.inc(y.dma_start_transpose(hqT[:, cs(ip, 128)], hq[:, cs(ip, 128)]), "hqt", 16)
                            if "fft" in dumps and hf == 0:
                                g.wait_ge(ctx.S("hqt"), ctx.cnt["hqt"])
                                ctx.inc(g.dma_start(out=dumps["fft"][128 * tc:128 * (tc + 1), :],
                                                    in_=hqT[:, 0:1408]), "dft", 16)
                            t.wait_ge(ctx.S("hqt"), ctx.cnt["hqt"])
                            if tc > 0:
                                ctx.w(t, "rv", ctx.cnt.get("rv", 0))
                            ins = None
                            for ip in range(11):
                                for c2 in range(2):
                                    ins = t.matmul(pdf[:, cs(c2, 512)], hqT[:, cs(ip, 128)],
                                                   wdT[:, 1024 * ip + 512 * c2:1024 * ip + 512 * (c2 + 1)],
                                                   start=(ip == 0), stop=(ip == 10))
                            ctx.inc(ins, "dnt")
                            ctx.w(v, "dnt", ctx.cnt["dnt"])
                            ctx.inc(v.tensor_tensor(out=R[:, cs(tc, 1024)], in0=R[:, cs(tc, 1024)],
                                                    in1=pdf[:, 0:1024], op=OP.add), "rv")
                    emit_full_barrier(ctx)
        ctx.inc(y.dma_start(out=ap3(x1_d, NT, 1024), in_=sb3(R[:], NT, 1024)), "sto", 16)
        y.wait_ge(ctx.S("sto"), ctx.cnt["sto"])
        if dumps:
            y.wait_ge(ctx.S("dmp"), ctx.cnt.get("dmp", 0))
            if "dxn" in ctx.cnt:
                y.wait_ge(ctx.S("dxn"), ctx.cnt["dxn"])
        if F is not None:
            x1ag = nc.dram_tensor("x1ag", [NCORE * TPC, D], F32, addr_space="Shared").ap()
            x1loc = nc.dram_tensor("x1loc", [TPC, D], F32).ap()
            F["x1loc"] = x1loc
            F["ident_d"] = ident_d
            g.wait_ge(ctx.S("sto"), ctx.cnt["sto"])
            ctx.inc(g.collective_compute("AllGather", OP.bypass, replica_groups=[list(range(NCORE))],
                                         ins=[x1_d], outs=[x1ag]), "x1c")
            ctx.w(g, "x1c")
            with g.register("xoff") as xoff, g.register("xq4") as xq4:
                pid = g.partition_id()
                g.reg_mov(xq4, pid)
                g.reg_div(xq4, xq4, 4)
                g.reg_mul(xq4, xq4, 6144 * 1024)
                g.reg_mov(xoff, pid)
                g.reg_mul(xoff, xoff, 512 * 1024)
                g.reg_add(xoff, xoff, xq4)
                for p4 in range(4):
                    g.reg_add(xq4, xoff, 2048 * 1024 * p4)
                    src = bass.AP(x1ag.tensor, xq4, [[128 * 1024, 4], [16384, 8], [1, 16384]])
                    dst = bass.AP(x1loc.tensor, 128 * 1024 * p4, [[512 * 1024, 4], [16384, 8], [1, 16384]])
                    ctx.inc(g.dma_start(out=dst, in_=src), "x1l", 16)
            g.wait_ge(ctx.S("x1l"), ctx.cnt["x1l"])
            emit_full_barrier(ctx)
    return nc



def make_in_maps_l1(inputs, x1_list):
    wq1 = np.ascontiguousarray(np.asarray(inputs["L1_attn_Wq"], np.float32).T)
    wk1 = np.ascontiguousarray(np.asarray(inputs["L1_attn_Wk"], np.float32).T)
    wv1 = np.ascontiguousarray(np.asarray(inputs["L1_attn_Wv"], np.float32).T)
    table = np.asarray(inputs["L1_attn_rel_bias_table"], np.float32)
    gam = np.ascontiguousarray(np.asarray(inputs["L1_attn_ln_gamma"], np.float32).reshape(8, 128).T)
    bet = np.ascontiguousarray(np.asarray(inputs["L1_attn_ln_beta"], np.float32).reshape(8, 128).T)
    router = np.ascontiguousarray(np.asarray(inputs["L1_ffn_router_w"], np.float32))
    w0 = np.asarray(inputs["L1_ffn_w0_kernel"], np.float32)
    w1 = np.asarray(inputs["L1_ffn_w1_kernel"], np.float32)
    wo_ = np.asarray(inputs["L1_ffn_wo_kernel"], np.float32)
    ident = np.eye(128, dtype=np.float32)
    ltri = (np.arange(128)[:, None] < np.arange(128)[None, :]).astype(np.float32)
    chead = np.broadcast_to(table[4094][None, :], (128, 16)).copy()
    ii = np.arange(128)[None, None, :]
    jj = np.arange(128)[None, :, None]
    dd = np.arange(32)[:, None, None]
    if x1_list is not None:
        x1f = np.zeros((B, S, D), np.float32)
        for c in range(NCORE):
            b, p = c // 4, c % 4
            x1f[b].reshape(GT, 128, D)[_pos_tiles(p)] = np.asarray(x1_list[c]).reshape(NT, 128, D)
        x1flat = x1f.reshape(B * S, D)
    maps = []
    for c in range(NCORE):
        p = c % 4
        delta = dd + 32 - 16 * p
        idx = np.minimum(8191 - 128 * delta + ii - jj, 4094)
        blk = table[np.maximum(idx, 0), :] * (delta >= 32)[..., None]
        blk = np.ascontiguousarray(blk.transpose(3, 0, 1, 2)).astype(np.float32)
        um = (np.arange(64)[:, None] <= np.minimum(16 * p + np.arange(16)[None, :] + 31, 63)).astype(np.float32)
        mp = dict(
            wq1=wq1, wk1=wk1, wv1=wv1,
            lngam=gam, lnbet=bet, router=router,
            w0e=np.ascontiguousarray(w0[c]), w1e=np.ascontiguousarray(w1[c]),
            woe=np.ascontiguousarray(wo_[c]),
            biasblk=blk, chead=chead, umat=um, ltri=ltri, ident=ident)
        if x1_list is not None:
            mp["x1"] = np.ascontiguousarray(x1flat[2048 * c:2048 * (c + 1)])
        maps.append(mp)
    return maps


def assemble_contig(res, key):
    out = np.zeros((B * S, D), np.float32)
    for c in range(NCORE):
        out[2048 * c:2048 * (c + 1)] = np.asarray(res[c][key])
    return out.reshape(B, S, D)


class _NcSuffix:
    def __init__(self, nc, suf):
        self._nc = nc
        self._suf = suf

    def sbuf_tensor(self, name, shape, dt, **kw):
        return self._nc.sbuf_tensor(name + self._suf, shape, dt, **kw)

    def psum_tensor(self, name, shape, dt=F32, **kw):
        return self._nc.psum_tensor(name + self._suf, shape, dt, **kw)

    def __getattr__(self, a):
        return getattr(self._nc, a)


def build_l1(dump=(), phases=9, p3sub=9, F=None):
    nc = F["nc"] if F is not None else bass.Bass("TRN2", target_bir_lowering=False, debug=False)
    if F is not None:
        nc = _NcSuffix(nc, "_1")
        F["ctx"].nc = nc
    g, y, t, v, s = nc.gpsimd, nc.sync, nc.tensor, nc.vector, nc.scalar
    x1_d = F["x1loc"] if F is not None else nc.dram_tensor("x1", [TPC, D], F32, kind="ExternalInput").ap()
    wq_d = nc.dram_tensor("wq1", [D, D], F32, kind="ExternalInput").ap()
    wk_d = nc.dram_tensor("wk1", [D, D], F32, kind="ExternalInput").ap()
    wv_d = nc.dram_tensor("wv1", [D, D], F32, kind="ExternalInput").ap()
    gam_d = nc.dram_tensor("lngam", [128, 8], F32, kind="ExternalInput").ap()
    bet_d = nc.dram_tensor("lnbet", [128, 8], F32, kind="ExternalInput").ap()
    rt_d = nc.dram_tensor("router", [D, E], F32, kind="ExternalInput").ap()
    w0_d = nc.dram_tensor("w0e", [D, HMOE], F32, kind="ExternalInput").ap()
    w1_d = nc.dram_tensor("w1e", [D, HMOE], F32, kind="ExternalInput").ap()
    wo_d = nc.dram_tensor("woe", [HMOE, D], F32, kind="ExternalInput").ap()
    bb_d = nc.dram_tensor("biasblk", [NH, 32, 128, 128], F32, kind="ExternalInput").ap()
    ch_d = nc.dram_tensor("chead", [128, NH], F32, kind="ExternalInput").ap()
    um_d = nc.dram_tensor("umat", [64, 16], F32, kind="ExternalInput").ap()
    lt_d = nc.dram_tensor("ltri", [128, 128], F32, kind="ExternalInput").ap()
    ident_d = F["ident_d"] if F is not None else nc.dram_tensor("ident", [128, 128], F32, kind="ExternalInput").ap()
    out_d = nc.dram_tensor("out", [TPC, D], F32, kind="ExternalOutput").ap()
    dumps = {}
    _dsh = {"xn1": ([TPC, D], F32), "aot1": ([128, 8 * TPC], F32), "y1": ([TPC, D], F32),
            "lgt": ([TPC, E], F32), "gat": ([TPC, 16], F32), "ei": ([4096, D], F32),
            "eo": ([4096, D], F32), "ffo": ([TPC, D], F32), "elo": ([4096, D], F32)}
    for name in dump:
        dumps[name] = nc.dram_tensor("d_" + name, _dsh[name][0], _dsh[name][1],
                                     kind="ExternalOutput").ap()

    kvin = nc.dram_tensor("kvin1", [2048, 2048], BF16).ap()
    kvag = nc.dram_tensor("kvag1", [NCORE * 2048, 2048], BF16, addr_space="Shared").ap()
    kvloc = nc.dram_tensor("kvloc1", [4 * 2048, 2048], BF16).ap()
    xnin = nc.dram_tensor("xnin", [TPC, D], F32).ap()
    y1d = nc.dram_tensor("y1d", [TPC, D], F32).ap()
    eiin = nc.dram_tensor("eiin", [4096, D], BF16).ap()
    eiag = nc.dram_tensor("eiag", [NCORE * 4096, D], BF16, addr_space="Shared").ap()
    eloc = nc.dram_tensor("eloc", [4096, D], BF16).ap()
    eoin = nc.dram_tensor("eoin", [4096, D], BF16).ap()
    eoag = nc.dram_tensor("eoag", [NCORE * 4096, D], BF16, addr_space="Shared").ap()
    eocd = nc.dram_tensor("eocd", [4096, D], BF16).ap()
    ffout = nc.dram_tensor("ffout", [TPC, D], F32).ap()

    with contextlib.ExitStack() as ex:
        if F is not None:
            ctx = F["ctx"]
            ctx.ex = ex
        else:
            ctx = Ctx(nc, ex)
        ph1 = ["lsv", "lsq", "lns", "ln1", "lnx", "ln2", "ln3", "xnw", "xb", "xld",
               "wab", "pj_t", "pj_v", "kvst", "kvc", "kvl", "dxn"]
        attn = ["at", "ktl", "bbl", "bt", "vt_y", "vt_v", "hp_done", "ex", "av", "ev",
                "ev1", "ev2", "rzv", "rbt", "rzc", "bev", "tsm", "tse", "tst", "tsv",
                "cum", "csc", "vone"]
        mixer = ["mst", "mse", "msq", "msm", "msr", "msl", "mbc", "mbe", "ym", "ytt", "ytr", "y1w"]
        moe = ["mio", "mld", "mxf", "xgc", "mgh", "mge", "rlt", "rle", "rtt", "rtv",
               "c1m", "c1me", "c2m", "c2me", "m1c", "m1ce", "m1cb", "m1cbe", "dbd", "dsp",
               "dse", "eiw", "eic", "elc", "els", "xet", "wst2", "h0m", "h1m", "gsq",
               "gs2", "gth", "sgx", "wol2", "eom", "eoe", "eow", "eoc", "eor", "ctt",
               "ctr", "cb2", "cbd", "ffm", "ffe", "ffw", "ffc"]
        final = ["fld", "fadd", "fst"]
        for i, n in enumerate(mixer):
            ctx.alias[n] = ph1[i]
        pool_moe = attn + [p for p in ph1 if p not in ("dxn",)] + ["xp%d" % i for i in range(10)]
        for i, n in enumerate(moe):
            ctx.alias[n] = pool_moe[i]
        for i, n in enumerate(final):
            ctx.alias[n] = mixer[i]
        ctx.alias = {k: ctx.alias.get(vv, vv) for k, vv in ctx.alias.items()}
        id_bf = ctx.sb("id_bf", [128, 128], BF16)
        id_f = ctx.sb("id_f", [128, 128], F32)
        ones_f = ctx.sb("ones_f", [128, 64], F32)
        onesc_bf = ctx.sb("onesc_bf", [128, 1], BF16)
        onesr_f = ctx.sb("onesr_f", [1, 128], F32)
        eps_c = ctx.sb("eps_c", [128, 1], F32)
        ctx.barscr = ctx.sb("barscr", [128, 8], F32)
        um_bf = ctx.sb("um_bf", [64, 16], BF16)
        lt_bf = ctx.sb("lt_bf", [128, 128], BF16)
        allon_bf = ctx.sb("allon_bf", [128, 128], BF16)
        ch_sb = ctx.sb("ch_sb", [128, NH], F32)
        gam_sb = ctx.sb("gam_sb", [128, 8], F32)
        bet_sb = ctx.sb("bet_sb", [128, 8], F32)

        ctx.inc(g.dma_start(out=id_bf[:], in_=ident_d), "ld", 16)
        ctx.inc(g.dma_start(out=id_f[:], in_=ident_d), "ld", 16)
        ctx.inc(g.dma_start(out=um_bf[:], in_=um_d), "ld", 16)
        ctx.inc(g.dma_start(out=lt_bf[:], in_=lt_d), "ld", 16)
        ctx.inc(g.dma_start(out=ch_sb[:], in_=ch_d), "ld", 16)
        ctx.inc(g.dma_start(out=gam_sb[:], in_=gam_d), "ld", 16)
        ctx.inc(g.dma_start(out=bet_sb[:], in_=bet_d), "ld", 16)
        g.memset(ones_f[:], 1.0)
        g.memset(onesc_bf[:], 1.0)
        g.memset(onesr_f[:], 1.0)
        g.memset(allon_bf[:], 1.0)
        ctx.inc(g.memset(eps_c[:], EPS), "ld", 1)

        exA1 = ctx.ex.enter_context(contextlib.ExitStack())
        AOT = exA1.enter_context(nc.sbuf_tensor("AOT1", [128, 8 * TPC], F32))
        exQ = ctx.ex.enter_context(contextlib.ExitStack())
        QT = exQ.enter_context(nc.sbuf_tensor("QT1", [128, NHP * TPC], BF16))
        exX = ctx.ex.enter_context(contextlib.ExitStack())
        xnT = exX.enter_context(nc.sbuf_tensor("xnT1", [128, 8 * TPC], BF16))
        with nc.sbuf_tensor("xt1", [128, 2 * 1024], F32) as R1t, \
             nc.sbuf_tensor("xnf", [128, 2 * 1024], F32) as xnf, \
             nc.sbuf_tensor("xnb", [128, 2 * 1024], BF16) as xnb, \
             nc.sbuf_tensor("lnt", [128, 8 * NT], F32) as lnt, \
             nc.sbuf_tensor("scr", [128, 2048], F32) as scr, \
             nc.sbuf_tensor("wA", [128, 8 * 1024], BF16) as wA, \
             nc.sbuf_tensor("kst", [128, 2048], BF16) as kst, \
             nc.psum_tensor("psqA", [128, 2048], F32) as psqA, \
             nc.psum_tensor("psqB", [128, 2048], F32) as psqB:
            pst = psqA[:].bitcast(BF16)
            st = dict(sumx=lnt[:, 0:NT], sumsq=lnt[:, NT:2 * NT], m=lnt[:, 2 * NT:3 * NT],
                      msq=lnt[:, 3 * NT:4 * NT], var=lnt[:, 4 * NT:5 * NT],
                      lnv=lnt[:, 5 * NT:6 * NT], rstd=lnt[:, 6 * NT:7 * NT],
                      mrn=lnt[:, 7 * NT:8 * NT])
            for tt in range(NT):
                sl = tt % 2
                if tt >= 2:
                    ctx.w(g, "lsv", ctx.cnt.get("lsv", 0) - 2)
                    ctx.w(g, "lsq", ctx.cnt.get("lsq", 0) - 1)
                ctx.inc(g.dma_start(out=sb3(R1t[:, cs(sl, 1024)], 1, 1024),
                                    in_=ap3(x1_d, 1, 1024, off_elems=1024 * 128 * tt)), "xld", 16)
                v.wait_ge(ctx.S("xld"), ctx.cnt["xld"])
                s.wait_ge(ctx.S("xld"), ctx.cnt["xld"])
                ctx.inc(v.reduce_sum(out=st["sumx"][:, tt:tt + 1], in_=R1t[:, cs(sl, 1024)],
                                     axis=AX.X), "lsv")
                ctx.inc(s.activation(out=scr[:, 0:1024], in_=R1t[:, cs(sl, 1024)],
                                     func=AF.Square), "lsq")
                ctx.w(v, "lsq")
                ctx.inc(v.reduce_sum(out=st["sumsq"][:, tt:tt + 1], in_=scr[:, 0:1024],
                                     axis=AX.X), "lsv")
                ctx.w(s, "lsv")
            ctx.w(v, "lsv")
            iv = 1.0 / 1024.0
            ctx.inc(v.tensor_scalar(out=st["m"], in0=st["sumx"], scalar1=iv, scalar2=None, op0=OP.mult), "lns")
            ctx.w(v, "lns")
            ctx.inc(v.tensor_scalar(out=st["msq"], in0=st["sumsq"], scalar1=iv, scalar2=None, op0=OP.mult), "lns")
            ctx.w(v, "lns")
            ctx.inc(v.tensor_tensor(out=st["var"], in0=st["m"], in1=st["m"], op=OP.mult), "lns")
            ctx.w(v, "lns")
            ctx.inc(v.tensor_tensor(out=st["var"], in0=st["msq"], in1=st["var"], op=OP.subtract), "ln1")
            ctx.w(s, "ln1")
            ctx.inc(s.activation(out=st["lnv"], in_=st["var"], func=AF.Ln, bias=eps_c[:, 0:1]), "lnx")
            ctx.w(s, "lnx")
            ctx.inc(s.activation(out=st["rstd"], in_=st["lnv"], func=AF.Exp, scale=-0.5), "ln2")
            ctx.w(v, "ln2")
            ctx.inc(v.scalar_tensor_tensor(out=st["mrn"], in0=st["m"], scalar=-1.0,
                                           in1=st["rstd"], op0=OP.mult, op1=OP.mult), "ln3")
            ctx.w(v, "ln3")
            for tt in range(NT):
                sl = tt % 2
                rst = st["rstd"][:, tt:tt + 1]
                mrn = st["mrn"][:, tt:tt + 1]
                if tt >= 2:
                    ctx.w(g, "xb", ctx.cnt.get("xb", 0) - 1)
                ctx.inc(g.dma_start(out=sb3(R1t[:, cs(sl, 1024)], 1, 1024),
                                    in_=ap3(x1_d, 1, 1024, off_elems=1024 * 128 * tt)), "xld", 16)
                v.wait_ge(ctx.S("xld"), ctx.cnt["xld"])
                if tt >= 2:
                    ctx.w(v, "dxn", ctx.cnt.get("dxn", 0))
                ctx.inc(v.tensor_scalar(out=xnf[:, cs(sl, 1024)], in0=R1t[:, cs(sl, 1024)],
                                        scalar1=rst, scalar2=mrn, op0=OP.mult, op1=OP.add), "xnw")
                if tt >= 2:
                    ctx.w(v, "tr_v", ctx.cnt.get("tr_v", 0) - 8)
                ctx.inc(v.tensor_scalar(out=xnb[:, cs(sl, 1024)], in0=R1t[:, cs(sl, 1024)],
                                        scalar1=rst, scalar2=mrn, op0=OP.mult, op1=OP.add), "xb")
                ctx.w(g, "xnw", ctx.cnt["xnw"])
                ctx.inc(g.dma_start(out=ap3(xnin, 1, 1024, off_elems=1024 * 128 * tt),
                                    in_=xnf[:, cs(sl, 1024)].unsqueeze(1)), "dxn", 16)
                ctx.w(t, "xb", ctx.cnt["xb"])
                for j in range(8):
                    emit_t128(ctx, xnb[:, cs(sl, 1024)][:, cs(j, 128)],
                              xnT[:, 2048 * j + 128 * tt:2048 * j + 128 * tt + 128], pst, id_bf)
            g.wait_ge(ctx.S("dxn"), ctx.cnt["dxn"])
            if "xn1" in dumps:
                ctx.inc(g.dma_start(out=dumps["xn1"], in_=xnin), "dmp", 16)
            ctx.inc(g.dma_start(out=sb3(wA[:], 8, 1024), in_=ap3(wq_d, 8, 1024)), "wab", 16)
            t.wait_ge(ctx.S("wab"), ctx.cnt["wab"])
            ctx.w(t, "tr_v", ctx.cnt["tr_v"])
            pjt0 = ctx.cnt.get("pj_t", 0)

            def evac_qt(hp, ps):
                ctx.w(v, "pj_t", ctx.cnt.get("pj_t", 0))
                ctx.inc(v.tensor_copy(QT[:, cs(hp, 2048)], ps[:, 0:2048]), "pj_v")

            emit_proj(ctx, wA, xnT, psqA, psqB, evac_qt, "pj")

            def evac_kv(hp, ps, base_row):
                ctx.w(v, "pj_t", ctx.cnt.get("pj_t", 0))
                nst = ctx.cnt.get("kvst", 0)
                if nst >= 16:
                    ctx.w(v, "kvst", nst)
                ctx.inc(v.tensor_copy(kst[:], ps[:, 0:2048]), "pj_v")
                ctx.w(g, "pj_v", ctx.cnt["pj_v"])
                ctx.inc(g.dma_start(out=kvin[base_row + 128 * hp:base_row + 128 * (hp + 1), :],
                                    in_=kst[:]), "kvst", 16)

            ctx.w(g, "pj_t", pjt0 + 8)
            ctx.inc(g.dma_start(out=sb3(wA[:], 8, 1024), in_=ap3(wk_d, 8, 1024)), "wab", 16)
            t.wait_ge(ctx.S("wab"), ctx.cnt["wab"])
            emit_proj(ctx, wA, xnT, psqA, psqB, lambda hp, ps: evac_kv(hp, ps, 0), "pj")
            ctx.w(g, "pj_t", pjt0 + 16)
            ctx.inc(g.dma_start(out=sb3(wA[:], 8, 1024), in_=ap3(wv_d, 8, 1024)), "wab", 16)
            t.wait_ge(ctx.S("wab"), ctx.cnt["wab"])
            emit_proj(ctx, wA, xnT, psqA, psqB, lambda hp, ps: evac_kv(hp, ps, 1024), "pj")
            g.wait_ge(ctx.S("kvst"), ctx.cnt["kvst"])
            ctx.inc(g.collective_compute("AllGather", OP.bypass, replica_groups=[list(range(NCORE))],
                                         ins=[kvin], outs=[kvag]), "kvc")
            ctx.w(g, "kvc")
            with g.register("roff1") as roff:
                pid = g.partition_id()
                g.reg_mov(roff, pid)
                g.reg_div(roff, roff, 4)
                g.reg_mul(roff, roff, 4 * 2048 * 2048)
                src = bass.AP(kvag.tensor, roff, [[2048, 8192], [1, 2048]])
                ctx.inc(g.dma_start(out=kvloc, in_=src), "kvl", 16)
            g.wait_ge(ctx.S("kvl"), ctx.cnt["kvl"])
            emit_full_barrier(ctx)
        exX.close()

        if phases >= 2:
            exAtt = ctx.ex.enter_context(contextlib.ExitStack())
            KTl = exAtt.enter_context(nc.sbuf_tensor("KTl", [128, 8192], BF16))
            Vl = exAtt.enter_context(nc.sbuf_tensor("Vl", [128, 64 * 130], BF16))
            vst = exAtt.enter_context(nc.sbuf_tensor("vst", [128, 256], BF16))
            PT = exAtt.enter_context(nc.sbuf_tensor("PT", [128, 2048], BF16))
            rzb = exAtt.enter_context(nc.sbuf_tensor("rzb", [65, 2048], F32))
            bb_bf = exAtt.enter_context(nc.sbuf_tensor("bb_bf", [128, 2 * 32 * 128], BF16))
            bias_sb = exAtt.enter_context(nc.sbuf_tensor("bias_sb", [64, 2048], F32))
            ts_sb = exAtt.enter_context(nc.sbuf_tensor("ts_sb", [64, 128], F32))
            tst_bf = exAtt.enter_context(nc.sbuf_tensor("tst_bf", [64, 128], BF16))
            cumnsc = exAtt.enter_context(nc.sbuf_tensor("cumnsc", [64, 32], F32))
            tmp_sb = exAtt.enter_context(nc.sbuf_tensor("tmp_sb", [64, 2048], F32))
            psS = exAtt.enter_context(nc.psum_tensor("psS", [128, 2048], F32))
            psO1 = exAtt.enter_context(nc.psum_tensor("psO1", [128, 1024], F32))
            psO2 = exAtt.enter_context(nc.psum_tensor("psO2", [128, 1024], F32))
            if True:
                ov = Vl[:].rearrange("p (k s) -> p k s", s=65)[:, :, 64:65]
                ctx.inc(v.memset(ov, 1.0), "vone")
                for hp in range(NHP):
                    if hp == 0:
                        ctx.w(g, "kvl")
                    else:
                        ctx.w(g, "at", ctx.cnt.get("at", 0))
                    for r in range(4):
                        ctx.inc(g.dma_start(out=KTl[:, cs(r, 2048)],
                                            in_=kvloc[2048 * r + 128 * hp:2048 * r + 128 * (hp + 1), :]),
                                "ktl", 16)
                    if hp > 0:
                        ctx.w(g, "bt", ctx.cnt.get("bt", 0))
                    for hh in range(2):
                        bbsrc = bass.AP(bb_d.tensor, (2 * hp + hh) * 32 * 128 * 128,
                                        [[128, 128], [128 * 128, 32], [1, 128]])
                        ctx.inc(g.dma_start(out=sb3(bb_bf[:, 4096 * hh:4096 * (hh + 1)], 32, 128),
                                            in_=bbsrc), "bbl", 16)
                    vt_b = ctx.cnt.get("vt_v", 0)
                    for kt in range(64):
                        r, lt = kt // 16, kt % 16
                        if kt >= 2:
                            ctx.w(y, "vt_v", vt_b + kt - 1)
                        if hp == 0 and kt == 0:
                            ctx.w(y, "kvl")
                            ctx.w(y, "vone")
                        if hp > 0 and kt == 0:
                            ctx.w(y, "hp_done", ctx.cnt.get("hp_done", 0))
                        ctx.inc(y.dma_start_transpose(vst[:, cs(kt % 2, 128)],
                                                      kvloc[2048 * r + 1024 + 128 * hp:2048 * r + 1024 + 128 * (hp + 1),
                                                            cs(lt, 128)]), "vt_y", 16)
                        ctx.w(v, "vt_y")
                        v.tensor_copy(Vl[:, 130 * kt:130 * kt + 64], vst[:, cs(kt % 2, 128)][:, 0:64])
                        ctx.inc(v.tensor_copy(Vl[:, 130 * kt + 65:130 * kt + 129],
                                              vst[:, cs(kt % 2, 128)][:, 64:128]), "vt_v")
                    for qh in range(2):
                        exb = ctx.cnt.get("ex", 0)
                        atb = ctx.cnt.get("at", 0)
                        avb = ctx.cnt.get("av", 0)
                        gi = 0
                        for kt in range(64):
                            kcol = 2048 * (kt // 16) + 128 * (kt % 16)
                            for qc in range(2):
                                off = 512 * qc
                                qcol = 2048 * hp + 1024 * qh + off
                                slot = gi % 2
                                if gi >= 2:
                                    ctx.w(t, "ex", exb + gi - 1)
                                if gi == 0:
                                    t.wait_ge(ctx.S("ktl"), ctx.cnt["ktl"])
                                    ctx.w(t, "ev", ctx.cnt.get("ev", 0))
                                t.matmul(psS[:, 1024 * slot:1024 * slot + 512],
                                         KTl[0:64, kcol:kcol + 128], QT[0:64, qcol:qcol + 512],
                                         start=True, stop=True, tile_position=(0, 0))
                                ctx.inc(t.matmul(psS[:, 1024 * slot + 512:1024 * slot + 1024],
                                                 KTl[64:128, kcol:kcol + 128], QT[64:128, qcol:qcol + 512],
                                                 start=True, stop=True, tile_position=(64, 0)), "at")
                                ctx.w(s, "at", atb + gi + 1)
                                if gi >= 2:
                                    ctx.w(s, "av", avb + gi - 1)
                                ctx.inc(s.activation(out=PT[:, 1024 * slot:1024 * slot + 1024],
                                                     in_=psS[:, 1024 * slot:1024 * slot + 1024],
                                                     func=AF.Exp, scale=SC1), "ex")
                                ctx.w(t, "ex", ctx.cnt["ex"])
                                ctx.w(t, "vt_v", vt_b + kt + 1)
                                t.matmul(psO1[0:65, off:off + 512], Vl[:, 130 * kt:130 * kt + 65],
                                         PT[:, 1024 * slot:1024 * slot + 512],
                                         start=(kt == 0), stop=False)
                                ctx.inc(t.matmul(psO2[0:65, off:off + 512], Vl[:, 130 * kt + 65:130 * kt + 130],
                                                 PT[:, 1024 * slot + 512:1024 * slot + 1024],
                                                 start=(kt == 0), stop=False), "av")
                                gi += 1
                        ctx.w(t, "ex", ctx.cnt["ex"])
                        ctx.w(t, "av", ctx.cnt["av"])
                        t.wait_ge(ctx.S("bbl"), ctx.cnt["bbl"])
                        if True:
                            tl0, tl1 = (0, 8) if qh == 0 else (8, 16)
                            for hh in range(2):
                                ins = None
                                for tt in range(tl0, tl1):
                                    for ss in range(tt + 32, 64):
                                        dpr = ss - tt - 32
                                        ins = t.matmul(psS[0:64, 1024 * hh + 128 * (tt - tl0):1024 * hh + 128 * (tt - tl0 + 1)],
                                                       Vl[:, 130 * ss + 65 * hh:130 * ss + 65 * hh + 64],
                                                       bb_bf[:, 4096 * hh + 128 * dpr:4096 * hh + 128 * (dpr + 1)],
                                                       start=(ss == tt + 32), stop=(ss == 63))
                                ctx.inc(ins, "bt")
                            ctx.w(v, "bt", ctx.cnt["bt"])
                            ctx.inc(v.tensor_copy(bias_sb[0:64, 0:2048], psS[0:64, 0:2048]), "bev")
                            ctx.w(t, "bev", ctx.cnt["bev"])
                            ins = None
                            for hh in range(2):
                                for kt in range(64):
                                    ins = t.matmul(psS[0:64, 1024 * hh + kt:1024 * hh + kt + 1],
                                                   Vl[:, 130 * kt + 65 * hh:130 * kt + 65 * hh + 64],
                                                   onesc_bf[:], start=True, stop=True)
                            ctx.inc(ins, "tsm")
                            ctx.w(v, "tsm", ctx.cnt["tsm"])
                            v.tensor_copy(ts_sb[0:64, 0:64], psS[0:64, 0:64])
                            ctx.inc(v.tensor_copy(ts_sb[0:64, 64:128], psS[0:64, 1024:1088]), "tse")
                            ctx.w(t, "tse", ctx.cnt["tse"])
                            for hh in range(2):
                                ctx.inc(t.transpose(psS[0:64, 0:64],
                                                    ts_sb[0:64, 64 * hh:64 * (hh + 1)], id_f[0:64, 0:64]), "tst")
                                ctx.w(v, "tst", ctx.cnt["tst"])
                                ctx.inc(v.tensor_copy(tst_bf[0:64, 64 * hh:64 * (hh + 1)], psS[0:64, 0:64]), "tsv")
                                ctx.w(t, "tsv", ctx.cnt["tsv"])
                            for hh in range(2):
                                ctx.inc(t.matmul(psS[0:64, 128 * hh:128 * hh + 16],
                                                 tst_bf[0:64, 64 * hh:64 * (hh + 1)], um_bf[:],
                                                 start=True, stop=True), "cum")
                            ctx.w(v, "cum", ctx.cnt["cum"])
                            for hh in range(2):
                                hgl = 2 * hp + hh
                                ctx.inc(v.tensor_scalar(out=cumnsc[0:64, 16 * hh:16 * (hh + 1)],
                                                        in0=psS[0:64, 128 * hh:128 * hh + 16],
                                                        scalar1=ch_sb[0:64, hgl:hgl + 1], scalar2=None,
                                                        op0=OP.mult), "csc")
                            ctx.w(v, "csc", ctx.cnt["csc"])
                        ctx.w(v, "av", ctx.cnt["av"])
                        v.reciprocal(out=rzb[64:65, 0:1024], in_=psO1[64:65, 0:1024])
                        ctx.inc(v.reciprocal(out=rzb[64:65, 1024:2048], in_=psO2[64:65, 0:1024]), "rzv")
                        ctx.w(t, "rzv", ctx.cnt["rzv"])
                        ins = None
                        for hh in range(2):
                            for c2 in range(2):
                                ins = t.matmul(psS[0:64, 1024 * hh + 512 * c2:1024 * hh + 512 * (c2 + 1)],
                                               ones_f[64:65, 0:64],
                                               rzb[64:65, 1024 * hh + 512 * c2:1024 * hh + 512 * (c2 + 1)],
                                               start=True, stop=True)
                        ctx.inc(ins, "rbt")
                        ctx.w(v, "rbt", ctx.cnt["rbt"])
                        ctx.inc(v.tensor_copy(rzb[0:64, 0:2048], psS[0:64, 0:2048]), "rzc")
                        ctx.w(v, "rzc")
                        acol = 2048 * hp + 1024 * qh
                        for hh, psO in ((0, psO1), (1, psO2)):
                            ctx.inc(v.tensor_tensor(out=tmp_sb[0:64, 1024 * hh:1024 * (hh + 1)],
                                                    in0=psO[0:64, 0:1024],
                                                    in1=rzb[0:64, 1024 * hh:1024 * (hh + 1)], op=OP.mult), "ev1")
                            ctx.w(v, "ev1")
                            ctx.inc(v.tensor_tensor(out=tmp_sb[0:64, 1024 * hh:1024 * (hh + 1)],
                                                    in0=tmp_sb[0:64, 1024 * hh:1024 * (hh + 1)],
                                                    in1=bias_sb[0:64, 1024 * hh:1024 * (hh + 1)],
                                                    op=OP.add), "ev1")
                            ctx.w(v, "ev1")
                            for tl in range(8):
                                tg = 8 * qh + tl
                                ctx.inc(v.tensor_scalar(
                                    out=AOT[64 * hh:64 * (hh + 1), acol + 128 * tl:acol + 128 * (tl + 1)],
                                    in0=tmp_sb[0:64, 1024 * hh + 128 * tl:1024 * hh + 128 * (tl + 1)],
                                    scalar1=cumnsc[0:64, 16 * hh + tg:16 * hh + tg + 1], scalar2=None,
                                    op0=OP.add), "ev2")
                        ctx.inc(v.tensor_copy(ctx.barscr[:, 6:7], ctx.barscr[:, 7:8]), "ev")
                    ctx.w(y, "ev", ctx.cnt.get("ev", 0))
                    ctx.inc(y.nop(), "hp_done")
                if "aot1" in dumps:
                    ctx.w(g, "ev", ctx.cnt.get("ev", 0))
                    ctx.inc(g.dma_start(out=dumps["aot1"], in_=AOT[:]), "dmp", 16)
            emit_full_barrier(ctx)
            exAtt.close()
            exQ.close()
        if phases >= 3:
            class _SkipRest(Exception):
                pass
            exMix = ctx.ex.enter_context(contextlib.ExitStack())
            sqs = exMix.enter_context(nc.sbuf_tensor("sqs", [128, 2048], F32))
            sm0 = exMix.enter_context(nc.sbuf_tensor("sm0", [1, 2048], F32))
            sm1 = exMix.enter_context(nc.sbuf_tensor("sm1", [1, 2048], F32))
            sm2 = exMix.enter_context(nc.sbuf_tensor("sm2", [1, 2048], F32))
            sm3 = exMix.enter_context(nc.sbuf_tensor("sm3", [1, 2048], F32))
            bcst = exMix.enter_context(nc.sbuf_tensor("bcst", [128, 2 * 2048], F32))
            y1T = exMix.enter_context(nc.sbuf_tensor("y1T", [128, 2048], BF16))
            y1f = exMix.enter_context(nc.sbuf_tensor("y1f", [128, NT * 1024], BF16))
            psm = exMix.enter_context(nc.psum_tensor("psm", [128, 2048], F32))
            psty = exMix.enter_context(nc.psum_tensor("psty", [128, 2048], BF16))
            if True:
                try:
                    if p3sub < 1:
                        raise _SkipRest
                    onescol = ones_f[0:128, 0:1]
                    ins = None
                    for dp in range(8):
                        for c4 in range(4):
                            ins = t.matmul(psm[0:1, cs(c4, 512)], onescol,
                                           AOT[:, 2048 * dp + 512 * c4:2048 * dp + 512 * (c4 + 1)],
                                           start=(dp == 0), stop=(dp == 7))
                    ctx.inc(ins, "mst")
                    ctx.w(v, "mst", ctx.cnt["mst"])
                    ctx.inc(v.tensor_copy(sm0[0:1, :], psm[0:1, 0:2048]), "mse")
                    if p3sub < 2:
                        raise _SkipRest
                    ctx.w(t, "mse", ctx.cnt["mse"])
                    for dp in range(8):
                        ctx.inc(s.activation(out=sqs[:, 0:2048], in_=AOT[:, cs(dp, 2048)], func=AF.Square), "msq")
                        t.wait_ge(ctx.S("msq"), ctx.cnt["msq"])
                        ins = None
                        for c4 in range(4):
                            ins = t.matmul(psm[0:1, cs(c4, 512)], onescol, sqs[:, cs(c4, 512)],
                                           start=(dp == 0), stop=(dp == 7))
                        ctx.inc(ins, "msm")
                        ctx.w(s, "msm", ctx.cnt["msm"])
                    ctx.w(v, "msm", ctx.cnt["msm"])
                    ctx.inc(v.tensor_copy(sm1[0:1, :], psm[0:1, 0:2048]), "mse")
                    ctx.w(v, "mse")
                    iv = 1.0 / 1024.0
                    ctx.inc(v.tensor_scalar(out=sm0[0:1, :], in0=sm0[0:1, :], scalar1=iv, scalar2=None, op0=OP.mult), "msr")
                    ctx.w(v, "msr")
                    ctx.inc(v.tensor_scalar(out=sm1[0:1, :], in0=sm1[0:1, :], scalar1=iv, scalar2=None, op0=OP.mult), "msr")
                    ctx.w(v, "msr")
                    ctx.inc(v.tensor_tensor(out=sm2[0:1, :], in0=sm0[0:1, :], in1=sm0[0:1, :], op=OP.mult), "msr")
                    ctx.w(v, "msr")
                    ctx.inc(v.tensor_tensor(out=sm1[0:1, :], in0=sm1[0:1, :], in1=sm2[0:1, :], op=OP.subtract), "msr")
                    ctx.w(s, "msr", ctx.cnt["msr"])
                    ctx.inc(s.activation(out=sm2[0:1, :], in_=sm1[0:1, :], func=AF.Ln, bias=eps_c[0:1, 0:1]), "msl")
                    ctx.w(s, "msl")
                    ctx.inc(s.activation(out=sm2[0:1, :], in_=sm2[0:1, :], func=AF.Exp, scale=-0.5), "msl")
                    ctx.w(v, "msl", ctx.cnt["msl"])
                    ctx.inc(v.tensor_tensor(out=sm3[0:1, :], in0=sm0[0:1, :], in1=sm2[0:1, :], op=OP.mult), "msr")
                    ctx.w(t, "msr", ctx.cnt["msr"])
                    ctx.w(t, "mse", ctx.cnt["mse"])
                    if p3sub < 3:
                        raise _SkipRest
                    for k, src_row in ((0, sm2), (1, sm3)):
                        ins = None
                        for c4 in range(4):
                            ins = t.matmul(psm[:, cs(c4, 512)], onesr_f[0:1, :],
                                           src_row[0:1, cs(c4, 512)], start=True, stop=True)
                        ctx.inc(ins, "mbc")
                        ctx.w(v, "mbc", ctx.cnt["mbc"])
                        ctx.inc(v.tensor_copy(bcst[:, cs(k, 2048)], psm[:, 0:2048]), "mbe")
                        ctx.w(t, "mbe", ctx.cnt["mbe"])
                    if p3sub < 4:
                        raise _SkipRest
                    ctx.S("ytt"); ctx.S("ytr")
                    ytt0 = ctx.cnt.get("ytt", 0)
                    ytr0 = ctx.cnt.get("ytr", 0)
                    for dp in range(8 if p3sub >= 7 else 1):
                        ctx.w(v, "mbe", ctx.cnt["mbe"])
                        if dp > 0:
                            ctx.w(v, "ytr", ctx.cnt.get("ytr", 0))
                        ctx.inc(v.tensor_tensor(out=sqs[:, 0:2048], in0=AOT[:, cs(dp, 2048)],
                                                in1=bcst[:, 0:2048], op=OP.mult), "ym")
                        ctx.w(v, "ym")
                        ctx.inc(v.tensor_tensor(out=sqs[:, 0:2048], in0=sqs[:, 0:2048],
                                                in1=bcst[:, 2048:4096], op=OP.subtract), "ym")
                        ctx.w(v, "ym")
                        ctx.inc(v.tensor_scalar(out=y1T[:, 0:2048], in0=sqs[:, 0:2048],
                                                scalar1=gam_sb[:, dp:dp + 1], scalar2=bet_sb[:, dp:dp + 1],
                                                op0=OP.mult, op1=OP.add), "ym")
                        ctx.w(t, "ym", ctx.cnt["ym"])
                        for tc in range(NT if p3sub >= 5 else 0):
                            jt = ctx.cnt["ytt"] - ytt0
                            sl2 = (jt % 2) * 1024
                            if jt >= 2:
                                ctx.w(t, "ytr", ytr0 + jt - 1)
                            ctx.inc(t.transpose(psty[:, sl2:sl2 + 128], y1T[:, cs(tc, 128)], id_bf[:]), "ytt")
                            ctx.w(v, "ytt", ytt0 + jt + 1)
                            ctx.inc(v.tensor_copy(y1f[:, 1024 * tc + 128 * dp:1024 * tc + 128 * (dp + 1)],
                                                  psty[:, sl2:sl2 + 128]), "ytr")
                    if p3sub < 6:
                        raise _SkipRest
                    ctx.w(g, "ytr", ctx.cnt.get("ytr", 0))
                    ctx.inc(g.dma_start(out=ap3(y1d, NT, 1024), in_=sb3(y1f[:], NT, 1024)), "y1w", 16)
                    g.wait_ge(ctx.S("y1w"), ctx.cnt["y1w"])
                    if "y1" in dumps:
                        ctx.inc(g.dma_start(out=dumps["y1"], in_=y1d), "dmp", 16)
                except _SkipRest:
                    pass
                finally:
                    emit_full_barrier(ctx)
            exMix.close()
            exA1.close()

        if phases >= 4:
            def hsv(instr, name="mgh"):
                ctx.inc(instr, name)
                ctx.w(v, name)
                return instr

            exMoe = ctx.ex.enter_context(contextlib.ExitStack())
            iotaq = exMoe.enter_context(nc.sbuf_tensor("iotaq", [128, 4096], F32))
            gq = exMoe.enter_context(nc.sbuf_tensor("gq", [128, 4 * 16], F32))
            m1f = exMoe.enter_context(nc.sbuf_tensor("m1f", [128, 16 * 8], F32))
            m2f = exMoe.enter_context(nc.sbuf_tensor("m2f", [128, 16 * 8], F32))
            m1b = exMoe.enter_context(nc.sbuf_tensor("m1b", [128, 16 * 8], BF16))
            m2b = exMoe.enter_context(nc.sbuf_tensor("m2b", [128, 16 * 8], BF16))
            m1mb = exMoe.enter_context(nc.sbuf_tensor("m1mb", [128, 16 * 8], BF16))
            gidxf2 = exMoe.enter_context(nc.sbuf_tensor("gidxf2", [128, 32], F32))
            gscu = exMoe.enter_context(nc.sbuf_tensor("gscu", [128, 8], U32))
            iota8 = exMoe.enter_context(nc.sbuf_tensor("iota8", [128, 8], F32))
            gsc = exMoe.enter_context(nc.sbuf_tensor("gsc", [128, 24], F32))
            if True:
                q1c, g1c = gq[:, 0:16], gq[:, 16:32]
                q2c, g2c = gq[:, 32:48], gq[:, 48:64]
                ctx.inc(g.iota(iotaq[:], [[1, 4096]], channel_multiplier=0,
                               allow_small_or_imprecise_dtypes=True), "mio")
                ctx.inc(g.iota(iota8[:], [[1, 8]], channel_multiplier=0,
                               allow_small_or_imprecise_dtypes=True), "mio")
                g.wait_ge(ctx.S("dxn"), ctx.cnt["dxn"])
                exD = ctx.ex.enter_context(contextlib.ExitStack())
                xng = exD.enter_context(nc.sbuf_tensor("xng", [128, NT * 1024], BF16))
                exDsp = ctx.ex.enter_context(contextlib.ExitStack())
                xtf = exDsp.enter_context(nc.sbuf_tensor("xtf", [128, 2 * 1024], F32))
                xngT = exDsp.enter_context(nc.sbuf_tensor("xngT", [128, 8 * 2048], F32))
                rt_sb = exDsp.enter_context(nc.sbuf_tensor("rt_sb", [128, 64], F32))
                lg_sb = exDsp.enter_context(nc.sbuf_tensor("lg_sb", [8, 2048], F32))
                lgt = exDsp.enter_context(nc.sbuf_tensor("lgt", [128, 16 * 8], F32))
                exs = exDsp.enter_context(nc.sbuf_tensor("exs", [128, 16 * 8], F32))
                p1f = exDsp.enter_context(nc.sbuf_tensor("p1f", [128, 16 * 8], F32))
                p2f = exDsp.enter_context(nc.sbuf_tensor("p2f", [128, 16 * 8], F32))
                m1cb = exDsp.enter_context(nc.sbuf_tensor("m1cb", [128, 8], F32))
                Dh = exDsp.enter_context(nc.sbuf_tensor("Dh", [128, 8 * 4096], BF16))
                eist = exDsp.enter_context(nc.sbuf_tensor("eist", [128, 2 * 1024], BF16))
                psg = exDsp.enter_context(nc.psum_tensor("psg", [128, 512], F32))
                psgf = exDsp.enter_context(nc.psum_tensor("psgf", [128, 2048], F32))
                psd1 = psgf[:, 0:1024]
                psd2 = psgf[:, 1024:2048]
                if True:
                    ctx.inc(g.dma_start(out=sb3(rt_sb[:], 8, 8), in_=ap3(rt_d, 8, 8)), "mld", 16)
                    for k in range(NT):
                        sl = k % 2
                        if k >= 2:
                            ctx.w(g, "tr_v", ctx.cnt.get("tr_v", 0) - 8)
                            ctx.w(g, "xgc", ctx.cnt.get("xgc", 0) - 1)
                        ctx.inc(g.dma_start(out=sb3(xtf[:, cs(sl, 1024)], 1, 1024),
                                            in_=ap3(xnin, 1, 1024, off_elems=1024 * 128 * k)), "mxf", 16)
                        t.wait_ge(ctx.S("mxf"), ctx.cnt["mxf"])
                        v.wait_ge(ctx.S("mxf"), ctx.cnt["mxf"])
                        ctx.inc(v.tensor_copy(xng[:, cs(k, 1024)], xtf[:, cs(sl, 1024)]), "xgc")
                        for j in range(8):
                            ctx.S("tr_t"); ctx.S("tr_v")
                            jt = ctx.cnt["tr_t"]
                            sl2 = (jt % 2) * 512
                            if jt >= 2:
                                ctx.w(t, "tr_v", jt - 1)
                            ctx.inc(t.transpose(psgf[:, sl2:sl2 + 128],
                                                xtf[:, cs(sl, 1024)][:, cs(j, 128)], id_f[:]), "tr_t")
                            ctx.w(v, "tr_t", jt + 1)
                            ctx.inc(v.tensor_copy(xngT[:, 2048 * j + 128 * k:2048 * j + 128 * (k + 1)],
                                                  psgf[:, sl2:sl2 + 128]), "tr_v")
                    ctx.w(t, "tr_v")
                    t.wait_ge(ctx.S("mld"), ctx.cnt["mld"])
                    ins = None
                    for mp in range(8):
                        for c4 in range(4):
                            ins = t.matmul(psgf[0:8, cs(c4, 512)], rt_sb[:, 8 * mp:8 * (mp + 1)],
                                           xngT[:, 2048 * mp + 512 * c4:2048 * mp + 512 * (c4 + 1)],
                                           start=(mp == 0), stop=(mp == 7))
                    ctx.inc(ins, "rlt")
                    ctx.w(v, "rlt", ctx.cnt["rlt"])
                    ctx.inc(v.tensor_copy(lg_sb[0:8, :], psgf[0:8, 0:2048]), "rle")
                    ctx.w(t, "rle", ctx.cnt["rle"])
                    for tt in range(NT):
                        ctx.inc(t.transpose(psg[0:128, 0:8], lg_sb[0:8, cs(tt, 128)], id_f[0:8, 0:8]), "rtt")
                        ctx.w(v, "rtt", ctx.cnt["rtt"])
                        ctx.inc(v.tensor_copy(lgt[:, cs(tt, 8)], psg[0:128, 0:8]), "rtv")
                        ctx.w(t, "rtv", ctx.cnt["rtv"])
                    if "lgt" in dumps:
                        ctx.w(g, "rtv", ctx.cnt["rtv"])
                        ctx.inc(g.dma_start(out=bass.AP(dumps["lgt"].tensor, 0, [[8, 128], [128 * 8, 16], [1, 8]]),
                                            in_=sb3(lgt[:], 16, 8)), "dmp", 16)
                    for tt in range(NT):
                        L = lgt[:, cs(tt, 8)]
                        hsv(v.max(gsc[:, 0:8], L))
                        hsv(v.tensor_scalar(out=gsc[:, 8:9], in0=gsc[:, 0:1], scalar1=-1.0, scalar2=None, op0=OP.mult))
                        ctx.w(s, "mgh", ctx.cnt["mgh"])
                        ctx.inc(s.activation(out=exs[:, cs(tt, 8)], in_=L, func=AF.Exp, bias=gsc[:, 8:9]), "mge")
                        ctx.w(v, "mge")
                        EXT = exs[:, cs(tt, 8)]
                        hsv(v.reduce_sum(out=gsc[:, 9:10], in_=EXT, axis=AX.X))
                        hsv(v.reciprocal(out=gsc[:, 10:11], in_=gsc[:, 9:10]))
                        hsv(v.max(gsc[:, 12:20], EXT))
                        hsv(v.max_index(gscu[:, 0:8], gsc[:, 12:20], EXT))
                        hsv(v.tensor_copy(gidxf2[:, 2 * tt:2 * tt + 2], gscu[:, 0:2]))
                        hsv(v.tensor_tensor(out=g1c[:, tt:tt + 1], in0=gsc[:, 12:13], in1=gsc[:, 10:11], op=OP.mult))
                        hsv(v.tensor_tensor(out=g2c[:, tt:tt + 1], in0=gsc[:, 13:14], in1=gsc[:, 10:11], op=OP.mult))
                        hsv(v.tensor_scalar(out=m1f[:, cs(tt, 8)], in0=iota8[:], scalar1=gidxf2[:, 2 * tt:2 * tt + 1],
                                            scalar2=None, op0=OP.is_equal))
                        hsv(v.tensor_scalar(out=m2f[:, cs(tt, 8)], in0=iota8[:], scalar1=gidxf2[:, 2 * tt + 1:2 * tt + 2],
                                            scalar2=None, op0=OP.is_equal))
                        hsv(v.tensor_copy(m1b[:, cs(tt, 8)], m1f[:, cs(tt, 8)]))
                        hsv(v.tensor_copy(m2b[:, cs(tt, 8)], m2f[:, cs(tt, 8)]))
                    ctx.w(t, "mgh", ctx.cnt["mgh"])
                    for (mb, pf, nm) in ((m1b, p1f, "c1m"), (m2b, p2f, "c2m")):
                        for tt in range(NT):
                            ins = None
                            for tp in range(tt + 1):
                                ins = t.matmul(psg[0:128, 0:8],
                                               allon_bf[:] if tp < tt else lt_bf[:],
                                               mb[:, cs(tp, 8)], start=(tp == 0), stop=(tp == tt))
                            ctx.inc(ins, nm)
                            ctx.w(v, nm, ctx.cnt[nm])
                            ctx.inc(v.tensor_copy(pf[:, cs(tt, 8)], psg[0:128, 0:8]), nm + "e")
                            ctx.w(t, nm + "e", ctx.cnt[nm + "e"])
                    ctx.w(v, "c2me", ctx.cnt["c2me"])
                    for tt in range(NT):
                        hsv(v.tensor_scalar(out=gsc[:, 0:8], in0=p1f[:, cs(tt, 8)], scalar1=float(CAP),
                                            scalar2=None, op0=OP.is_lt))
                        hsv(v.tensor_tensor(out=m1f[:, cs(tt, 8)], in0=m1f[:, cs(tt, 8)], in1=gsc[:, 0:8], op=OP.mult))
                        hsv(v.tensor_copy(m1mb[:, cs(tt, 8)], m1f[:, cs(tt, 8)]))
                        hsv(v.tensor_tensor(out=gsc[:, 0:8], in0=p1f[:, cs(tt, 8)], in1=m1f[:, cs(tt, 8)], op=OP.mult))
                        hsv(v.reduce_sum(out=gsc[:, 8:9], in_=gsc[:, 0:8], axis=AX.X))
                        hsv(v.reduce_sum(out=gsc[:, 9:10], in_=m1f[:, cs(tt, 8)], axis=AX.X))
                        hsv(v.tensor_scalar(out=gsc[:, 10:11], in0=gidxf2[:, 2 * tt:2 * tt + 1], scalar1=512.0,
                                            scalar2=None, op0=OP.mult))
                        hsv(v.tensor_tensor(out=gsc[:, 10:11], in0=gsc[:, 10:11], in1=gsc[:, 8:9], op=OP.add))
                        hsv(v.tensor_scalar(out=gsc[:, 11:12], in0=gsc[:, 9:10], scalar1=-1e6,
                                            scalar2=1e6, op0=OP.mult, op1=OP.add))
                        hsv(v.tensor_tensor(out=q1c[:, tt:tt + 1], in0=gsc[:, 10:11], in1=gsc[:, 11:12], op=OP.add))
                        hsv(v.tensor_tensor(out=g1c[:, tt:tt + 1], in0=g1c[:, tt:tt + 1], in1=gsc[:, 9:10], op=OP.mult))
                    ctx.w(t, "mgh", ctx.cnt["mgh"])
                    ins = None
                    for tt in range(NT):
                        ins = t.matmul(psg[0:1, 0:8], onesc_bf[:], m1mb[:, cs(tt, 8)],
                                       start=(tt == 0), stop=(tt == NT - 1))
                    ctx.inc(ins, "m1c")
                    ctx.w(v, "m1c", ctx.cnt["m1c"])
                    ctx.inc(v.tensor_copy(gsc[0:1, 16:24][0:1, 0:8], psg[0:1, 0:8]), "m1ce")
                    ctx.w(t, "m1ce", ctx.cnt["m1ce"])
                    ctx.inc(t.matmul(psg[0:128, 8:16], onesr_f[0:1, :], gsc[0:1, 16:24][0:1, 0:8],
                                     start=True, stop=True), "m1cb")
                    ctx.w(v, "m1cb", ctx.cnt["m1cb"])
                    ctx.inc(v.tensor_copy(m1cb[:], psg[0:128, 8:16]), "m1cbe")
                    ctx.w(v, "m1cbe")
                    for tt in range(NT):
                        hsv(v.tensor_tensor(out=p2f[:, cs(tt, 8)], in0=p2f[:, cs(tt, 8)], in1=m1cb[:], op=OP.add))
                        hsv(v.tensor_scalar(out=gsc[:, 0:8], in0=p2f[:, cs(tt, 8)], scalar1=float(CAP),
                                            scalar2=None, op0=OP.is_lt))
                        hsv(v.tensor_tensor(out=m2f[:, cs(tt, 8)], in0=m2f[:, cs(tt, 8)], in1=gsc[:, 0:8], op=OP.mult))
                        hsv(v.tensor_tensor(out=gsc[:, 0:8], in0=p2f[:, cs(tt, 8)], in1=m2f[:, cs(tt, 8)], op=OP.mult))
                        hsv(v.reduce_sum(out=gsc[:, 8:9], in_=gsc[:, 0:8], axis=AX.X))
                        hsv(v.reduce_sum(out=gsc[:, 9:10], in_=m2f[:, cs(tt, 8)], axis=AX.X))
                        hsv(v.tensor_scalar(out=gsc[:, 10:11], in0=gidxf2[:, 2 * tt + 1:2 * tt + 2], scalar1=512.0,
                                            scalar2=None, op0=OP.mult))
                        hsv(v.tensor_tensor(out=gsc[:, 10:11], in0=gsc[:, 10:11], in1=gsc[:, 8:9], op=OP.add))
                        hsv(v.tensor_scalar(out=gsc[:, 11:12], in0=gsc[:, 9:10], scalar1=-1e6,
                                            scalar2=1e6, op0=OP.mult, op1=OP.add))
                        hsv(v.tensor_tensor(out=q2c[:, tt:tt + 1], in0=gsc[:, 10:11], in1=gsc[:, 11:12], op=OP.add))
                        hsv(v.tensor_tensor(out=g2c[:, tt:tt + 1], in0=g2c[:, tt:tt + 1], in1=gsc[:, 9:10], op=OP.mult))
                    if "gat" in dumps:
                        ctx.w(g, "mgh", ctx.cnt["mgh"])
                        for kk in range(4):
                            ctx.inc(g.dma_start(out=bass.AP(dumps["gat"].tensor, kk * 128 * 16, [[16, 128], [1, 16]]),
                                                in_=gq[:, 16 * kk:16 * (kk + 1)]), "dmp", 16)
                    for half in range(2):
                        for k8 in range(8):
                            tt = 8 * half + k8
                            if ctx.cnt.get("dsp", 0):
                                ctx.w(v, "dsp", ctx.cnt["dsp"])
                            hsv(v.tensor_scalar(out=Dh[:, cs(k8, 4096)], in0=iotaq[:], scalar1=q1c[:, tt:tt + 1],
                                                scalar2=None, op0=OP.is_equal), "mgh")
                            ctx.inc(v.scalar_tensor_tensor(out=Dh[:, cs(k8, 4096)], in0=iotaq[:],
                                                           scalar=q2c[:, tt:tt + 1], in1=Dh[:, cs(k8, 4096)],
                                                           op0=OP.is_equal, op1=OP.add), "dbd")
                            ctx.w(v, "dbd")
                        ctx.w(t, "dbd", ctx.cnt["dbd"])
                        for qt in range(32):
                            ps = psd1 if qt % 2 == 0 else psd2
                            nev = ctx.cnt.get("dse", 0)
                            if nev >= 2:
                                ctx.w(t, "dse", nev - 1)
                            ins = None
                            for k8 in range(8):
                                for c2 in range(2):
                                    ins = t.matmul(ps[:, cs(c2, 512)],
                                                   Dh[:, 4096 * k8 + 128 * qt:4096 * k8 + 128 * (qt + 1)],
                                                   xng[:, 1024 * (8 * half + k8) + 512 * c2:1024 * (8 * half + k8) + 512 * (c2 + 1)],
                                                   start=(k8 == 0), stop=(k8 == 7))
                            ctx.inc(ins, "dsp")
                            ctx.w(v, "dsp", ctx.cnt["dsp"])
                            nst = ctx.cnt.get("eiw", 0)
                            if nst >= 32:
                                ctx.w(v, "eiw", nst - 16)
                            ctx.inc(v.tensor_copy(eist[:, cs(qt % 2, 1024)], ps[:, 0:1024]), "dse")
                            ctx.w(g, "dse", ctx.cnt["dse"])
                            if half == 0:
                                ctx.inc(g.dma_start(out=eiin[128 * qt:128 * (qt + 1), :],
                                                    in_=eist[:, cs(qt % 2, 1024)]), "eiw", 16)
                            else:
                                ctx.inc(g.dma_start(out=eiin[128 * qt:128 * (qt + 1), :],
                                                    in_=eist[:, cs(qt % 2, 1024)], accum_op=OP.add), "eiw", 16)
                    g.wait_ge(ctx.S("eiw"), ctx.cnt["eiw"])
                    ctx.inc(g.collective_compute("AllGather", OP.bypass, replica_groups=[list(range(NCORE))],
                                                 ins=[eiin], outs=[eiag]), "eic")
                    if "ei" in dumps:
                        ctx.w(g, "eic")
                        ctx.inc(g.dma_start(out=dumps["ei"], in_=eiin), "dmp", 16)
                    emit_full_barrier(ctx)
                    exDsp.close()
                exD.close()

                NHC = 43
                exExp = ctx.ex.enter_context(contextlib.ExitStack())
                XeT = exExp.enter_context(nc.sbuf_tensor("XeT", [128, 2 * 8 * 512], BF16))
                hsT = exExp.enter_context(nc.sbuf_tensor("hsT", [128, NHC * 512], BF16))
                w01 = exExp.enter_context(nc.sbuf_tensor("w01", [128, 4 * 1024], BF16))
                wos = exExp.enter_context(nc.sbuf_tensor("wos", [128, 43 * 1024], BF16))
                gsx = exExp.enter_context(nc.sbuf_tensor("gsx", [128, 5 * 512], F32))
                eos = exExp.enter_context(nc.sbuf_tensor("eos", [128, 2 * 1024], BF16))
                ph0 = exExp.enter_context(nc.psum_tensor("ph0", [128, 1024], F32))
                ph1 = exExp.enter_context(nc.psum_tensor("ph1", [128, 1024], F32))
                pso1 = exExp.enter_context(nc.psum_tensor("pso1", [128, 1024], F32))
                pso2 = exExp.enter_context(nc.psum_tensor("pso2", [128, 1024], F32))
                if True:
                    ctx.w(g, "eic")
                    with g.register("ebase") as ebase:
                        pid = g.partition_id()
                        g.reg_mov(ebase, pid)
                        g.reg_mul(ebase, ebase, 512 * 1024)
                        src = bass.AP(eiag.tensor, ebase,
                                      [[4096 * 1024, NCORE], [16384, 32], [1, 16384]])
                        dst = bass.AP(eloc.tensor, 0,
                                      [[512 * 1024, NCORE], [16384, 32], [1, 16384]])
                        ctx.inc(g.dma_start(out=dst, in_=src), "elc", 16)
                    g.wait_ge(ctx.S("elc"), ctx.cnt["elc"])
                    y.wait_ge(ctx.S("elc"), ctx.cnt["elc"])
                    for hc in range(NHC):
                        hw = 88 if hc == NHC - 1 else 128
                        ctx.inc(g.dma_start(out=wos[:, cs(hc, 1024)][0:hw, :] if hw < 128 else wos[:, cs(hc, 1024)],
                                            in_=wo_d[128 * hc:128 * hc + hw, :]), "wol2", 16)
                    if "elo" in dumps:
                        with nc.sbuf_tensor("elstg", [128, 1024], BF16) as elstg:
                            for j in range(32):
                                ctx.w(g, "dmp", ctx.cnt.get("dmp", 0))
                                ctx.inc(g.dma_start(out=elstg[:], in_=eloc[128 * j:128 * (j + 1), :]), "dmp", 16)
                                g.wait_ge(ctx.S("dmp"), ctx.cnt["dmp"])
                                ctx.inc(g.dma_start(out=dumps["elo"][128 * j:128 * (j + 1), :], in_=elstg[:]), "dmp", 16)
                            g.wait_ge(ctx.S("dmp"), ctx.cnt["dmp"])
                    for sc in range(8):
                        ssl = sc % 2
                        if sc >= 2:
                            ctx.w(y, "h1m", ctx.cnt.get("h1m", 0) - NHC)
                        for sb in range(4):
                            for mb in range(8):
                                ctx.inc(y.dma_start_transpose(
                                    XeT[:, 4096 * ssl + 512 * mb + 128 * sb:4096 * ssl + 512 * mb + 128 * (sb + 1)],
                                    eloc[512 * sc + 128 * sb:512 * sc + 128 * (sb + 1), 128 * mb:128 * (mb + 1)]),
                                    "xet", 16)
                        t.wait_ge(ctx.S("xet"), ctx.cnt["xet"])
                        for hc in range(NHC):
                            hw = 88 if hc == NHC - 1 else 128
                            wsl = hc % 2
                            nwl = ctx.cnt.get("wst2", 0)
                            if nwl >= 64:
                                ctx.w(g, "h1m", ctx.cnt.get("h1m", 0) - 1)
                            w0src = bass.AP(w0_d.tensor, 128 * hc, [[HMOE, 128], [128 * HMOE, 8], [1, hw]])
                            w1src = bass.AP(w1_d.tensor, 128 * hc, [[HMOE, 128], [128 * HMOE, 8], [1, hw]])
                            ctx.inc(g.dma_start(out=sb3(w01[:, cs(2 * wsl, 1024)], 8, 128)[:, :, 0:hw], in_=w0src), "wst2", 16)
                            ctx.inc(g.dma_start(out=sb3(w01[:, cs(2 * wsl + 1, 1024)], 8, 128)[:, :, 0:hw], in_=w1src), "wst2", 16)
                            t.wait_ge(ctx.S("wst2"), ctx.cnt["wst2"])
                            phA = ph0 if hc % 2 == 0 else ph1
                            nsg = ctx.cnt.get("sgx", 0)
                            if nsg >= 2:
                                ctx.w(t, "sgx", nsg - 1)
                            ins0, ins1 = None, None
                            for mp in range(8):
                                ins0 = t.matmul(phA[0:hw, 0:512],
                                                w01[:, 1024 * (2 * wsl) + 128 * mp:1024 * (2 * wsl) + 128 * mp + hw],
                                                XeT[:, 4096 * (sc % 2) + 512 * mp:4096 * (sc % 2) + 512 * (mp + 1)],
                                                start=(mp == 0), stop=(mp == 7))
                            ctx.inc(ins0, "h0m")
                            for mp in range(8):
                                ins1 = t.matmul(phA[0:hw, 512:1024],
                                                w01[:, 1024 * (2 * wsl + 1) + 128 * mp:1024 * (2 * wsl + 1) + 128 * mp + hw],
                                                XeT[:, 4096 * (sc % 2) + 512 * mp:4096 * (sc % 2) + 512 * (mp + 1)],
                                                start=(mp == 0), stop=(mp == 7))
                            ctx.inc(ins1, "h1m")
                            ctx.w(s, "h1m", ctx.cnt["h1m"])
                            ctx.inc(s.activation(out=gsx[0:hw, 0:512], in_=phA[0:hw, 0:512], func=AF.Square), "gsq")
                            ctx.w(v, "gsq")
                            hsv(v.tensor_scalar(out=gsx[0:hw, 512:1024], in0=gsx[0:hw, 0:512],
                                                scalar1=0.044715 * 0.7978845608028654, scalar2=0.7978845608028654,
                                                op0=OP.mult, op1=OP.add))
                            hsv(v.tensor_tensor(out=gsx[0:hw, 1024:1536], in0=gsx[0:hw, 512:1024],
                                                in1=phA[0:hw, 0:512], op=OP.mult))
                            ctx.inc(v.tensor_copy(ctx.barscr[:, 6:7], ctx.barscr[:, 7:8]), "gs2")
                            ctx.w(s, "gs2", ctx.cnt["gs2"])
                            ctx.inc(s.activation(out=gsx[0:hw, 1536:2048], in_=gsx[0:hw, 1024:1536], func=AF.Tanh), "gth")
                            ctx.w(v, "gth")
                            hsv(v.tensor_scalar(out=gsx[0:hw, 1536:2048], in0=gsx[0:hw, 1536:2048],
                                                scalar1=1.0, scalar2=0.5, op0=OP.add, op1=OP.mult))
                            hsv(v.tensor_tensor(out=gsx[0:hw, 2048:2560], in0=gsx[0:hw, 1536:2048],
                                                in1=phA[0:hw, 0:512], op=OP.mult))
                            ctx.inc(v.tensor_tensor(out=hsT[0:hw, cs(hc, 512)], in0=gsx[0:hw, 2048:2560],
                                                    in1=phA[0:hw, 512:1024], op=OP.mult), "sgx")
                            ctx.w(v, "sgx")
                        t.wait_ge(ctx.S("wol2"), ctx.cnt["wol2"])
                        ctx.w(t, "sgx", ctx.cnt["sgx"])
                        for st4 in range(4):
                            pso = pso1 if st4 % 2 == 0 else pso2
                            nev = ctx.cnt.get("eoe", 0)
                            if nev >= 2:
                                ctx.w(t, "eoe", nev - 1)
                            ins = None
                            for hc in range(NHC):
                                hw = 88 if hc == NHC - 1 else 128
                                for c2 in range(2):
                                    ins = t.matmul(pso[:, cs(c2, 512)],
                                                   hsT[0:hw, 512 * hc + 128 * st4:512 * hc + 128 * (st4 + 1)],
                                                   wos[0:hw, 1024 * hc + 512 * c2:1024 * hc + 512 * (c2 + 1)],
                                                   start=(hc == 0), stop=(hc == NHC - 1))
                            ctx.inc(ins, "eom")
                            ctx.w(v, "eom", ctx.cnt["eom"])
                            nei = ctx.cnt.get("eow", 0)
                            if nei >= 32:
                                ctx.w(v, "eow", nei - 16)
                            ctx.inc(v.tensor_copy(eos[:, cs(st4 % 2, 1024)], pso[:, 0:1024]), "eoe")
                            ctx.w(g, "eoe", ctx.cnt["eoe"])
                            ctx.inc(g.dma_start(out=eoin[512 * sc + 128 * st4:512 * sc + 128 * (st4 + 1), :],
                                                in_=eos[:, cs(st4 % 2, 1024)]), "eow", 16)
                    g.wait_ge(ctx.S("eow"), ctx.cnt["eow"])
                    ctx.inc(g.collective_compute("AllGather", OP.bypass, replica_groups=[list(range(NCORE))],
                                                 ins=[eoin], outs=[eoag]), "eoc")
                    if "eo" in dumps:
                        ctx.w(g, "eoc")
                        with nc.sbuf_tensor("eostg", [128, 1024], BF16) as eostg:
                            for j in range(32):
                                ctx.w(g, "dmp", ctx.cnt.get("dmp", 0))
                                ctx.inc(g.dma_start(out=eostg[:], in_=eoin[128 * j:128 * (j + 1), :]), "dmp", 16)
                                g.wait_ge(ctx.S("dmp"), ctx.cnt["dmp"])
                                ctx.inc(g.dma_start(out=dumps["eo"][128 * j:128 * (j + 1), :], in_=eostg[:]), "dmp", 16)
                            g.wait_ge(ctx.S("dmp"), ctx.cnt["dmp"])
                    emit_full_barrier(ctx)

                    exExp.close()
                exCmb = ctx.ex.enter_context(contextlib.ExitStack())
                eosb = exCmb.enter_context(nc.sbuf_tensor("eosb", [128, 32 * 1024], BF16))
                cmbr = exCmb.enter_context(nc.sbuf_tensor("cmb", [128, 4096], BF16))
                cmbT = exCmb.enter_context(nc.sbuf_tensor("cmbT", [128, 4096], BF16))
                ffst = exCmb.enter_context(nc.sbuf_tensor("ffst", [128, 2 * 1024], F32))
                psct = exCmb.enter_context(nc.psum_tensor("psct", [128, 2048], BF16))
                psc1 = exCmb.enter_context(nc.psum_tensor("psc1", [128, 1024], F32))
                psc2 = exCmb.enter_context(nc.psum_tensor("psc2", [128, 1024], F32))
                if True:
                    ctx.w(g, "eoc")
                    with g.register("obase") as obase:
                        pid = g.partition_id()
                        g.reg_mov(obase, pid)
                        g.reg_mul(obase, obase, 512 * 1024)
                        src = bass.AP(eoag.tensor, obase,
                                      [[4096 * 1024, NCORE], [16384, 32], [1, 16384]])
                        dst = bass.AP(eocd.tensor, 0,
                                      [[512 * 1024, NCORE], [16384, 32], [1, 16384]])
                        ctx.inc(g.dma_start(out=dst, in_=src), "eor", 16)
                    g.wait_ge(ctx.S("eor"), ctx.cnt["eor"])
                    for ee in range(NCORE):
                        src = bass.AP(eocd.tensor, 512 * 1024 * ee,
                                      [[1024, 128], [128 * 1024, 4], [1, 1024]])
                        ctx.inc(g.dma_start(out=sb3(eosb[:, 1024 * 4 * ee:1024 * 4 * (ee + 1)], 4, 1024),
                                            in_=src), "eor", 16)
                    t.wait_ge(ctx.S("eor"), ctx.cnt["eor"])
                    for st in range(NT):
                        nts = ctx.cnt.get("ctt", 0)
                        if nts:
                            ctx.w(v, "ctt", nts)
                        hsv(v.tensor_scalar(out=cmbr[:, 0:4096], in0=iotaq[:], scalar1=q1c[:, st:st + 1],
                                            scalar2=g1c[:, st:st + 1], op0=OP.is_equal, op1=OP.mult))
                        ctx.inc(v.tensor_scalar(out=cmbT[:, 0:4096], in0=iotaq[:], scalar1=q2c[:, st:st + 1],
                                                scalar2=g2c[:, st:st + 1], op0=OP.is_equal, op1=OP.mult), "cb2")
                        ctx.w(v, "cb2")
                        ctx.inc(v.tensor_tensor(out=cmbr[:, 0:4096], in0=cmbr[:, 0:4096], in1=cmbT[:, 0:4096],
                                                op=OP.add), "cbd")
                        ctx.w(t, "cbd", ctx.cnt["cbd"])
                        for qt in range(32):
                            jt = ctx.cnt.get("ctr", 0)
                            sl2 = (jt % 2) * 1024
                            if jt >= 2:
                                ctx.w(t, "ctt", jt - 1)
                            ctx.inc(t.transpose(psct[:, sl2:sl2 + 128], cmbr[:, cs(qt, 128)], id_bf[:]), "ctr")
                            ctx.w(v, "ctr", jt + 1)
                            ctx.inc(v.tensor_copy(cmbT[:, cs(qt, 128)], psct[:, sl2:sl2 + 128]), "ctt")
                        ctx.w(t, "ctt", ctx.cnt["ctt"])
                        ps = psc1 if st % 2 == 0 else psc2
                        nev = ctx.cnt.get("ffe", 0)
                        if nev >= 2:
                            ctx.w(t, "ffe", nev - 1)
                        ins = None
                        for qt in range(32):
                            for c2 in range(2):
                                ins = t.matmul(ps[:, cs(c2, 512)], cmbT[:, cs(qt, 128)],
                                               eosb[:, 1024 * qt + 512 * c2:1024 * qt + 512 * (c2 + 1)],
                                               start=(qt == 0), stop=(qt == 31))
                        ctx.inc(ins, "ffm")
                        ctx.w(v, "ffm", ctx.cnt["ffm"])
                        nfw = ctx.cnt.get("ffw", 0)
                        if nfw >= 32:
                            ctx.w(v, "ffw", nfw - 16)
                        ctx.inc(v.tensor_copy(ffst[:, cs(st % 2, 1024)], ps[:, 0:1024]), "ffe")
                        ctx.w(g, "ffe", ctx.cnt["ffe"])
                        ctx.inc(g.dma_start(out=ap3(ffout, 1, 1024, off_elems=1024 * 128 * st),
                                            in_=ffst[:, cs(st % 2, 1024)].unsqueeze(1)), "ffw", 16)
                    g.wait_ge(ctx.S("ffw"), ctx.cnt["ffw"])
                    ctx.inc(g.nop(), "ffc")
                    if "ffo" in dumps:
                        ctx.inc(g.dma_start(out=dumps["ffo"], in_=ffout), "dmp", 16)
                    emit_full_barrier(ctx)

                    exCmb.close()
            exMoe.close()
        with nc.sbuf_tensor("fr", [128, 3 * 1024], F32) as fr, \
             nc.sbuf_tensor("fo", [128, 2 * 1024], F32) as fo:
            ctx.w(g, "ffc")
            for tc in range(NT):
                if tc >= 1:
                    ctx.w(g, "fadd", ctx.cnt.get("fadd", 0))
                if tc >= 2:
                    ctx.w(g, "fst", ctx.cnt.get("fst", 0) - 16)
                ctx.inc(g.dma_start(out=sb3(fr[:, 0:1024], 1, 1024),
                                    in_=ap3(x1_d, 1, 1024, off_elems=1024 * 128 * tc)), "fld", 16)
                ctx.inc(g.dma_start(out=sb3(fr[:, 1024:2048], 1, 1024),
                                    in_=ap3(y1d, 1, 1024, off_elems=1024 * 128 * tc)), "fld", 16)
                ctx.inc(g.dma_start(out=sb3(fr[:, 2048:3072], 1, 1024),
                                    in_=ap3(ffout, 1, 1024, off_elems=1024 * 128 * tc)), "fld", 16)
                v.wait_ge(ctx.S("fld"), ctx.cnt["fld"])
                sl = tc % 2
                ctx.inc(v.tensor_tensor(out=fo[:, cs(sl, 1024)], in0=fr[:, 0:1024],
                                        in1=fr[:, 1024:2048], op=OP.add), "fadd")
                ctx.w(v, "fadd")
                ctx.inc(v.tensor_tensor(out=fo[:, cs(sl, 1024)], in0=fo[:, cs(sl, 1024)],
                                        in1=fr[:, 2048:3072], op=OP.add), "fadd")
                ctx.w(g, "fadd", ctx.cnt["fadd"])
                ctx.inc(g.dma_start(out=ap3(out_d, 1, 1024, off_elems=1024 * 128 * tc),
                                    in_=fo[:, cs(sl, 1024)].unsqueeze(1)), "fst", 16)
            g.wait_ge(ctx.S("fst"), ctx.cnt["fst"])
            if dumps:
                g.wait_ge(ctx.S("dmp"), ctx.cnt.get("dmp", 0))
    return nc

def build_fused():
    nc = bass.Bass("TRN2", target_bir_lowering=False, debug=False)
    with contextlib.ExitStack() as ex0:
        ctx = Ctx(nc, ex0)
        F = dict(nc=nc, ctx=ctx)
        build_l0(F=F)
        build_l1(F=F)
    return nc


def make_in_maps_fused(inputs):
    m0 = make_in_maps_l0(inputs)
    m1 = make_in_maps_l1(inputs, None)
    return [dict(m0[c], **m1[c]) for c in range(NCORE)]


def kernel(**inputs):
    maps = make_in_maps_fused(inputs)
    if "f" not in _BASS_CACHE:
        _BASS_CACHE["f"] = build_fused()
    res = run_bass_kernel_spmd(_BASS_CACHE["f"], maps, core_ids=list(range(NCORE)))
    return assemble_contig(res.results, "out")


def kernel_2launch(**inputs):
    maps0 = make_in_maps_l0(inputs)
    if "l0" not in _BASS_CACHE:
        _BASS_CACHE["l0"] = build_l0()
    res0 = run_bass_kernel_spmd(_BASS_CACHE["l0"], maps0, core_ids=list(range(NCORE)))
    x1_list = [np.asarray(r["x1"]) for r in res0.results]
    maps1 = make_in_maps_l1(inputs, x1_list)
    if "l1" not in _BASS_CACHE:
        _BASS_CACHE["l1"] = build_l1()
    res1 = run_bass_kernel_spmd(_BASS_CACHE["l1"], maps1, core_ids=list(range(NCORE)))
    return assemble_contig(res1.results, "out")


# version 04
# speedup vs baseline: 1.0228x; 1.0025x over previous
import contextlib
import numpy as np
import ml_dtypes
import concourse.bass as bass
import concourse.mybir as mybir
from concourse.bass_utils import run_bass_kernel_spmd

F32 = mybir.dt.float32
BF16 = mybir.dt.bfloat16
U32 = mybir.dt.uint32
AF = mybir.ActivationFunctionType
OP = mybir.AluOpType
AX = mybir.AxisListType

NCORE = 8
B, S, D = 2, 8192, 1024
NH, DH, NHP = 16, 64, 8
TPC, NT, GT = 2048, 16, 64
FF0 = 2816
E, CAP, HMOE = 8, 512, 5464
EPS = 1e-5
SQD = 1.0 / 8.0
SC1 = 1024 ** -0.5

_BASS_CACHE = {}


def _pos_tiles(p):
    return [4 * t + p for t in range(NT)]


def make_in_maps_l0(inputs):
    x = np.asarray(inputs["L0_attn_x"], np.float32)
    wq = np.ascontiguousarray(np.asarray(inputs["L0_attn_W_q"], np.float32))
    wk = np.ascontiguousarray(np.asarray(inputs["L0_attn_W_k"], np.float32))
    wv = np.ascontiguousarray(np.asarray(inputs["L0_attn_W_v"], np.float32))
    wo = np.ascontiguousarray(np.asarray(inputs["L0_attn_W_o"], np.float32))
    wg = np.asarray(inputs["L0_ffn_w_gate"], np.float32)
    wu = np.asarray(inputs["L0_ffn_w_up"], np.float32)
    wd = np.asarray(inputs["L0_ffn_w_down"], np.float32)
    ident = np.eye(128, dtype=np.float32)
    wgp = np.zeros((3072, 1024), np.float32); wgp[:2816] = wg
    wup = np.zeros((3072, 1024), np.float32); wup[:2816] = wu
    maps = []
    for c in range(NCORE):
        b, p = c // 4, c % 4
        xt = x[b].reshape(GT, 128, D)[_pos_tiles(p)].reshape(TPC, D).copy()
        bm = np.zeros((4, 128, 128), np.float32)
        for r in range(4):
            if r < p:
                bm[r] = 1.0
            elif r == p:
                bm[r] = (np.arange(128)[:, None] <= np.arange(128)[None, :]).astype(np.float32)
        maps.append(dict(
            x0=xt, wq=wq, wk=wk, wv=wv, wo=wo,
            wgs=np.ascontiguousarray(wgp[384 * c:384 * (c + 1)]),
            wus=np.ascontiguousarray(wup[384 * c:384 * (c + 1)]),
            wds=np.ascontiguousarray(wd[128 * c:128 * (c + 1)]),
            ident=ident, bmask=bm))
    return maps


def assemble_output(res, key):
    out = np.zeros((B, S, D), np.float32)
    for c in range(NCORE):
        b, p = c // 4, c % 4
        y = np.asarray(res[c][key]).reshape(NT, 128, D)
        out[b].reshape(GT, 128, D)[_pos_tiles(p)] = y
    return out


class Ctx:
    def __init__(self, nc, ex):
        self.nc = nc
        self.ex = ex
        self.sem_ex = ex
        self.sem = {}
        self._cnt = {}
        self.alias = {}

    def canon(self, name):
        return self.alias.get(name, name)

    @property
    def cnt(self):
        return _CntView(self)

    def S(self, name):
        name = self.canon(name)
        if name not in self.sem:
            self.sem[name] = self.sem_ex.enter_context(self.nc.semaphore(name))
            self._cnt[name] = 0
        return self.sem[name]

    def inc(self, instr, name, n=1):
        cn = self.canon(name)
        instr.then_inc(self.S(name), n)
        self._cnt[cn] += n
        return self._cnt[cn]

    def w(self, eng, name, val=None):
        cn = self.canon(name)
        self.S(name)
        eng.wait_ge(self.sem[cn], self._cnt[cn] if val is None else val)

    def sb(self, name, shape, dt):
        return self.ex.enter_context(self.nc.sbuf_tensor(name, shape, dt))

    def ps(self, name, shape, dt=F32):
        return self.ex.enter_context(self.nc.psum_tensor(name, shape, dt))


class _CntView:
    def __init__(self, ctx):
        self._ctx = ctx

    def __getitem__(self, k):
        return self._ctx._cnt[self._ctx.canon(k)]

    def __contains__(self, k):
        return self._ctx.canon(k) in self._ctx._cnt

    def get(self, k, d=None):
        return self._ctx._cnt.get(self._ctx.canon(k), d)

def cs(t, w):
    return slice(t * w, (t + 1) * w)


def emit_ln_tile(ctx, v, s, src_ap, xnf_ap, xnb_ap, st, t):
    v.tensor_scalar(out=st["scr"][:, 0:1024], in0=src_ap, scalar1=1.0, scalar2=None,
                    op0=OP.mult, op1=OP.add, accum_out=st["sumx"][:, t:t + 1])
    s.activation(out=st["scr2"][:, 0:1024], in_=src_ap, func=AF.Square,
                 accum_out=st["sumsq"][:, t:t + 1])


def emit_ln_stats(ctx, v, s, st, nt=NT):
    iv = 1.0 / 1024.0
    v.tensor_scalar(out=st["m"][:, 0:nt], in0=st["sumx"][:, 0:nt], scalar1=iv, scalar2=None, op0=OP.mult)
    v.tensor_scalar(out=st["msq"][:, 0:nt], in0=st["sumsq"][:, 0:nt], scalar1=iv, scalar2=None, op0=OP.mult)
    v.tensor_tensor(out=st["var"][:, 0:nt], in0=st["m"][:, 0:nt], in1=st["m"][:, 0:nt], op=OP.mult)
    v.tensor_tensor(out=st["var"][:, 0:nt], in0=st["msq"][:, 0:nt], in1=st["var"][:, 0:nt], op=OP.subtract)
    s.activation(out=st["lnv"][:, 0:nt], in_=st["var"][:, 0:nt], func=AF.Ln, bias=EPS)
    s.activation(out=st["rstd"][:, 0:nt], in_=st["lnv"][:, 0:nt], func=AF.Exp, scale=-0.5)
    v.scalar_tensor_tensor(out=st["mrn"][:, 0:nt], in0=st["m"][:, 0:nt], scalar=-1.0,
                           in1=st["rstd"][:, 0:nt], op0=OP.mult, op1=OP.mult)


def emit_fp4_tile(ctx, v, s, src_ap, out_ap, q, width, parts=128):
    nb = width // 16
    P = slice(0, parts)
    src3 = src_ap.rearrange("p (b s) -> p b s", s=16)

    def hs(instr):
        ctx.inc(instr, "fqh")
        ctx.w(v, "fqh")

    hs(v.tensor_reduce(out=q["blk"][P, 0:nb], in_=src3, axis=AX.X, op=OP.max,
                       apply_absolute_value=True))
    hs(v.tensor_scalar(out=q["blk"][P, 0:nb], in0=q["blk"][P, 0:nb], scalar1=1.0 / 6.0,
                       scalar2=1e-12, op0=OP.mult, op1=OP.max))
    hs(v.reciprocal(out=q["rs"][P, 0:nb], in_=q["blk"][P, 0:nb]))
    rse = q["rs"][P, 0:nb].unsqueeze(-1).broadcast_to([parts, nb, 16])
    ctx.inc(v.tensor_tensor(out=q["t"][P, 0:width], in0=src3, in1=rse, op=OP.mult), "fq_v")
    ctx.w(s, "fq_v")
    s.activation(out=q["u"][P, 0:width], in_=q["t"][P, 0:width], func=AF.Abs)
    ctx.inc(s.activation(out=q["sg"][P, 0:width], in_=q["t"][P, 0:width], func=AF.Sign), "fq_s")
    ctx.w(v, "fq_s")
    u = q["u"][P, 0:width]
    hs(v.tensor_scalar(out=q["a"][P, 0:width], in0=u, scalar1=0.25, scalar2=None, op0=OP.is_gt))
    for thr in (0.75, 1.25, 1.75):
        hs(v.scalar_tensor_tensor(out=q["a"][P, 0:width], in0=u, scalar=thr,
                                  in1=q["a"][P, 0:width], op0=OP.is_gt, op1=OP.add))
    hs(v.tensor_scalar(out=q["acc"][P, 0:width], in0=u, scalar1=2.5, scalar2=None, op0=OP.is_gt))
    hs(v.scalar_tensor_tensor(out=q["acc"][P, 0:width], in0=u, scalar=3.5,
                              in1=q["acc"][P, 0:width], op0=OP.is_gt, op1=OP.add))
    hs(v.scalar_tensor_tensor(out=q["acc"][P, 0:width], in0=q["a"][P, 0:width], scalar=0.5,
                              in1=q["acc"][P, 0:width], op0=OP.mult, op1=OP.add))
    for _ in range(2):
        hs(v.scalar_tensor_tensor(out=q["acc"][P, 0:width], in0=u, scalar=5.0,
                                  in1=q["acc"][P, 0:width], op0=OP.is_gt, op1=OP.add))
    hs(v.tensor_tensor(out=q["acc"][P, 0:width], in0=q["acc"][P, 0:width],
                       in1=q["sg"][P, 0:width], op=OP.mult))
    sce = q["blk"][P, 0:nb].unsqueeze(-1).broadcast_to([parts, nb, 16])
    return v.tensor_tensor(out=out_ap, in0=q["acc"][P, 0:width].rearrange("p (b s) -> p b s", s=16),
                           in1=sce, op=OP.mult)


def emit_t128(ctx, src_ap, dst_ap, pst, id_bf):
    nc = ctx.nc
    t, v = nc.tensor, nc.vector
    if not isinstance(id_bf, bass.AP):
        id_bf = id_bf[:]
    ctx.S("tr_t"); ctx.S("tr_v")
    j = ctx.cnt["tr_t"]
    slot = (j % 2) * 1024
    if j >= 2:
        ctx.w(t, "tr_v", j - 1)
    ctx.inc(t.transpose(pst[:, slot:slot + 128], src_ap, id_bf), "tr_t")
    ctx.w(v, "tr_t", j + 1)
    ctx.inc(v.tensor_copy(dst_ap, pst[:, slot:slot + 128]), "tr_v")


def emit_bulk_transpose(ctx, src, dst, ntile, pst, id_bf, src_stride=1024, dst_stride=2048,
                        dst_off=0, src_off=0, nptile=8):
    for tt in range(ntile):
        for j in range(nptile):
            emit_t128(ctx, src[:, src_off + tt * src_stride + 128 * j: src_off + tt * src_stride + 128 * j + 128],
                      dst[:, dst_off + j * dst_stride + 128 * tt: dst_off + j * dst_stride + 128 * tt + 128],
                      pst, id_bf)




def ap3(dram_ap, ngrp, width, off_elems=0):
    return bass.AP(dram_ap.tensor, dram_ap.offset + off_elems,
                   [[width, 128], [128 * width, ngrp], [1, width]])


def sb3(sbuf_ap, ngrp, width):
    return sbuf_ap.rearrange("p (g w) -> p g w", w=width)

def emit_full_barrier(ctx):
    nc = ctx.nc
    bs = ctx.barscr
    s = ctx.S("bar")
    ctx.inc(nc.gpsimd.tensor_copy(bs[:, 0:1], bs[:, 1:2]), "bar")
    ctx.inc(nc.sync.nop(), "bar")
    ctx.inc(nc.tensor.nop(), "bar")
    ctx.inc(nc.vector.tensor_copy(bs[:, 2:3], bs[:, 3:4]), "bar")
    ctx.inc(nc.scalar.activation(out=bs[:, 4:5], in_=bs[:, 5:6], func=AF.Copy), "bar")
    tgt = ctx.cnt["bar"]
    for e in [nc.gpsimd, nc.sync, nc.tensor, nc.vector, nc.scalar]:
        e.wait_ge(s, tgt)


def emit_proj(ctx, w_sb, xnT, psA, psB, evac, sem, nhp=8):
    nc = ctx.nc
    t = nc.tensor
    tb = ctx.cnt.get(sem + "_t", 0)
    for hp in range(nhp):
        ps = psA if hp % 2 == 0 else psB
        ev = ctx.cnt.get(sem + "_v", 0)
        if ev >= 2:
            ctx.w(t, sem + "_v", ev - 1)
        ins = None
        for mp in range(8):
            for ch in range(4):
                ins = t.matmul(ps[:, cs(ch, 512)],
                               w_sb[:, 1024 * mp + 128 * hp:1024 * mp + 128 * hp + 128],
                               xnT[:, 2048 * mp + 512 * ch:2048 * mp + 512 * (ch + 1)],
                               start=(mp == 0), stop=(mp == 7))
        ctx.inc(ins, sem + "_t")
        evac(hp, ps)


def l0_granules(qh):
    out = []
    for kt in range(32 * qh + 32):
        tmin = kt // 4
        qfrom = max(0, 128 * (tmin - 8 * qh))
        off, first = qfrom, True
        while off < 1024:
            wdt = min(512, 1024 - off)
            out.append((kt, off, wdt, first and (8 * qh <= tmin)))
            off += wdt
            first = False
    return out


def build_l0(dump=(), phases=9, p1sub=9, p4sub=127, F=None):
    nc = F["nc"] if F is not None else bass.Bass("TRN2", target_bir_lowering=False, debug=False)
    g, y, t, v, s = nc.gpsimd, nc.sync, nc.tensor, nc.vector, nc.scalar
    x0 = nc.dram_tensor("x0", [TPC, D], F32, kind="ExternalInput").ap()
    wq_d = nc.dram_tensor("wq", [D, D], F32, kind="ExternalInput").ap()
    wk_d = nc.dram_tensor("wk", [D, D], F32, kind="ExternalInput").ap()
    wv_d = nc.dram_tensor("wv", [D, D], F32, kind="ExternalInput").ap()
    wo_d = nc.dram_tensor("wo", [D, D], F32, kind="ExternalInput").ap()
    wgs_d = nc.dram_tensor("wgs", [384, 1024], F32, kind="ExternalInput").ap()
    wus_d = nc.dram_tensor("wus", [384, 1024], F32, kind="ExternalInput").ap()
    wds_d = nc.dram_tensor("wds", [128, 2816], F32, kind="ExternalInput").ap()
    ident_d = nc.dram_tensor("ident", [128, 128], F32, kind="ExternalInput").ap()
    bmask_d = nc.dram_tensor("bmask", [4, 128, 128], F32, kind="ExternalInput").ap()
    x1_d = (nc.dram_tensor("x1", [TPC, D], F32).ap() if F is not None
            else nc.dram_tensor("x1", [TPC, D], F32, kind="ExternalOutput").ap())
    dumps = {}
    _dsh = {"xnf": ([TPC, D], F32), "xqt": ([128, 8 * TPC], F32), "qt": ([128, NHP * TPC], F32),
            "kv": ([2048, 2048], F32), "aot": ([128, 8 * TPC], F32), "wqa": ([8960, 1024], F32),
            "ffg": ([TPC, 1408], F32), "ffh": ([TPC, 1408], F32), "fft": ([TPC, 1408], F32),
            "ffw": ([128, 11264], F32)}
    for name in dump:
        dumps[name] = nc.dram_tensor("d_" + name, _dsh[name][0], _dsh[name][1],
                                     kind="ExternalOutput").ap()

    kvin = nc.dram_tensor("kvin", [2048, 2048], BF16).ap()
    kvag = nc.dram_tensor("kvag", [NCORE * 2048, 2048], BF16, addr_space="Shared").ap()
    kvloc = nc.dram_tensor("kvloc", [4 * 2048, 2048], BF16).ap()
    wqin = nc.dram_tensor("wqin", [1120, 1024], BF16).ap()
    wqag = nc.dram_tensor("wqag", [NCORE * 1120, 1024], BF16, addr_space="Shared").ap()

    with contextlib.ExitStack() as ex:
        if F is not None:
            ctx = F["ctx"]
            ctx.ex = ex
        else:
            ctx = Ctx(nc, ex)
        R = ctx.sb("R", [128, NT * 1024], F32)
        xqT = ctx.sb("xqT", [128, 8 * TPC], BF16)
        id_bf = ctx.sb("id_bf", [128, 128], BF16)
        bm_bf = ctx.sb("bm_bf", [128, 4 * 128], BF16)
        ones_f = ctx.sb("ones_f", [128, 64], F32)
        eps_c = ctx.sb("eps_c", [128, 1], F32)
        ctx.barscr = ctx.sb("barscr", [128, 8], F32)
        exQ = ctx.ex.enter_context(contextlib.ExitStack())
        QT = exQ.enter_context(nc.sbuf_tensor("QT", [128, NHP * TPC], BF16))
        ex.enter_context(nc.Block())

        ctx.inc(g.dma_start(out=sb3(R[:], NT, 1024), in_=ap3(x0, NT, 1024)), "ld", 16)
        ctx.inc(g.dma_start(out=id_bf[:], in_=ident_d), "ld", 16)
        ctx.inc(g.dma_start(out=sb3(bm_bf[:], 4, 128), in_=bass.AP(bmask_d.tensor, 0, [[128, 128], [128 * 128, 4], [1, 128]])), "ld", 16)
        g.memset(ones_f[:], 1.0)
        g.memset(eps_c[:], EPS)
        ctx.inc(g.nop(), "ld")

        if phases >= 1:
            exA = ctx.ex.enter_context(contextlib.ExitStack())
            xnT = exA.enter_context(nc.sbuf_tensor("xnT", [128, 8 * TPC], BF16))
            with nc.sbuf_tensor("xnf", [128, 2 * 1024], F32) as xnf, \
                 nc.sbuf_tensor("xnb", [128, 2 * 1024], BF16) as xnb, \
                 nc.sbuf_tensor("lnt", [128, 8 * NT], F32) as lnt, \
                 nc.sbuf_tensor("fq1", [128, 5 * 1024], F32) as fq1, \
                 nc.sbuf_tensor("fq2", [128, 2 * 64], F32) as fq2, \
                 nc.sbuf_tensor("wstg", [128, 2 * 1024], F32) as wstg, \
                 nc.sbuf_tensor("wstq", [128, 2 * 1024], BF16) as wstq, \
                 nc.psum_tensor("pst", [128, 2048], BF16) as pst:
                qd = dict(blk=fq2[:, 0:64], rs=fq2[:, 64:128],
                          t=fq1[:, cs(0, 1024)], u=fq1[:, cs(1, 1024)], sg=fq1[:, cs(2, 1024)],
                          a=fq1[:, cs(3, 1024)], acc=fq1[:, cs(4, 1024)])
                st = dict(sumx=lnt[:, 0:NT], sumsq=lnt[:, NT:2 * NT], m=lnt[:, 2 * NT:3 * NT],
                          msq=lnt[:, 3 * NT:4 * NT], var=lnt[:, 4 * NT:5 * NT],
                          lnv=lnt[:, 5 * NT:6 * NT], rstd=lnt[:, 6 * NT:7 * NT],
                          mrn=lnt[:, 7 * NT:8 * NT])
                v.wait_ge(ctx.S("ld"), ctx.cnt["ld"])
                s.wait_ge(ctx.S("ld"), ctx.cnt["ld"])
                for tt in range(NT):
                    ctx.inc(v.reduce_sum(out=st["sumx"][:, tt:tt + 1], in_=R[:, cs(tt, 1024)],
                                         axis=AX.X), "lsv")
                    ctx.inc(s.activation(out=fq1[:, 1024:2048], in_=R[:, cs(tt, 1024)],
                                         func=AF.Square), "lsq")
                    ctx.w(v, "lsq")
                    ctx.inc(v.reduce_sum(out=st["sumsq"][:, tt:tt + 1], in_=fq1[:, 1024:2048],
                                         axis=AX.X), "lsv")
                    ctx.w(s, "lsv")
                ctx.w(v, "lsv")
                iv = 1.0 / 1024.0
                ctx.inc(v.tensor_scalar(out=st["m"], in0=st["sumx"], scalar1=iv, scalar2=None, op0=OP.mult), "lns")
                ctx.w(v, "lns")
                ctx.inc(v.tensor_scalar(out=st["msq"], in0=st["sumsq"], scalar1=iv, scalar2=None, op0=OP.mult), "lns")
                ctx.w(v, "lns")
                ctx.inc(v.tensor_tensor(out=st["var"], in0=st["m"], in1=st["m"], op=OP.mult), "lns")
                ctx.w(v, "lns")
                ctx.inc(v.tensor_tensor(out=st["var"], in0=st["msq"], in1=st["var"], op=OP.subtract), "ln1")
                ctx.w(s, "ln1")
                ctx.inc(s.activation(out=st["lnv"], in_=st["var"], func=AF.Ln, bias=eps_c[:, 0:1]), "lnx")
                ctx.w(s, "lnx")
                ctx.inc(s.activation(out=st["rstd"], in_=st["lnv"], func=AF.Exp, scale=-0.5), "ln2")
                ctx.w(v, "ln2")
                ctx.inc(v.scalar_tensor_tensor(out=st["mrn"], in0=st["m"], scalar=-1.0,
                                               in1=st["rstd"], op0=OP.mult, op1=OP.mult), "ln3")
                ctx.w(v, "ln3")
                for tt in range(NT if p1sub >= 2 else 0):
                    sl = tt % 2
                    rst = st["rstd"][:, tt:tt + 1]
                    mrn = st["mrn"][:, tt:tt + 1]
                    if tt >= 2:
                        ctx.w(v, "fq_s", ctx.cnt.get("fq_s", 0))
                        if "xnf" in dumps:
                            ctx.w(v, "dxn", ctx.cnt.get("dxn", 0))
                    v.tensor_scalar(out=xnf[:, cs(sl, 1024)], in0=R[:, cs(tt, 1024)],
                                    scalar1=rst, scalar2=mrn, op0=OP.mult, op1=OP.add)
                    if tt >= 2:
                        ctx.w(v, "tr_v", ctx.cnt.get("tr_v", 0) - 8)
                    ctx.inc(v.tensor_scalar(out=xnb[:, cs(sl, 1024)], in0=R[:, cs(tt, 1024)],
                                            scalar1=rst, scalar2=mrn, op0=OP.mult, op1=OP.add), "xb")
                    ctx.w(t, "xb", ctx.cnt["xb"])
                    for j in range(8):
                        emit_t128(ctx, xnb[:, cs(sl, 1024)][:, cs(j, 128)],
                                  xnT[:, 2048 * j + 128 * tt:2048 * j + 128 * tt + 128], pst, id_bf)
                    if "xnf" in dumps:
                        ctx.w(y, "xb", ctx.cnt["xb"])
                        ctx.inc(y.dma_start(out=ap3(dumps["xnf"], 1, 1024, off_elems=1024 * 128 * tt),
                                            in_=xnf[:, cs(sl, 1024)].unsqueeze(1)), "dxn", 16)
                    if p1sub >= 3:
                        qi = emit_fp4_tile(ctx, v, s, xnf[:, cs(sl, 1024)], wstq[:, cs(sl, 1024)], qd, 1024)
                        ctx.inc(qi, "xqb")
                        ctx.w(t, "xqb", ctx.cnt["xqb"])
                        for j in range(8):
                            emit_t128(ctx, wstq[:, cs(sl, 1024)][:, cs(j, 128)],
                                      xqT[:, 2048 * j + 128 * tt:2048 * j + 128 * tt + 128], pst, id_bf)
                if "xqt" in dumps:
                    ctx.w(g, "tr_v")
                    ctx.inc(g.dma_start(out=dumps["xqt"], in_=xqT[:]), "dmp", 16)
                plist = []
                for j in range(3):
                    plist.append((wgs_d[128 * j:128 * (j + 1), :], 1024, 128 * j))
                for j in range(3):
                    plist.append((wus_d[128 * j:128 * (j + 1), :], 1024, 384 + 128 * j))
                for hh in range(4):
                    src = bass.AP(wds_d.tensor, 704 * hh, [[2816, 128], [1, 704]])
                    plist.append((src, 704, 1000 + hh))
                ctx.w(v, "tr_v")
                if p1sub < 4:
                    plist = []
                for pi, (srcp, wdt, rowoff) in enumerate(plist):
                    sl = pi % 2
                    so = sl * 1024
                    ctx.inc(g.dma_start(out=wstg[:, so:so + wdt], in_=srcp), "wl", 16)
                    if pi >= 2:
                        ctx.w(v, "wst", ctx.cnt.get("wst", 0) - 16)
                    v.wait_ge(ctx.S("wl"), ctx.cnt["wl"])
                    s.wait_ge(ctx.S("wl"), ctx.cnt["wl"])
                    qi = emit_fp4_tile(ctx, v, s, wstg[:, so:so + wdt], wstq[:, so:so + wdt], qd, wdt)
                    ctx.inc(qi, "wqp")
                    ctx.w(g, "wqp", ctx.cnt["wqp"])
                    if wdt == 1024:
                        dst = wqin[rowoff:rowoff + 128, :]
                    else:
                        hh = rowoff - 1000
                        dst = bass.AP(wqin.tensor, 768 * 1024 + 704 * hh, [[2816, 128], [1, 704]])
                    ctx.inc(g.dma_start(out=dst, in_=wstq[:, so:so + wdt]), "wst", 16)
                g.wait_ge(ctx.S("wst"), ctx.cnt["wst"])
                if p1sub >= 5:
                    ctx.inc(g.collective_compute("AllGather", OP.bypass, replica_groups=[list(range(NCORE))],
                                                 ins=[wqin], outs=[wqag]), "wqc")
                if "wqa" in dumps:
                    ctx.w(g, "wqc")
                    ctx.inc(g.dma_start(out=dumps["wqa"], in_=wqag[:]), "dmp", 16)
                emit_full_barrier(ctx)

        if phases >= 2:
            with nc.sbuf_tensor("wA", [128, 8 * 1024], BF16) as wA, \
                 nc.sbuf_tensor("kst", [128, 2048], BF16) as kst, \
                 nc.psum_tensor("psqA", [128, 2048], F32) as psqA, \
                 nc.psum_tensor("psqB", [128, 2048], F32) as psqB:
                ctx.inc(g.dma_start(out=sb3(wA[:], 8, 1024), in_=ap3(wq_d, 8, 1024)), "wab", 16)
                t.wait_ge(ctx.S("wab"), ctx.cnt["wab"])
                ctx.w(t, "ln3")
                ctx.w(t, "tr_v", 256)

                def evac_qt(hp, ps):
                    ctx.w(v, "pj_t", ctx.cnt.get("pj_t", 0))
                    ctx.inc(v.tensor_copy(QT[:, cs(hp, 2048)], ps[:, 0:2048]), "pj_v")

                emit_proj(ctx, wA, xnT, psqA, psqB, evac_qt, "pj")

                def evac_kv(hp, ps, base_row):
                    ctx.w(v, "pj_t", ctx.cnt.get("pj_t", 0))
                    nst = ctx.cnt.get("kvst", 0)
                    if nst >= 16:
                        ctx.w(v, "kvst", nst)
                    ctx.inc(v.tensor_copy(kst[:], ps[:, 0:2048]), "pj_v")
                    ctx.w(g, "pj_v", ctx.cnt["pj_v"])
                    ctx.inc(g.dma_start(out=kvin[base_row + 128 * hp:base_row + 128 * (hp + 1), :],
                                        in_=kst[:]), "kvst", 16)

                ctx.w(g, "pj_t", 8)
                ctx.inc(g.dma_start(out=sb3(wA[:], 8, 1024), in_=ap3(wk_d, 8, 1024)), "wab", 16)
                t.wait_ge(ctx.S("wab"), ctx.cnt["wab"])
                emit_proj(ctx, wA, xnT, psqA, psqB, lambda hp, ps: evac_kv(hp, ps, 0), "pj")
                ctx.w(g, "pj_t", 16)
                ctx.inc(g.dma_start(out=sb3(wA[:], 8, 1024), in_=ap3(wv_d, 8, 1024)), "wab", 16)
                t.wait_ge(ctx.S("wab"), ctx.cnt["wab"])
                emit_proj(ctx, wA, xnT, psqA, psqB, lambda hp, ps: evac_kv(hp, ps, 1024), "pj")
                g.wait_ge(ctx.S("kvst"), ctx.cnt["kvst"])
                ctx.inc(g.collective_compute("AllGather", OP.bypass, replica_groups=[list(range(NCORE))],
                                             ins=[kvin], outs=[kvag]), "kvc")
                ctx.w(g, "kvc")
                with g.register("roff") as roff:
                    pid = g.partition_id()
                    g.reg_mov(roff, pid)
                    g.reg_div(roff, roff, 4)
                    g.reg_mul(roff, roff, 4 * 2048 * 2048)
                    src = bass.AP(kvag.tensor, roff, [[2048, 8192], [1, 2048]])
                    ctx.inc(g.dma_start(out=kvloc, in_=src), "kvl", 16)
                g.wait_ge(ctx.S("kvl"), ctx.cnt["kvl"])
                if "kv" in dumps:
                    ctx.inc(g.dma_start(out=dumps["kv"], in_=kvloc[0:2048, :]), "dmp", 16)
                if "qt" in dumps:
                    ctx.inc(g.dma_start(out=dumps["qt"], in_=QT[:]), "dmp", 16)
                emit_full_barrier(ctx)
            exA.close()

        if phases >= 3:
            with nc.sbuf_tensor("AOT", [128, 8 * TPC], BF16) as AOT, \
                 nc.sbuf_tensor("KTl", [128, 8192], BF16) as KTl, \
                 nc.sbuf_tensor("Vl", [128, 64 * 130], BF16) as Vl, \
                 nc.sbuf_tensor("vst", [128, 256], BF16) as vst, \
                 nc.sbuf_tensor("PT", [128, 2048], BF16) as PT, \
                 nc.sbuf_tensor("rzb", [65, 2048], F32) as rzb, \
                 nc.psum_tensor("psS", [128, 2048], F32) as psS, \
                 nc.psum_tensor("psO1", [128, 1024], F32) as psO1, \
                 nc.psum_tensor("psO2", [128, 1024], F32) as psO2:
                ov = Vl[:].rearrange("p (k s) -> p k s", s=65)[:, :, 64:65]
                ctx.inc(v.memset(ov, 1.0), "vone")
                for hp in range(NHP):
                    if hp == 0:
                        ctx.w(g, "kvl")
                    else:
                        ctx.w(g, "at", ctx.cnt.get("at", 0))
                    for r in range(4):
                        ctx.inc(g.dma_start(out=KTl[:, cs(r, 2048)],
                                            in_=kvloc[2048 * r + 128 * hp:2048 * r + 128 * (hp + 1), :]),
                                "ktl", 16)
                    vt_b = ctx.cnt.get("vt_v", 0)
                    for kt in range(64):
                        r, lt = kt % 4, kt // 4
                        if kt >= 2:
                            ctx.w(y, "vt_v", vt_b + kt - 1)
                        if hp == 0 and kt == 0:
                            ctx.w(y, "kvl")
                            ctx.w(y, "vone")
                        if hp > 0 and kt == 0:
                            ctx.w(y, "hp_done", hp)
                        ctx.inc(y.dma_start_transpose(vst[:, cs(kt % 2, 128)],
                                                      kvloc[2048 * r + 1024 + 128 * hp:2048 * r + 1024 + 128 * (hp + 1),
                                                            cs(lt, 128)]), "vt_y", 16)
                        ctx.w(v, "vt_y")
                        v.tensor_copy(Vl[:, 130 * kt:130 * kt + 64], vst[:, cs(kt % 2, 128)][:, 0:64])
                        ctx.inc(v.tensor_copy(Vl[:, 130 * kt + 65:130 * kt + 129],
                                              vst[:, cs(kt % 2, 128)][:, 64:128]), "vt_v")
                    for qh in range(2):
                        grans = l0_granules(qh)
                        ng = len(grans)
                        exb = ctx.cnt.get("ex", 0)
                        atb = ctx.cnt.get("at", 0)
                        avb = ctx.cnt.get("av", 0)
                        for gi, (kt, off, wdt, msk) in enumerate(grans):
                            slot = gi % 2
                            kcol = 2048 * (kt % 4) + 128 * (kt // 4)
                            qcol = 2048 * hp + 1024 * qh + off
                            if gi >= 2:
                                ctx.w(t, "ex", exb + gi - 1)
                            if gi == 0:
                                t.wait_ge(ctx.S("ktl"), ctx.cnt["ktl"])
                                ctx.w(t, "ev", ctx.cnt.get("ev", 0))
                            t.matmul(psS[:, 1024 * slot:1024 * slot + wdt],
                                     KTl[0:64, kcol:kcol + 128], QT[0:64, qcol:qcol + wdt],
                                     start=True, stop=True, tile_position=(0, 0))
                            ctx.inc(t.matmul(psS[:, 1024 * slot + 512:1024 * slot + 512 + wdt],
                                             KTl[64:128, kcol:kcol + 128], QT[64:128, qcol:qcol + wdt],
                                             start=True, stop=True, tile_position=(64, 0)), "at")
                            ctx.w(s, "at", atb + gi + 1)
                            if gi >= 2:
                                ctx.w(s, "av", avb + gi - 1)
                            ctx.inc(s.activation(out=PT[:, 1024 * slot:1024 * slot + 512 + wdt],
                                                 in_=psS[:, 1024 * slot:1024 * slot + 512 + wdt],
                                                 func=AF.Exp, scale=SQD), "ex")
                            if msk:
                                ctx.w(v, "ex", ctx.cnt["ex"])
                                bms = bm_bf[:, cs(kt % 4, 128)]
                                v.tensor_tensor(out=PT[:, 1024 * slot:1024 * slot + 128],
                                                in0=PT[:, 1024 * slot:1024 * slot + 128], in1=bms, op=OP.mult)
                                ctx.inc(v.tensor_tensor(out=PT[:, 1024 * slot + 512:1024 * slot + 640],
                                                        in0=PT[:, 1024 * slot + 512:1024 * slot + 640],
                                                        in1=bms, op=OP.mult), "mk")
                            if msk:
                                ctx.w(t, "mk", ctx.cnt["mk"])
                            else:
                                ctx.w(t, "ex", ctx.cnt["ex"])
                            ctx.w(t, "vt_v", vt_b + kt + 1)
                            t.matmul(psO1[0:65, off:off + wdt], Vl[:, 130 * kt:130 * kt + 65],
                                     PT[:, 1024 * slot:1024 * slot + wdt],
                                     start=(kt == 0), stop=False)
                            ctx.inc(t.matmul(psO2[0:65, off:off + wdt], Vl[:, 130 * kt + 65:130 * kt + 130],
                                             PT[:, 1024 * slot + 512:1024 * slot + 512 + wdt],
                                             start=(kt == 0), stop=False), "av")
                        ctx.w(v, "av", ctx.cnt["av"])
                        v.reciprocal(out=rzb[64:65, 0:1024], in_=psO1[64:65, 0:1024])
                        ctx.inc(v.reciprocal(out=rzb[64:65, 1024:2048], in_=psO2[64:65, 0:1024]), "rzv")
                        ctx.w(t, "rzv", ctx.cnt["rzv"])
                        ctx.w(t, "ex", ctx.cnt["ex"])
                        ins = None
                        for h in range(2):
                            for c2 in range(2):
                                ins = t.matmul(psS[0:64, 1024 * h + 512 * c2:1024 * h + 512 * (c2 + 1)],
                                               ones_f[64:65, 0:64],
                                               rzb[64:65, 1024 * h + 512 * c2:1024 * h + 512 * (c2 + 1)],
                                               start=True, stop=True)
                        ctx.inc(ins, "rbt")
                        ctx.w(v, "rbt", ctx.cnt["rbt"])
                        ctx.inc(v.tensor_copy(rzb[0:64, 0:2048], psS[0:64, 0:2048]), "rzc")
                        ctx.w(v, "rzc")
                        acol = 2048 * hp + 1024 * qh
                        v.tensor_tensor(out=AOT[0:64, acol:acol + 1024], in0=psO1[0:64, 0:1024],
                                        in1=rzb[0:64, 0:1024], op=OP.mult)
                        ctx.inc(v.tensor_tensor(out=AOT[64:128, acol:acol + 1024], in0=psO2[0:64, 0:1024],
                                                in1=rzb[0:64, 1024:2048], op=OP.mult), "ev")
                    ctx.w(y, "ev", ctx.cnt.get("ev", 0))
                    ctx.inc(y.nop(), "hp_done")
                if "aot" in dumps:
                    ctx.w(g, "ev", ctx.cnt.get("ev", 0))
                    ctx.inc(g.dma_start(out=dumps["aot"], in_=AOT[:]), "dmp", 16)
                ctx.w(g, "ev", ctx.cnt.get("ev", 0))
                ctx.inc(g.dma_start(out=sb3(KTl[:], 8, 1024), in_=ap3(wo_d, 8, 1024)), "wol", 16)
                t.wait_ge(ctx.S("wol"), ctx.cnt["wol"])
                ctx.w(t, "ev", ctx.cnt.get("ev", 0))
                for tc in range(NT):
                    ps = psO1 if tc % 2 == 0 else psO2
                    nyv = ctx.cnt.get("yv", 0)
                    if nyv >= 2:
                        ctx.w(t, "yv", nyv - 1)
                    ins = None
                    for dp in range(8):
                        for c2 in range(2):
                            ins = t.matmul(ps[:, cs(c2, 512)],
                                           AOT[:, 2048 * dp + 128 * tc:2048 * dp + 128 * (tc + 1)],
                                           KTl[:, 1024 * dp + 512 * c2:1024 * dp + 512 * (c2 + 1)],
                                           start=(dp == 0), stop=(dp == 7))
                    ctx.inc(ins, "yt")
                    ctx.w(v, "yt", ctx.cnt["yt"])
                    ctx.inc(v.tensor_tensor(out=R[:, cs(tc, 1024)], in0=R[:, cs(tc, 1024)],
                                            in1=ps[:, 0:1024], op=OP.add), "yv")
                emit_full_barrier(ctx)
            exQ.close()

        if phases >= 4:
            FH = 1408
            with nc.sbuf_tensor("wgT", [128, 8 * FH], BF16) as wgT, \
                 nc.sbuf_tensor("wuT", [128, 8 * FH], BF16) as wuT, \
                 nc.sbuf_tensor("wdT", [128, 11 * 1024], BF16) as wdT, \
                 nc.sbuf_tensor("wtg", [128, 2 * 1408], BF16) as wtg, \
                 nc.sbuf_tensor("sgat", [128, FH], F32) as sgat, \
                 nc.sbuf_tensor("hbuf", [128, FH], F32) as hbuf, \
                 nc.sbuf_tensor("hq", [128, FH], BF16) as hq, \
                 nc.sbuf_tensor("hqT", [128, 11 * 128], BF16) as hqT, \
                 nc.sbuf_tensor("fq1b", [128, 3 * FH], F32) as fq1b, \
                 nc.sbuf_tensor("fq2b", [128, 2 * 88], F32) as fq2b:
                qd2 = dict(blk=fq2b[:, 0:88], rs=fq2b[:, 88:176],
                           t=sgat[:, 0:FH], u=fq1b[:, cs(0, FH)], sg=fq1b[:, cs(1, FH)],
                           a=hbuf[:, 0:FH], acc=fq1b[:, cs(2, FH)])
                g.wait_ge(ctx.S("wqc"), ctx.cnt["wqc"])
                for hf in range(2 if p4sub else 0):
                    with nc.psum_tensor("pstf%d" % hf, [128, 2048], BF16) as pstf:
                        for wi, (dstT, base) in enumerate(((wgT, 0), (wuT, 384))):
                            for jj in range(11):
                                jn = 11 * hf + jj
                                rrow = 1120 * (jn // 3) + base + 128 * (jn % 3)
                                slot = (jj + wi * 11) % 2
                                so = slot * 1408
                                if ctx.cnt.get("wtl", 0) >= 32:
                                    ctx.w(g, "wtt", ctx.cnt.get("wtt", 0) - 1)
                                ctx.inc(g.dma_start(out=wtg[:, so:so + 1024],
                                                    in_=wqag[rrow:rrow + 128, :]), "wtl", 16)
                                t.wait_ge(ctx.S("wtl"), ctx.cnt["wtl"])
                                for kp in range(8):
                                    ctx.S("tr_t"); ctx.S("tr_v")
                                    jt = ctx.cnt["tr_t"]
                                    sl2 = (jt % 2) * 1024
                                    if jt >= 2:
                                        ctx.w(t, "tr_v", jt - 1)
                                    ctx.inc(t.transpose(pstf[:, sl2:sl2 + 128],
                                                        wtg[:, so + 128 * kp:so + 128 * (kp + 1)], id_bf[:]), "tr_t")
                                    ctx.w(v, "tr_t", jt + 1)
                                    ctx.inc(v.tensor_copy(dstT[:, FH * kp + 128 * jj:FH * kp + 128 * (jj + 1)],
                                                          pstf[:, sl2:sl2 + 128]), "tr_v")
                                ctx.inc(t.nop(), "wtt")
                        for j in range(8):
                            so = (j % 2) * 1408
                            if ctx.cnt.get("wtl", 0) >= 32:
                                ctx.w(g, "wtt", ctx.cnt.get("wtt", 0) - 1)
                            src = bass.AP(wqag.tensor, (1120 * j + 768) * 1024 + FH * hf, [[2816, 128], [1, FH]])
                            ctx.inc(g.dma_start(out=wtg[:, so:so + FH], in_=src), "wtl", 16)
                            t.wait_ge(ctx.S("wtl"), ctx.cnt["wtl"])
                            for ip in range(11):
                                jt = ctx.cnt["tr_t"]
                                sl2 = (jt % 2) * 1024
                                if jt >= 2:
                                    ctx.w(t, "tr_v", jt - 1)
                                srcb = wtg[:, so + 128 * ip:so + 128 * (ip + 1)]
                                ctx.inc(t.transpose(pstf[:, sl2:sl2 + 128], srcb, id_bf[:]), "tr_t")
                                ctx.w(v, "tr_t", jt + 1)
                                ctx.inc(v.tensor_copy(wdT[:, 1024 * ip + 128 * j:1024 * ip + 128 * (j + 1)],
                                                      pstf[:, sl2:sl2 + 128]), "tr_v")
                            ctx.inc(t.nop(), "wtt")
                        ctx.w(t, "tr_v")
                    emit_full_barrier(ctx)
                    if "ffw" in dumps and hf == 0:
                        ctx.inc(g.dma_start(out=dumps["ffw"], in_=wdT[:]), "dfw", 16)
                        g.wait_ge(ctx.S("dfw"), ctx.cnt["dfw"])
                    with nc.psum_tensor("pgf%d" % hf, [128, 1536], F32) as pgf, \
                         nc.psum_tensor("puf%d" % hf, [128, 1536], F32) as puf, \
                         nc.psum_tensor("pdf%d" % hf, [128, 1024], F32) as pdf:
                        for tc in range(NT if (p4sub & 8) else 0):
                            ins = None
                            if tc > 0:
                                ctx.w(t, "sil", ctx.cnt.get("sil", 0))
                            for kp in range(8):
                                for c3, (o3, w3) in enumerate(((0, 512), (512, 512), (1024, 384))):
                                    ins = t.matmul(pgf[:, o3:o3 + w3],
                                                   xqT[:, 2048 * kp + 128 * tc:2048 * kp + 128 * (tc + 1)],
                                                   wgT[:, FH * kp + o3:FH * kp + o3 + w3],
                                                   start=(kp == 0), stop=(kp == 7))
                            ctx.inc(ins, "gt")
                            ctx.w(s, "gt", ctx.cnt["gt"])
                            if "ffg" in dumps and ctx.cnt.get("dfg", 0):
                                ctx.w(s, "dfg", ctx.cnt["dfg"])
                            ctx.inc(s.activation(out=sgat[:, 0:FH], in_=pgf[:, 0:FH], func=AF.Silu), "sil")
                            if "ffg" in dumps and hf == 0:
                                ctx.w(y, "sil", ctx.cnt["sil"])
                                ctx.inc(y.dma_start(out=dumps["ffg"][128 * tc:128 * (tc + 1), :],
                                                    in_=sgat[:, 0:FH]), "dfg", 16)
                            if not (p4sub & 16):
                                continue
                            ins = None
                            if tc > 0:
                                ctx.w(t, "hv", ctx.cnt.get("hv", 0))
                            for kp in range(8):
                                for c3, (o3, w3) in enumerate(((0, 512), (512, 512), (1024, 384))):
                                    ins = t.matmul(puf[:, o3:o3 + w3],
                                                   xqT[:, 2048 * kp + 128 * tc:2048 * kp + 128 * (tc + 1)],
                                                   wuT[:, FH * kp + o3:FH * kp + o3 + w3],
                                                   start=(kp == 0), stop=(kp == 7))
                            ctx.inc(ins, "ut")
                            ctx.w(v, "ut", ctx.cnt["ut"])
                            ctx.w(v, "sil", ctx.cnt["sil"])
                            ctx.inc(v.tensor_tensor(out=hbuf[:, 0:FH], in0=puf[:, 0:FH],
                                                    in1=sgat[:, 0:FH], op=OP.mult), "hv")
                            ctx.w(v, "hv")
                            if not (p4sub & 32):
                                continue
                            if ctx.cnt.get("hqt", 0):
                                ctx.w(v, "hqt", ctx.cnt["hqt"])
                            if "ffh" in dumps and ctx.cnt.get("dfh", 0):
                                ctx.w(v, "dfh", ctx.cnt["dfh"])
                            qi = emit_fp4_tile(ctx, v, s, hbuf[:, 0:FH], hq[:, 0:FH], qd2, FH)
                            ctx.inc(qi, "hqd")
                            if "ffh" in dumps and hf == 0:
                                ctx.w(g, "hqd", ctx.cnt["hqd"])
                                ctx.inc(g.dma_start(out=dumps["ffh"][128 * tc:128 * (tc + 1), :],
                                                    in_=hq[:, 0:FH]), "dfh", 16)
                            if not (p4sub & 64):
                                continue
                            ctx.w(y, "hqd", ctx.cnt["hqd"])
                            if tc > 0:
                                ctx.w(y, "dnt", ctx.cnt.get("dnt", 0))
                                if "fft" in dumps and ctx.cnt.get("dft", 0):
                                    ctx.w(y, "dft", ctx.cnt["dft"])
                            for ip in range(11):
                                ctx.inc(y.dma_start_transpose(hqT[:, cs(ip, 128)], hq[:, cs(ip, 128)]), "hqt", 16)
                            if "fft" in dumps and hf == 0:
                                g.wait_ge(ctx.S("hqt"), ctx.cnt["hqt"])
                                ctx.inc(g.dma_start(out=dumps["fft"][128 * tc:128 * (tc + 1), :],
                                                    in_=hqT[:, 0:1408]), "dft", 16)
                            t.wait_ge(ctx.S("hqt"), ctx.cnt["hqt"])
                            if tc > 0:
                                ctx.w(t, "rv", ctx.cnt.get("rv", 0))
                            ins = None
                            for ip in range(11):
                                for c2 in range(2):
                                    ins = t.matmul(pdf[:, cs(c2, 512)], hqT[:, cs(ip, 128)],
                                                   wdT[:, 1024 * ip + 512 * c2:1024 * ip + 512 * (c2 + 1)],
                                                   start=(ip == 0), stop=(ip == 10))
                            ctx.inc(ins, "dnt")
                            ctx.w(v, "dnt", ctx.cnt["dnt"])
                            ctx.inc(v.tensor_tensor(out=R[:, cs(tc, 1024)], in0=R[:, cs(tc, 1024)],
                                                    in1=pdf[:, 0:1024], op=OP.add), "rv")
                    emit_full_barrier(ctx)
        ctx.inc(y.dma_start(out=ap3(x1_d, NT, 1024), in_=sb3(R[:], NT, 1024)), "sto", 16)
        y.wait_ge(ctx.S("sto"), ctx.cnt["sto"])
        if dumps:
            y.wait_ge(ctx.S("dmp"), ctx.cnt.get("dmp", 0))
            if "dxn" in ctx.cnt:
                y.wait_ge(ctx.S("dxn"), ctx.cnt["dxn"])
        if F is not None:
            x1ag = nc.dram_tensor("x1ag", [NCORE * TPC, D], F32, addr_space="Shared").ap()
            x1loc = nc.dram_tensor("x1loc", [TPC, D], F32).ap()
            F["x1loc"] = x1loc
            F["ident_d"] = ident_d
            g.wait_ge(ctx.S("sto"), ctx.cnt["sto"])
            ctx.inc(g.collective_compute("AllGather", OP.bypass, replica_groups=[list(range(NCORE))],
                                         ins=[x1_d], outs=[x1ag]), "x1c")
            ctx.w(g, "x1c")
            with g.register("xoff") as xoff, g.register("xq4") as xq4:
                pid = g.partition_id()
                g.reg_mov(xq4, pid)
                g.reg_div(xq4, xq4, 4)
                g.reg_mul(xq4, xq4, 6144 * 1024)
                g.reg_mov(xoff, pid)
                g.reg_mul(xoff, xoff, 512 * 1024)
                g.reg_add(xoff, xoff, xq4)
                for p4 in range(4):
                    g.reg_add(xq4, xoff, 2048 * 1024 * p4)
                    src = bass.AP(x1ag.tensor, xq4, [[128 * 1024, 4], [16384, 8], [1, 16384]])
                    dst = bass.AP(x1loc.tensor, 128 * 1024 * p4, [[512 * 1024, 4], [16384, 8], [1, 16384]])
                    ctx.inc(g.dma_start(out=dst, in_=src), "x1l", 16)
            g.wait_ge(ctx.S("x1l"), ctx.cnt["x1l"])
            emit_full_barrier(ctx)
    return nc



def make_in_maps_l1(inputs, x1_list):
    wq1 = np.ascontiguousarray(np.asarray(inputs["L1_attn_Wq"], np.float32).T)
    wk1 = np.ascontiguousarray(np.asarray(inputs["L1_attn_Wk"], np.float32).T)
    wv1 = np.ascontiguousarray(np.asarray(inputs["L1_attn_Wv"], np.float32).T)
    table = np.asarray(inputs["L1_attn_rel_bias_table"], np.float32)
    gam = np.ascontiguousarray(np.asarray(inputs["L1_attn_ln_gamma"], np.float32).reshape(8, 128).T)
    bet = np.ascontiguousarray(np.asarray(inputs["L1_attn_ln_beta"], np.float32).reshape(8, 128).T)
    router = np.ascontiguousarray(np.asarray(inputs["L1_ffn_router_w"], np.float32))
    w0 = np.asarray(inputs["L1_ffn_w0_kernel"], np.float32)
    w1 = np.asarray(inputs["L1_ffn_w1_kernel"], np.float32)
    wo_ = np.asarray(inputs["L1_ffn_wo_kernel"], np.float32)
    ident = np.eye(128, dtype=np.float32)
    ltri = (np.arange(128)[:, None] < np.arange(128)[None, :]).astype(np.float32)
    chead = np.broadcast_to(table[4094][None, :], (128, 16)).copy()
    ii = np.arange(128)[None, None, :]
    jj = np.arange(128)[None, :, None]
    dd = np.arange(32)[:, None, None]
    if x1_list is not None:
        x1f = np.zeros((B, S, D), np.float32)
        for c in range(NCORE):
            b, p = c // 4, c % 4
            x1f[b].reshape(GT, 128, D)[_pos_tiles(p)] = np.asarray(x1_list[c]).reshape(NT, 128, D)
        x1flat = x1f.reshape(B * S, D)
    maps = []
    for c in range(NCORE):
        p = c % 4
        delta = dd + 32 - 16 * p
        idx = np.minimum(8191 - 128 * delta + ii - jj, 4094)
        blk = table[np.maximum(idx, 0), :] * (delta >= 32)[..., None]
        blk = np.ascontiguousarray(blk.transpose(3, 0, 1, 2)).astype(np.float32)
        um = (np.arange(64)[:, None] <= np.minimum(16 * p + np.arange(16)[None, :] + 31, 63)).astype(np.float32)
        mp = dict(
            wq1=wq1, wk1=wk1, wv1=wv1,
            lngam=gam, lnbet=bet, router=router,
            w0e=np.ascontiguousarray(w0[c]), w1e=np.ascontiguousarray(w1[c]),
            woe=np.ascontiguousarray(wo_[c]),
            biasblk=blk, chead=chead, umat=um, ltri=ltri, ident=ident)
        if x1_list is not None:
            mp["x1"] = np.ascontiguousarray(x1flat[2048 * c:2048 * (c + 1)])
        maps.append(mp)
    return maps


def assemble_contig(res, key):
    out = np.zeros((B * S, D), np.float32)
    for c in range(NCORE):
        out[2048 * c:2048 * (c + 1)] = np.asarray(res[c][key])
    return out.reshape(B, S, D)


class _NcSuffix:
    def __init__(self, nc, suf):
        self._nc = nc
        self._suf = suf

    def sbuf_tensor(self, name, shape, dt, **kw):
        return self._nc.sbuf_tensor(name + self._suf, shape, dt, **kw)

    def psum_tensor(self, name, shape, dt=F32, **kw):
        return self._nc.psum_tensor(name + self._suf, shape, dt, **kw)

    def __getattr__(self, a):
        return getattr(self._nc, a)


def build_l1(dump=(), phases=9, p3sub=9, F=None):
    nc = F["nc"] if F is not None else bass.Bass("TRN2", target_bir_lowering=False, debug=False)
    if F is not None:
        nc = _NcSuffix(nc, "_1")
        F["ctx"].nc = nc
    g, y, t, v, s = nc.gpsimd, nc.sync, nc.tensor, nc.vector, nc.scalar
    x1_d = F["x1loc"] if F is not None else nc.dram_tensor("x1", [TPC, D], F32, kind="ExternalInput").ap()
    wq_d = nc.dram_tensor("wq1", [D, D], F32, kind="ExternalInput").ap()
    wk_d = nc.dram_tensor("wk1", [D, D], F32, kind="ExternalInput").ap()
    wv_d = nc.dram_tensor("wv1", [D, D], F32, kind="ExternalInput").ap()
    gam_d = nc.dram_tensor("lngam", [128, 8], F32, kind="ExternalInput").ap()
    bet_d = nc.dram_tensor("lnbet", [128, 8], F32, kind="ExternalInput").ap()
    rt_d = nc.dram_tensor("router", [D, E], F32, kind="ExternalInput").ap()
    w0_d = nc.dram_tensor("w0e", [D, HMOE], F32, kind="ExternalInput").ap()
    w1_d = nc.dram_tensor("w1e", [D, HMOE], F32, kind="ExternalInput").ap()
    wo_d = nc.dram_tensor("woe", [HMOE, D], F32, kind="ExternalInput").ap()
    bb_d = nc.dram_tensor("biasblk", [NH, 32, 128, 128], F32, kind="ExternalInput").ap()
    ch_d = nc.dram_tensor("chead", [128, NH], F32, kind="ExternalInput").ap()
    um_d = nc.dram_tensor("umat", [64, 16], F32, kind="ExternalInput").ap()
    lt_d = nc.dram_tensor("ltri", [128, 128], F32, kind="ExternalInput").ap()
    ident_d = F["ident_d"] if F is not None else nc.dram_tensor("ident", [128, 128], F32, kind="ExternalInput").ap()
    out_d = nc.dram_tensor("out", [TPC, D], F32, kind="ExternalOutput").ap()
    dumps = {}
    _dsh = {"xn1": ([TPC, D], F32), "aot1": ([128, 8 * TPC], F32), "y1": ([TPC, D], F32),
            "lgt": ([TPC, E], F32), "gat": ([TPC, 16], F32), "ei": ([4096, D], F32),
            "eo": ([4096, D], F32), "ffo": ([TPC, D], F32), "elo": ([4096, D], F32)}
    for name in dump:
        dumps[name] = nc.dram_tensor("d_" + name, _dsh[name][0], _dsh[name][1],
                                     kind="ExternalOutput").ap()

    kvin = nc.dram_tensor("kvin1", [2048, 2048], BF16).ap()
    kvag = nc.dram_tensor("kvag1", [NCORE * 2048, 2048], BF16, addr_space="Shared").ap()
    kvloc = nc.dram_tensor("kvloc1", [4 * 2048, 2048], BF16).ap()
    xnin = nc.dram_tensor("xnin", [TPC, D], F32).ap()
    y1d = nc.dram_tensor("y1d", [TPC, D], F32).ap()
    eiin = nc.dram_tensor("eiin", [4096, D], BF16).ap()
    eiag = nc.dram_tensor("eiag", [NCORE * 4096, D], BF16, addr_space="Shared").ap()
    eloc = nc.dram_tensor("eloc", [4096, D], BF16).ap()
    eoin = nc.dram_tensor("eoin", [4096, D], BF16).ap()
    eoag = nc.dram_tensor("eoag", [NCORE * 4096, D], BF16, addr_space="Shared").ap()
    eocd = nc.dram_tensor("eocd", [4096, D], BF16).ap()
    w0b = nc.dram_tensor("w0b", [D, HMOE], BF16).ap()
    w1b = nc.dram_tensor("w1b", [D, HMOE], BF16).ap()
    ffout = nc.dram_tensor("ffout", [TPC, D], F32).ap()

    with contextlib.ExitStack() as ex:
        if F is not None:
            ctx = F["ctx"]
            ctx.ex = ex
        else:
            ctx = Ctx(nc, ex)
        ph1 = ["lsv", "lsq", "lns", "ln1", "lnx", "ln2", "ln3", "xnw", "xb", "xld",
               "wab", "pj_t", "pj_v", "kvst", "kvc", "kvl", "dxn"]
        attn = ["at", "ktl", "bbl", "bt", "vt_y", "vt_v", "hp_done", "ex", "av", "ev",
                "ev1", "ev2", "rzv", "rbt", "rzc", "bev", "tsm", "tse", "tst", "tsv",
                "cum", "csc", "vone"]
        mixer = ["mst", "mse", "msq", "msm", "msr", "msl", "mbc", "mbe", "ym", "ytt", "ytr", "y1w"]
        moe = ["mio", "mld", "mxf", "xgc", "mgh", "mge", "rlt", "rle", "rtt", "rtv",
               "c1m", "c1me", "c2m", "c2me", "m1c", "m1ce", "m1cb", "m1cbe", "dbd", "dsp",
               "dse", "eiw", "eic", "elc", "els", "xet", "wst2", "h0m", "h1m", "gsq",
               "gs2", "gth", "sgx", "wol2", "eom", "eoe", "eow", "eoc", "eor", "ctt",
               "ctr", "cb2", "cbd", "ffm", "ffe", "ffw", "ffc"]
        final = ["fld", "fadd", "fst"]
        for i, n in enumerate(mixer):
            ctx.alias[n] = ph1[i]
        pool_moe = attn + [p for p in ph1 if p not in ("dxn",)] + ["xp%d" % i for i in range(10)]
        for i, n in enumerate(moe):
            ctx.alias[n] = pool_moe[i]
        for i, n in enumerate(final):
            ctx.alias[n] = mixer[i]
        ctx.alias = {k: ctx.alias.get(vv, vv) for k, vv in ctx.alias.items()}
        id_bf = ctx.sb("id_bf", [128, 128], BF16)
        id_f = ctx.sb("id_f", [128, 128], F32)
        ones_f = ctx.sb("ones_f", [128, 64], F32)
        onesc_bf = ctx.sb("onesc_bf", [128, 1], BF16)
        onesr_f = ctx.sb("onesr_f", [1, 128], F32)
        eps_c = ctx.sb("eps_c", [128, 1], F32)
        ctx.barscr = ctx.sb("barscr", [128, 8], F32)
        um_bf = ctx.sb("um_bf", [64, 16], BF16)
        lt_bf = ctx.sb("lt_bf", [128, 128], BF16)
        allon_bf = ctx.sb("allon_bf", [128, 128], BF16)
        ch_sb = ctx.sb("ch_sb", [128, NH], F32)
        gam_sb = ctx.sb("gam_sb", [128, 8], F32)
        bet_sb = ctx.sb("bet_sb", [128, 8], F32)

        ctx.inc(g.dma_start(out=id_bf[:], in_=ident_d), "ld", 16)
        ctx.inc(g.dma_start(out=id_f[:], in_=ident_d), "ld", 16)
        ctx.inc(g.dma_start(out=um_bf[:], in_=um_d), "ld", 16)
        ctx.inc(g.dma_start(out=lt_bf[:], in_=lt_d), "ld", 16)
        ctx.inc(g.dma_start(out=ch_sb[:], in_=ch_d), "ld", 16)
        ctx.inc(g.dma_start(out=gam_sb[:], in_=gam_d), "ld", 16)
        ctx.inc(g.dma_start(out=bet_sb[:], in_=bet_d), "ld", 16)
        g.memset(ones_f[:], 1.0)
        g.memset(onesc_bf[:], 1.0)
        g.memset(onesr_f[:], 1.0)
        g.memset(allon_bf[:], 1.0)
        ctx.inc(g.memset(eps_c[:], EPS), "ld", 1)
        with nc.sbuf_tensor("wcst", [128, 2 * HMOE], BF16) as wcst:
            for j in range(8):
                for wi, (wsrc, wdst) in enumerate(((w0_d, w0b), (w1_d, w1b))):
                    eng = g
                    nm = "wcl" if wi == 0 else "wcl2"
                    nm2 = "wcs" if wi == 0 else "wcs2"
                    ctx.w(eng, nm2, ctx.cnt.get(nm2, 0))
                    ctx.inc(eng.dma_start(out=wcst[:, wi * HMOE:(wi + 1) * HMOE],
                                          in_=wsrc[128 * j:128 * (j + 1), :]), nm, 16)
                    eng.wait_ge(ctx.S(nm), ctx.cnt[nm])
                    ctx.inc(eng.dma_start(out=wdst[128 * j:128 * (j + 1), :],
                                          in_=wcst[:, wi * HMOE:(wi + 1) * HMOE]), nm2, 16)
            g.wait_ge(ctx.S("wcs"), ctx.cnt["wcs"])
            g.wait_ge(ctx.S("wcs2"), ctx.cnt["wcs2"])

        exA1 = ctx.ex.enter_context(contextlib.ExitStack())
        AOT = exA1.enter_context(nc.sbuf_tensor("AOT1", [128, 8 * TPC], F32))
        exQ = ctx.ex.enter_context(contextlib.ExitStack())
        QT = exQ.enter_context(nc.sbuf_tensor("QT1", [128, NHP * TPC], BF16))
        exX = ctx.ex.enter_context(contextlib.ExitStack())
        xnT = exX.enter_context(nc.sbuf_tensor("xnT1", [128, 8 * TPC], BF16))
        with nc.sbuf_tensor("xt1", [128, 2 * 1024], F32) as R1t, \
             nc.sbuf_tensor("xnf", [128, 2 * 1024], F32) as xnf, \
             nc.sbuf_tensor("xnb", [128, 2 * 1024], BF16) as xnb, \
             nc.sbuf_tensor("lnt", [128, 8 * NT], F32) as lnt, \
             nc.sbuf_tensor("scr", [128, 2048], F32) as scr, \
             nc.sbuf_tensor("wA", [128, 8 * 1024], BF16) as wA, \
             nc.sbuf_tensor("kst", [128, 2048], BF16) as kst, \
             nc.psum_tensor("psqA", [128, 2048], F32) as psqA, \
             nc.psum_tensor("psqB", [128, 2048], F32) as psqB:
            pst = psqA[:].bitcast(BF16)
            st = dict(sumx=lnt[:, 0:NT], sumsq=lnt[:, NT:2 * NT], m=lnt[:, 2 * NT:3 * NT],
                      msq=lnt[:, 3 * NT:4 * NT], var=lnt[:, 4 * NT:5 * NT],
                      lnv=lnt[:, 5 * NT:6 * NT], rstd=lnt[:, 6 * NT:7 * NT],
                      mrn=lnt[:, 7 * NT:8 * NT])
            for tt in range(NT):
                sl = tt % 2
                if tt >= 2:
                    ctx.w(g, "lsv", ctx.cnt.get("lsv", 0) - 2)
                    ctx.w(g, "lsq", ctx.cnt.get("lsq", 0) - 1)
                ctx.inc(g.dma_start(out=sb3(R1t[:, cs(sl, 1024)], 1, 1024),
                                    in_=ap3(x1_d, 1, 1024, off_elems=1024 * 128 * tt)), "xld", 16)
                v.wait_ge(ctx.S("xld"), ctx.cnt["xld"])
                s.wait_ge(ctx.S("xld"), ctx.cnt["xld"])
                ctx.inc(v.reduce_sum(out=st["sumx"][:, tt:tt + 1], in_=R1t[:, cs(sl, 1024)],
                                     axis=AX.X), "lsv")
                ctx.inc(s.activation(out=scr[:, 0:1024], in_=R1t[:, cs(sl, 1024)],
                                     func=AF.Square), "lsq")
                ctx.w(v, "lsq")
                ctx.inc(v.reduce_sum(out=st["sumsq"][:, tt:tt + 1], in_=scr[:, 0:1024],
                                     axis=AX.X), "lsv")
                ctx.w(s, "lsv")
            ctx.w(v, "lsv")
            iv = 1.0 / 1024.0
            ctx.inc(v.tensor_scalar(out=st["m"], in0=st["sumx"], scalar1=iv, scalar2=None, op0=OP.mult), "lns")
            ctx.w(v, "lns")
            ctx.inc(v.tensor_scalar(out=st["msq"], in0=st["sumsq"], scalar1=iv, scalar2=None, op0=OP.mult), "lns")
            ctx.w(v, "lns")
            ctx.inc(v.tensor_tensor(out=st["var"], in0=st["m"], in1=st["m"], op=OP.mult), "lns")
            ctx.w(v, "lns")
            ctx.inc(v.tensor_tensor(out=st["var"], in0=st["msq"], in1=st["var"], op=OP.subtract), "ln1")
            ctx.w(s, "ln1")
            ctx.inc(s.activation(out=st["lnv"], in_=st["var"], func=AF.Ln, bias=eps_c[:, 0:1]), "lnx")
            ctx.w(s, "lnx")
            ctx.inc(s.activation(out=st["rstd"], in_=st["lnv"], func=AF.Exp, scale=-0.5), "ln2")
            ctx.w(v, "ln2")
            ctx.inc(v.scalar_tensor_tensor(out=st["mrn"], in0=st["m"], scalar=-1.0,
                                           in1=st["rstd"], op0=OP.mult, op1=OP.mult), "ln3")
            ctx.w(v, "ln3")
            for tt in range(NT):
                sl = tt % 2
                rst = st["rstd"][:, tt:tt + 1]
                mrn = st["mrn"][:, tt:tt + 1]
                if tt >= 2:
                    ctx.w(g, "xb", ctx.cnt.get("xb", 0) - 1)
                ctx.inc(g.dma_start(out=sb3(R1t[:, cs(sl, 1024)], 1, 1024),
                                    in_=ap3(x1_d, 1, 1024, off_elems=1024 * 128 * tt)), "xld", 16)
                v.wait_ge(ctx.S("xld"), ctx.cnt["xld"])
                if tt >= 2:
                    ctx.w(v, "dxn", ctx.cnt.get("dxn", 0))
                ctx.inc(v.tensor_scalar(out=xnf[:, cs(sl, 1024)], in0=R1t[:, cs(sl, 1024)],
                                        scalar1=rst, scalar2=mrn, op0=OP.mult, op1=OP.add), "xnw")
                if tt >= 2:
                    ctx.w(v, "tr_v", ctx.cnt.get("tr_v", 0) - 8)
                ctx.inc(v.tensor_scalar(out=xnb[:, cs(sl, 1024)], in0=R1t[:, cs(sl, 1024)],
                                        scalar1=rst, scalar2=mrn, op0=OP.mult, op1=OP.add), "xb")
                ctx.w(g, "xnw", ctx.cnt["xnw"])
                ctx.inc(g.dma_start(out=ap3(xnin, 1, 1024, off_elems=1024 * 128 * tt),
                                    in_=xnf[:, cs(sl, 1024)].unsqueeze(1)), "dxn", 16)
                ctx.w(t, "xb", ctx.cnt["xb"])
                for j in range(8):
                    emit_t128(ctx, xnb[:, cs(sl, 1024)][:, cs(j, 128)],
                              xnT[:, 2048 * j + 128 * tt:2048 * j + 128 * tt + 128], pst, id_bf)
            g.wait_ge(ctx.S("dxn"), ctx.cnt["dxn"])
            if "xn1" in dumps:
                ctx.inc(g.dma_start(out=dumps["xn1"], in_=xnin), "dmp", 16)
            ctx.inc(g.dma_start(out=sb3(wA[:], 8, 1024), in_=ap3(wq_d, 8, 1024)), "wab", 16)
            t.wait_ge(ctx.S("wab"), ctx.cnt["wab"])
            ctx.w(t, "tr_v", ctx.cnt["tr_v"])
            pjt0 = ctx.cnt.get("pj_t", 0)

            def evac_qt(hp, ps):
                ctx.w(v, "pj_t", ctx.cnt.get("pj_t", 0))
                ctx.inc(v.tensor_copy(QT[:, cs(hp, 2048)], ps[:, 0:2048]), "pj_v")

            emit_proj(ctx, wA, xnT, psqA, psqB, evac_qt, "pj")

            def evac_kv(hp, ps, base_row):
                ctx.w(v, "pj_t", ctx.cnt.get("pj_t", 0))
                nst = ctx.cnt.get("kvst", 0)
                if nst >= 16:
                    ctx.w(v, "kvst", nst)
                ctx.inc(v.tensor_copy(kst[:], ps[:, 0:2048]), "pj_v")
                ctx.w(g, "pj_v", ctx.cnt["pj_v"])
                ctx.inc(g.dma_start(out=kvin[base_row + 128 * hp:base_row + 128 * (hp + 1), :],
                                    in_=kst[:]), "kvst", 16)

            ctx.w(g, "pj_t", pjt0 + 8)
            ctx.inc(g.dma_start(out=sb3(wA[:], 8, 1024), in_=ap3(wk_d, 8, 1024)), "wab", 16)
            t.wait_ge(ctx.S("wab"), ctx.cnt["wab"])
            emit_proj(ctx, wA, xnT, psqA, psqB, lambda hp, ps: evac_kv(hp, ps, 0), "pj")
            ctx.w(g, "pj_t", pjt0 + 16)
            ctx.inc(g.dma_start(out=sb3(wA[:], 8, 1024), in_=ap3(wv_d, 8, 1024)), "wab", 16)
            t.wait_ge(ctx.S("wab"), ctx.cnt["wab"])
            emit_proj(ctx, wA, xnT, psqA, psqB, lambda hp, ps: evac_kv(hp, ps, 1024), "pj")
            g.wait_ge(ctx.S("kvst"), ctx.cnt["kvst"])
            ctx.inc(g.collective_compute("AllGather", OP.bypass, replica_groups=[list(range(NCORE))],
                                         ins=[kvin], outs=[kvag]), "kvc")
            ctx.w(g, "kvc")
            with g.register("roff1") as roff:
                pid = g.partition_id()
                g.reg_mov(roff, pid)
                g.reg_div(roff, roff, 4)
                g.reg_mul(roff, roff, 4 * 2048 * 2048)
                src = bass.AP(kvag.tensor, roff, [[2048, 8192], [1, 2048]])
                ctx.inc(g.dma_start(out=kvloc, in_=src), "kvl", 16)
            g.wait_ge(ctx.S("kvl"), ctx.cnt["kvl"])
            emit_full_barrier(ctx)
        exX.close()

        if phases >= 2:
            exAtt = ctx.ex.enter_context(contextlib.ExitStack())
            KTl = exAtt.enter_context(nc.sbuf_tensor("KTl", [128, 8192], BF16))
            Vl = exAtt.enter_context(nc.sbuf_tensor("Vl", [128, 64 * 130], BF16))
            vst = exAtt.enter_context(nc.sbuf_tensor("vst", [128, 256], BF16))
            PT = exAtt.enter_context(nc.sbuf_tensor("PT", [128, 2048], BF16))
            rzb = exAtt.enter_context(nc.sbuf_tensor("rzb", [65, 2048], F32))
            bb_bf = exAtt.enter_context(nc.sbuf_tensor("bb_bf", [128, 2 * 32 * 128], BF16))
            bias_sb = exAtt.enter_context(nc.sbuf_tensor("bias_sb", [64, 2048], F32))
            ts_sb = exAtt.enter_context(nc.sbuf_tensor("ts_sb", [64, 128], F32))
            tst_bf = exAtt.enter_context(nc.sbuf_tensor("tst_bf", [64, 128], BF16))
            cumnsc = exAtt.enter_context(nc.sbuf_tensor("cumnsc", [64, 32], F32))
            tmp_sb = exAtt.enter_context(nc.sbuf_tensor("tmp_sb", [64, 2048], F32))
            psS = exAtt.enter_context(nc.psum_tensor("psS", [128, 2048], F32))
            psO1 = exAtt.enter_context(nc.psum_tensor("psO1", [128, 1024], F32))
            psO2 = exAtt.enter_context(nc.psum_tensor("psO2", [128, 1024], F32))
            if True:
                ov = Vl[:].rearrange("p (k s) -> p k s", s=65)[:, :, 64:65]
                ctx.inc(v.memset(ov, 1.0), "vone")
                for hp in range(NHP):
                    if hp == 0:
                        ctx.w(g, "kvl")
                    else:
                        ctx.w(g, "at", ctx.cnt.get("at", 0))
                    for r in range(4):
                        ctx.inc(g.dma_start(out=KTl[:, cs(r, 2048)],
                                            in_=kvloc[2048 * r + 128 * hp:2048 * r + 128 * (hp + 1), :]),
                                "ktl", 16)
                    if hp > 0:
                        ctx.w(g, "bt", ctx.cnt.get("bt", 0))
                    for hh in range(2):
                        bbsrc = bass.AP(bb_d.tensor, (2 * hp + hh) * 32 * 128 * 128,
                                        [[128, 128], [128 * 128, 32], [1, 128]])
                        ctx.inc(g.dma_start(out=sb3(bb_bf[:, 4096 * hh:4096 * (hh + 1)], 32, 128),
                                            in_=bbsrc), "bbl", 16)
                    vt_b = ctx.cnt.get("vt_v", 0)
                    for kt in range(64):
                        r, lt = kt // 16, kt % 16
                        if kt >= 2:
                            ctx.w(y, "vt_v", vt_b + kt - 1)
                        if hp == 0 and kt == 0:
                            ctx.w(y, "kvl")
                            ctx.w(y, "vone")
                        if hp > 0 and kt == 0:
                            ctx.w(y, "hp_done", ctx.cnt.get("hp_done", 0))
                        ctx.inc(y.dma_start_transpose(vst[:, cs(kt % 2, 128)],
                                                      kvloc[2048 * r + 1024 + 128 * hp:2048 * r + 1024 + 128 * (hp + 1),
                                                            cs(lt, 128)]), "vt_y", 16)
                        ctx.w(v, "vt_y")
                        v.tensor_copy(Vl[:, 130 * kt:130 * kt + 64], vst[:, cs(kt % 2, 128)][:, 0:64])
                        ctx.inc(v.tensor_copy(Vl[:, 130 * kt + 65:130 * kt + 129],
                                              vst[:, cs(kt % 2, 128)][:, 64:128]), "vt_v")
                    for qh in range(2):
                        exb = ctx.cnt.get("ex", 0)
                        atb = ctx.cnt.get("at", 0)
                        avb = ctx.cnt.get("av", 0)
                        gi = 0
                        for kt in range(64):
                            kcol = 2048 * (kt // 16) + 128 * (kt % 16)
                            for qc in range(2):
                                off = 512 * qc
                                qcol = 2048 * hp + 1024 * qh + off
                                slot = gi % 2
                                if gi >= 2:
                                    ctx.w(t, "ex", exb + gi - 1)
                                if gi == 0:
                                    t.wait_ge(ctx.S("ktl"), ctx.cnt["ktl"])
                                    ctx.w(t, "ev", ctx.cnt.get("ev", 0))
                                t.matmul(psS[:, 1024 * slot:1024 * slot + 512],
                                         KTl[0:64, kcol:kcol + 128], QT[0:64, qcol:qcol + 512],
                                         start=True, stop=True, tile_position=(0, 0))
                                ctx.inc(t.matmul(psS[:, 1024 * slot + 512:1024 * slot + 1024],
                                                 KTl[64:128, kcol:kcol + 128], QT[64:128, qcol:qcol + 512],
                                                 start=True, stop=True, tile_position=(64, 0)), "at")
                                ctx.w(s, "at", atb + gi + 1)
                                if gi >= 2:
                                    ctx.w(s, "av", avb + gi - 1)
                                ctx.inc(s.activation(out=PT[:, 1024 * slot:1024 * slot + 1024],
                                                     in_=psS[:, 1024 * slot:1024 * slot + 1024],
                                                     func=AF.Exp, scale=SC1), "ex")
                                ctx.w(t, "ex", ctx.cnt["ex"])
                                ctx.w(t, "vt_v", vt_b + kt + 1)
                                t.matmul(psO1[0:65, off:off + 512], Vl[:, 130 * kt:130 * kt + 65],
                                         PT[:, 1024 * slot:1024 * slot + 512],
                                         start=(kt == 0), stop=False)
                                ctx.inc(t.matmul(psO2[0:65, off:off + 512], Vl[:, 130 * kt + 65:130 * kt + 130],
                                                 PT[:, 1024 * slot + 512:1024 * slot + 1024],
                                                 start=(kt == 0), stop=False), "av")
                                gi += 1
                        ctx.w(t, "ex", ctx.cnt["ex"])
                        ctx.w(t, "av", ctx.cnt["av"])
                        t.wait_ge(ctx.S("bbl"), ctx.cnt["bbl"])
                        if True:
                            tl0, tl1 = (0, 8) if qh == 0 else (8, 16)
                            for hh in range(2):
                                ins = None
                                for tt in range(tl0, tl1):
                                    for ss in range(tt + 32, 64):
                                        dpr = ss - tt - 32
                                        ins = t.matmul(psS[0:64, 1024 * hh + 128 * (tt - tl0):1024 * hh + 128 * (tt - tl0 + 1)],
                                                       Vl[:, 130 * ss + 65 * hh:130 * ss + 65 * hh + 64],
                                                       bb_bf[:, 4096 * hh + 128 * dpr:4096 * hh + 128 * (dpr + 1)],
                                                       start=(ss == tt + 32), stop=(ss == 63))
                                ctx.inc(ins, "bt")
                            ctx.w(v, "bt", ctx.cnt["bt"])
                            ctx.inc(v.tensor_copy(bias_sb[0:64, 0:2048], psS[0:64, 0:2048]), "bev")
                            ctx.w(t, "bev", ctx.cnt["bev"])
                            ins = None
                            for hh in range(2):
                                for kt in range(64):
                                    ins = t.matmul(psS[0:64, 1024 * hh + kt:1024 * hh + kt + 1],
                                                   Vl[:, 130 * kt + 65 * hh:130 * kt + 65 * hh + 64],
                                                   onesc_bf[:], start=True, stop=True)
                            ctx.inc(ins, "tsm")
                            ctx.w(v, "tsm", ctx.cnt["tsm"])
                            v.tensor_copy(ts_sb[0:64, 0:64], psS[0:64, 0:64])
                            ctx.inc(v.tensor_copy(ts_sb[0:64, 64:128], psS[0:64, 1024:1088]), "tse")
                            ctx.w(t, "tse", ctx.cnt["tse"])
                            for hh in range(2):
                                ctx.inc(t.transpose(psS[0:64, 0:64],
                                                    ts_sb[0:64, 64 * hh:64 * (hh + 1)], id_f[0:64, 0:64]), "tst")
                                ctx.w(v, "tst", ctx.cnt["tst"])
                                ctx.inc(v.tensor_copy(tst_bf[0:64, 64 * hh:64 * (hh + 1)], psS[0:64, 0:64]), "tsv")
                                ctx.w(t, "tsv", ctx.cnt["tsv"])
                            for hh in range(2):
                                ctx.inc(t.matmul(psS[0:64, 128 * hh:128 * hh + 16],
                                                 tst_bf[0:64, 64 * hh:64 * (hh + 1)], um_bf[:],
                                                 start=True, stop=True), "cum")
                            ctx.w(v, "cum", ctx.cnt["cum"])
                            for hh in range(2):
                                hgl = 2 * hp + hh
                                ctx.inc(v.tensor_scalar(out=cumnsc[0:64, 16 * hh:16 * (hh + 1)],
                                                        in0=psS[0:64, 128 * hh:128 * hh + 16],
                                                        scalar1=ch_sb[0:64, hgl:hgl + 1], scalar2=None,
                                                        op0=OP.mult), "csc")
                            ctx.w(v, "csc", ctx.cnt["csc"])
                        ctx.w(v, "av", ctx.cnt["av"])
                        v.reciprocal(out=rzb[64:65, 0:1024], in_=psO1[64:65, 0:1024])
                        ctx.inc(v.reciprocal(out=rzb[64:65, 1024:2048], in_=psO2[64:65, 0:1024]), "rzv")
                        ctx.w(t, "rzv", ctx.cnt["rzv"])
                        ins = None
                        for hh in range(2):
                            for c2 in range(2):
                                ins = t.matmul(psS[0:64, 1024 * hh + 512 * c2:1024 * hh + 512 * (c2 + 1)],
                                               ones_f[64:65, 0:64],
                                               rzb[64:65, 1024 * hh + 512 * c2:1024 * hh + 512 * (c2 + 1)],
                                               start=True, stop=True)
                        ctx.inc(ins, "rbt")
                        ctx.w(v, "rbt", ctx.cnt["rbt"])
                        ctx.inc(v.tensor_copy(rzb[0:64, 0:2048], psS[0:64, 0:2048]), "rzc")
                        ctx.w(v, "rzc")
                        acol = 2048 * hp + 1024 * qh
                        for hh, psO in ((0, psO1), (1, psO2)):
                            ctx.inc(v.tensor_tensor(out=tmp_sb[0:64, 1024 * hh:1024 * (hh + 1)],
                                                    in0=psO[0:64, 0:1024],
                                                    in1=rzb[0:64, 1024 * hh:1024 * (hh + 1)], op=OP.mult), "ev1")
                            ctx.w(v, "ev1")
                            ctx.inc(v.tensor_tensor(out=tmp_sb[0:64, 1024 * hh:1024 * (hh + 1)],
                                                    in0=tmp_sb[0:64, 1024 * hh:1024 * (hh + 1)],
                                                    in1=bias_sb[0:64, 1024 * hh:1024 * (hh + 1)],
                                                    op=OP.add), "ev1")
                            ctx.w(v, "ev1")
                            for tl in range(8):
                                tg = 8 * qh + tl
                                ctx.inc(v.tensor_scalar(
                                    out=AOT[64 * hh:64 * (hh + 1), acol + 128 * tl:acol + 128 * (tl + 1)],
                                    in0=tmp_sb[0:64, 1024 * hh + 128 * tl:1024 * hh + 128 * (tl + 1)],
                                    scalar1=cumnsc[0:64, 16 * hh + tg:16 * hh + tg + 1], scalar2=None,
                                    op0=OP.add), "ev2")
                        ctx.inc(v.tensor_copy(ctx.barscr[:, 6:7], ctx.barscr[:, 7:8]), "ev")
                    ctx.w(y, "ev", ctx.cnt.get("ev", 0))
                    ctx.inc(y.nop(), "hp_done")
                if "aot1" in dumps:
                    ctx.w(g, "ev", ctx.cnt.get("ev", 0))
                    ctx.inc(g.dma_start(out=dumps["aot1"], in_=AOT[:]), "dmp", 16)
            emit_full_barrier(ctx)
            exAtt.close()
            exQ.close()
        if phases >= 3:
            class _SkipRest(Exception):
                pass
            exMix = ctx.ex.enter_context(contextlib.ExitStack())
            sqs = exMix.enter_context(nc.sbuf_tensor("sqs", [128, 2048], F32))
            sm0 = exMix.enter_context(nc.sbuf_tensor("sm0", [1, 2048], F32))
            sm1 = exMix.enter_context(nc.sbuf_tensor("sm1", [1, 2048], F32))
            sm2 = exMix.enter_context(nc.sbuf_tensor("sm2", [1, 2048], F32))
            sm3 = exMix.enter_context(nc.sbuf_tensor("sm3", [1, 2048], F32))
            bcst = exMix.enter_context(nc.sbuf_tensor("bcst", [128, 2 * 2048], F32))
            y1T = exMix.enter_context(nc.sbuf_tensor("y1T", [128, 2048], BF16))
            y1f = exMix.enter_context(nc.sbuf_tensor("y1f", [128, NT * 1024], BF16))
            psm = exMix.enter_context(nc.psum_tensor("psm", [128, 2048], F32))
            psty = exMix.enter_context(nc.psum_tensor("psty", [128, 2048], BF16))
            if True:
                try:
                    if p3sub < 1:
                        raise _SkipRest
                    onescol = ones_f[0:128, 0:1]
                    ins = None
                    for dp in range(8):
                        for c4 in range(4):
                            ins = t.matmul(psm[0:1, cs(c4, 512)], onescol,
                                           AOT[:, 2048 * dp + 512 * c4:2048 * dp + 512 * (c4 + 1)],
                                           start=(dp == 0), stop=(dp == 7))
                    ctx.inc(ins, "mst")
                    ctx.w(v, "mst", ctx.cnt["mst"])
                    ctx.inc(v.tensor_copy(sm0[0:1, :], psm[0:1, 0:2048]), "mse")
                    if p3sub < 2:
                        raise _SkipRest
                    ctx.w(t, "mse", ctx.cnt["mse"])
                    for dp in range(8):
                        ctx.inc(s.activation(out=sqs[:, 0:2048], in_=AOT[:, cs(dp, 2048)], func=AF.Square), "msq")
                        t.wait_ge(ctx.S("msq"), ctx.cnt["msq"])
                        ins = None
                        for c4 in range(4):
                            ins = t.matmul(psm[0:1, cs(c4, 512)], onescol, sqs[:, cs(c4, 512)],
                                           start=(dp == 0), stop=(dp == 7))
                        ctx.inc(ins, "msm")
                        ctx.w(s, "msm", ctx.cnt["msm"])
                    ctx.w(v, "msm", ctx.cnt["msm"])
                    ctx.inc(v.tensor_copy(sm1[0:1, :], psm[0:1, 0:2048]), "mse")
                    ctx.w(v, "mse")
                    iv = 1.0 / 1024.0
                    ctx.inc(v.tensor_scalar(out=sm0[0:1, :], in0=sm0[0:1, :], scalar1=iv, scalar2=None, op0=OP.mult), "msr")
                    ctx.w(v, "msr")
                    ctx.inc(v.tensor_scalar(out=sm1[0:1, :], in0=sm1[0:1, :], scalar1=iv, scalar2=None, op0=OP.mult), "msr")
                    ctx.w(v, "msr")
                    ctx.inc(v.tensor_tensor(out=sm2[0:1, :], in0=sm0[0:1, :], in1=sm0[0:1, :], op=OP.mult), "msr")
                    ctx.w(v, "msr")
                    ctx.inc(v.tensor_tensor(out=sm1[0:1, :], in0=sm1[0:1, :], in1=sm2[0:1, :], op=OP.subtract), "msr")
                    ctx.w(s, "msr", ctx.cnt["msr"])
                    ctx.inc(s.activation(out=sm2[0:1, :], in_=sm1[0:1, :], func=AF.Ln, bias=eps_c[0:1, 0:1]), "msl")
                    ctx.w(s, "msl")
                    ctx.inc(s.activation(out=sm2[0:1, :], in_=sm2[0:1, :], func=AF.Exp, scale=-0.5), "msl")
                    ctx.w(v, "msl", ctx.cnt["msl"])
                    ctx.inc(v.tensor_tensor(out=sm3[0:1, :], in0=sm0[0:1, :], in1=sm2[0:1, :], op=OP.mult), "msr")
                    ctx.w(t, "msr", ctx.cnt["msr"])
                    ctx.w(t, "mse", ctx.cnt["mse"])
                    if p3sub < 3:
                        raise _SkipRest
                    for k, src_row in ((0, sm2), (1, sm3)):
                        ins = None
                        for c4 in range(4):
                            ins = t.matmul(psm[:, cs(c4, 512)], onesr_f[0:1, :],
                                           src_row[0:1, cs(c4, 512)], start=True, stop=True)
                        ctx.inc(ins, "mbc")
                        ctx.w(v, "mbc", ctx.cnt["mbc"])
                        ctx.inc(v.tensor_copy(bcst[:, cs(k, 2048)], psm[:, 0:2048]), "mbe")
                        ctx.w(t, "mbe", ctx.cnt["mbe"])
                    if p3sub < 4:
                        raise _SkipRest
                    ctx.S("ytt"); ctx.S("ytr")
                    ytt0 = ctx.cnt.get("ytt", 0)
                    ytr0 = ctx.cnt.get("ytr", 0)
                    for dp in range(8 if p3sub >= 7 else 1):
                        ctx.w(v, "mbe", ctx.cnt["mbe"])
                        if dp > 0:
                            ctx.w(v, "ytr", ctx.cnt.get("ytr", 0))
                        ctx.inc(v.tensor_tensor(out=sqs[:, 0:2048], in0=AOT[:, cs(dp, 2048)],
                                                in1=bcst[:, 0:2048], op=OP.mult), "ym")
                        ctx.w(v, "ym")
                        ctx.inc(v.tensor_tensor(out=sqs[:, 0:2048], in0=sqs[:, 0:2048],
                                                in1=bcst[:, 2048:4096], op=OP.subtract), "ym")
                        ctx.w(v, "ym")
                        ctx.inc(v.tensor_scalar(out=y1T[:, 0:2048], in0=sqs[:, 0:2048],
                                                scalar1=gam_sb[:, dp:dp + 1], scalar2=bet_sb[:, dp:dp + 1],
                                                op0=OP.mult, op1=OP.add), "ym")
                        ctx.w(t, "ym", ctx.cnt["ym"])
                        for tc in range(NT if p3sub >= 5 else 0):
                            jt = ctx.cnt["ytt"] - ytt0
                            sl2 = (jt % 2) * 1024
                            if jt >= 2:
                                ctx.w(t, "ytr", ytr0 + jt - 1)
                            ctx.inc(t.transpose(psty[:, sl2:sl2 + 128], y1T[:, cs(tc, 128)], id_bf[:]), "ytt")
                            ctx.w(v, "ytt", ytt0 + jt + 1)
                            ctx.inc(v.tensor_copy(y1f[:, 1024 * tc + 128 * dp:1024 * tc + 128 * (dp + 1)],
                                                  psty[:, sl2:sl2 + 128]), "ytr")
                    if p3sub < 6:
                        raise _SkipRest
                    ctx.w(g, "ytr", ctx.cnt.get("ytr", 0))
                    ctx.inc(g.dma_start(out=ap3(y1d, NT, 1024), in_=sb3(y1f[:], NT, 1024)), "y1w", 16)
                    g.wait_ge(ctx.S("y1w"), ctx.cnt["y1w"])
                    if "y1" in dumps:
                        ctx.inc(g.dma_start(out=dumps["y1"], in_=y1d), "dmp", 16)
                except _SkipRest:
                    pass
                finally:
                    emit_full_barrier(ctx)
            exMix.close()
            exA1.close()

        if phases >= 4:
            def hsv(instr, name="mgh"):
                ctx.inc(instr, name)
                ctx.w(v, name)
                return instr

            exMoe = ctx.ex.enter_context(contextlib.ExitStack())
            iotaq = exMoe.enter_context(nc.sbuf_tensor("iotaq", [128, 4096], F32))
            gq = exMoe.enter_context(nc.sbuf_tensor("gq", [128, 4 * 16], F32))
            m1f = exMoe.enter_context(nc.sbuf_tensor("m1f", [128, 16 * 8], F32))
            m2f = exMoe.enter_context(nc.sbuf_tensor("m2f", [128, 16 * 8], F32))
            m1b = exMoe.enter_context(nc.sbuf_tensor("m1b", [128, 16 * 8], BF16))
            m2b = exMoe.enter_context(nc.sbuf_tensor("m2b", [128, 16 * 8], BF16))
            m1mb = exMoe.enter_context(nc.sbuf_tensor("m1mb", [128, 16 * 8], BF16))
            gidxf2 = exMoe.enter_context(nc.sbuf_tensor("gidxf2", [128, 32], F32))
            gscu = exMoe.enter_context(nc.sbuf_tensor("gscu", [128, 8], U32))
            iota8 = exMoe.enter_context(nc.sbuf_tensor("iota8", [128, 8], F32))
            gsc = exMoe.enter_context(nc.sbuf_tensor("gsc", [128, 24], F32))
            if True:
                q1c, g1c = gq[:, 0:16], gq[:, 16:32]
                q2c, g2c = gq[:, 32:48], gq[:, 48:64]
                ctx.inc(g.iota(iotaq[:], [[1, 4096]], channel_multiplier=0,
                               allow_small_or_imprecise_dtypes=True), "mio")
                ctx.inc(g.iota(iota8[:], [[1, 8]], channel_multiplier=0,
                               allow_small_or_imprecise_dtypes=True), "mio")
                g.wait_ge(ctx.S("dxn"), ctx.cnt["dxn"])
                exD = ctx.ex.enter_context(contextlib.ExitStack())
                xng = exD.enter_context(nc.sbuf_tensor("xng", [128, NT * 1024], BF16))
                exDsp = ctx.ex.enter_context(contextlib.ExitStack())
                xtf = exDsp.enter_context(nc.sbuf_tensor("xtf", [128, 2 * 1024], F32))
                xngT = exDsp.enter_context(nc.sbuf_tensor("xngT", [128, 8 * 2048], F32))
                rt_sb = exDsp.enter_context(nc.sbuf_tensor("rt_sb", [128, 64], F32))
                lg_sb = exDsp.enter_context(nc.sbuf_tensor("lg_sb", [8, 2048], F32))
                lgt = exDsp.enter_context(nc.sbuf_tensor("lgt", [128, 16 * 8], F32))
                exs = exDsp.enter_context(nc.sbuf_tensor("exs", [128, 16 * 8], F32))
                p1f = exDsp.enter_context(nc.sbuf_tensor("p1f", [128, 16 * 8], F32))
                p2f = exDsp.enter_context(nc.sbuf_tensor("p2f", [128, 16 * 8], F32))
                m1cb = exDsp.enter_context(nc.sbuf_tensor("m1cb", [128, 8], F32))
                Dh = exDsp.enter_context(nc.sbuf_tensor("Dh", [128, 8 * 4096], BF16))
                eist = exDsp.enter_context(nc.sbuf_tensor("eist", [128, 2 * 1024], BF16))
                psg = exDsp.enter_context(nc.psum_tensor("psg", [128, 512], F32))
                psgf = exDsp.enter_context(nc.psum_tensor("psgf", [128, 2048], F32))
                psd1 = psgf[:, 0:1024]
                psd2 = psgf[:, 1024:2048]
                if True:
                    ctx.inc(g.dma_start(out=sb3(rt_sb[:], 8, 8), in_=ap3(rt_d, 8, 8)), "mld", 16)
                    for k in range(NT):
                        sl = k % 2
                        if k >= 2:
                            ctx.w(g, "tr_v", ctx.cnt.get("tr_v", 0) - 8)
                            ctx.w(g, "xgc", ctx.cnt.get("xgc", 0) - 1)
                        ctx.inc(g.dma_start(out=sb3(xtf[:, cs(sl, 1024)], 1, 1024),
                                            in_=ap3(xnin, 1, 1024, off_elems=1024 * 128 * k)), "mxf", 16)
                        t.wait_ge(ctx.S("mxf"), ctx.cnt["mxf"])
                        v.wait_ge(ctx.S("mxf"), ctx.cnt["mxf"])
                        ctx.inc(v.tensor_copy(xng[:, cs(k, 1024)], xtf[:, cs(sl, 1024)]), "xgc")
                        for j in range(8):
                            ctx.S("tr_t"); ctx.S("tr_v")
                            jt = ctx.cnt["tr_t"]
                            sl2 = (jt % 2) * 512
                            if jt >= 2:
                                ctx.w(t, "tr_v", jt - 1)
                            ctx.inc(t.transpose(psgf[:, sl2:sl2 + 128],
                                                xtf[:, cs(sl, 1024)][:, cs(j, 128)], id_f[:]), "tr_t")
                            ctx.w(v, "tr_t", jt + 1)
                            ctx.inc(v.tensor_copy(xngT[:, 2048 * j + 128 * k:2048 * j + 128 * (k + 1)],
                                                  psgf[:, sl2:sl2 + 128]), "tr_v")
                    ctx.w(t, "tr_v")
                    t.wait_ge(ctx.S("mld"), ctx.cnt["mld"])
                    ins = None
                    for mp in range(8):
                        for c4 in range(4):
                            ins = t.matmul(psgf[0:8, cs(c4, 512)], rt_sb[:, 8 * mp:8 * (mp + 1)],
                                           xngT[:, 2048 * mp + 512 * c4:2048 * mp + 512 * (c4 + 1)],
                                           start=(mp == 0), stop=(mp == 7))
                    ctx.inc(ins, "rlt")
                    ctx.w(v, "rlt", ctx.cnt["rlt"])
                    ctx.inc(v.tensor_copy(lg_sb[0:8, :], psgf[0:8, 0:2048]), "rle")
                    ctx.w(t, "rle", ctx.cnt["rle"])
                    for tt in range(NT):
                        ctx.inc(t.transpose(psg[0:128, 0:8], lg_sb[0:8, cs(tt, 128)], id_f[0:8, 0:8]), "rtt")
                        ctx.w(v, "rtt", ctx.cnt["rtt"])
                        ctx.inc(v.tensor_copy(lgt[:, cs(tt, 8)], psg[0:128, 0:8]), "rtv")
                        ctx.w(t, "rtv", ctx.cnt["rtv"])
                    if "lgt" in dumps:
                        ctx.w(g, "rtv", ctx.cnt["rtv"])
                        ctx.inc(g.dma_start(out=bass.AP(dumps["lgt"].tensor, 0, [[8, 128], [128 * 8, 16], [1, 8]]),
                                            in_=sb3(lgt[:], 16, 8)), "dmp", 16)
                    for tt in range(NT):
                        L = lgt[:, cs(tt, 8)]
                        hsv(v.max(gsc[:, 0:8], L))
                        hsv(v.tensor_scalar(out=gsc[:, 8:9], in0=gsc[:, 0:1], scalar1=-1.0, scalar2=None, op0=OP.mult))
                        ctx.w(s, "mgh", ctx.cnt["mgh"])
                        ctx.inc(s.activation(out=exs[:, cs(tt, 8)], in_=L, func=AF.Exp, bias=gsc[:, 8:9]), "mge")
                        ctx.w(v, "mge")
                        EXT = exs[:, cs(tt, 8)]
                        hsv(v.reduce_sum(out=gsc[:, 9:10], in_=EXT, axis=AX.X))
                        hsv(v.reciprocal(out=gsc[:, 10:11], in_=gsc[:, 9:10]))
                        hsv(v.max(gsc[:, 12:20], EXT))
                        hsv(v.max_index(gscu[:, 0:8], gsc[:, 12:20], EXT))
                        hsv(v.tensor_copy(gidxf2[:, 2 * tt:2 * tt + 2], gscu[:, 0:2]))
                        hsv(v.tensor_tensor(out=g1c[:, tt:tt + 1], in0=gsc[:, 12:13], in1=gsc[:, 10:11], op=OP.mult))
                        hsv(v.tensor_tensor(out=g2c[:, tt:tt + 1], in0=gsc[:, 13:14], in1=gsc[:, 10:11], op=OP.mult))
                        hsv(v.tensor_scalar(out=m1f[:, cs(tt, 8)], in0=iota8[:], scalar1=gidxf2[:, 2 * tt:2 * tt + 1],
                                            scalar2=None, op0=OP.is_equal))
                        hsv(v.tensor_scalar(out=m2f[:, cs(tt, 8)], in0=iota8[:], scalar1=gidxf2[:, 2 * tt + 1:2 * tt + 2],
                                            scalar2=None, op0=OP.is_equal))
                        hsv(v.tensor_copy(m1b[:, cs(tt, 8)], m1f[:, cs(tt, 8)]))
                        hsv(v.tensor_copy(m2b[:, cs(tt, 8)], m2f[:, cs(tt, 8)]))
                    ctx.w(t, "mgh", ctx.cnt["mgh"])
                    for (mb, pf, nm) in ((m1b, p1f, "c1m"), (m2b, p2f, "c2m")):
                        for tt in range(NT):
                            ins = None
                            for tp in range(tt + 1):
                                ins = t.matmul(psg[0:128, 0:8],
                                               allon_bf[:] if tp < tt else lt_bf[:],
                                               mb[:, cs(tp, 8)], start=(tp == 0), stop=(tp == tt))
                            ctx.inc(ins, nm)
                            ctx.w(v, nm, ctx.cnt[nm])
                            ctx.inc(v.tensor_copy(pf[:, cs(tt, 8)], psg[0:128, 0:8]), nm + "e")
                            ctx.w(t, nm + "e", ctx.cnt[nm + "e"])
                    ctx.w(v, "c2me", ctx.cnt["c2me"])
                    for tt in range(NT):
                        hsv(v.tensor_scalar(out=gsc[:, 0:8], in0=p1f[:, cs(tt, 8)], scalar1=float(CAP),
                                            scalar2=None, op0=OP.is_lt))
                        hsv(v.tensor_tensor(out=m1f[:, cs(tt, 8)], in0=m1f[:, cs(tt, 8)], in1=gsc[:, 0:8], op=OP.mult))
                        hsv(v.tensor_copy(m1mb[:, cs(tt, 8)], m1f[:, cs(tt, 8)]))
                        hsv(v.tensor_tensor(out=gsc[:, 0:8], in0=p1f[:, cs(tt, 8)], in1=m1f[:, cs(tt, 8)], op=OP.mult))
                        hsv(v.reduce_sum(out=gsc[:, 8:9], in_=gsc[:, 0:8], axis=AX.X))
                        hsv(v.reduce_sum(out=gsc[:, 9:10], in_=m1f[:, cs(tt, 8)], axis=AX.X))
                        hsv(v.tensor_scalar(out=gsc[:, 10:11], in0=gidxf2[:, 2 * tt:2 * tt + 1], scalar1=512.0,
                                            scalar2=None, op0=OP.mult))
                        hsv(v.tensor_tensor(out=gsc[:, 10:11], in0=gsc[:, 10:11], in1=gsc[:, 8:9], op=OP.add))
                        hsv(v.tensor_scalar(out=gsc[:, 11:12], in0=gsc[:, 9:10], scalar1=-1e6,
                                            scalar2=1e6, op0=OP.mult, op1=OP.add))
                        hsv(v.tensor_tensor(out=q1c[:, tt:tt + 1], in0=gsc[:, 10:11], in1=gsc[:, 11:12], op=OP.add))
                        hsv(v.tensor_tensor(out=g1c[:, tt:tt + 1], in0=g1c[:, tt:tt + 1], in1=gsc[:, 9:10], op=OP.mult))
                    ctx.w(t, "mgh", ctx.cnt["mgh"])
                    ins = None
                    for tt in range(NT):
                        ins = t.matmul(psg[0:1, 0:8], onesc_bf[:], m1mb[:, cs(tt, 8)],
                                       start=(tt == 0), stop=(tt == NT - 1))
                    ctx.inc(ins, "m1c")
                    ctx.w(v, "m1c", ctx.cnt["m1c"])
                    ctx.inc(v.tensor_copy(gsc[0:1, 16:24][0:1, 0:8], psg[0:1, 0:8]), "m1ce")
                    ctx.w(t, "m1ce", ctx.cnt["m1ce"])
                    ctx.inc(t.matmul(psg[0:128, 8:16], onesr_f[0:1, :], gsc[0:1, 16:24][0:1, 0:8],
                                     start=True, stop=True), "m1cb")
                    ctx.w(v, "m1cb", ctx.cnt["m1cb"])
                    ctx.inc(v.tensor_copy(m1cb[:], psg[0:128, 8:16]), "m1cbe")
                    ctx.w(v, "m1cbe")
                    for tt in range(NT):
                        hsv(v.tensor_tensor(out=p2f[:, cs(tt, 8)], in0=p2f[:, cs(tt, 8)], in1=m1cb[:], op=OP.add))
                        hsv(v.tensor_scalar(out=gsc[:, 0:8], in0=p2f[:, cs(tt, 8)], scalar1=float(CAP),
                                            scalar2=None, op0=OP.is_lt))
                        hsv(v.tensor_tensor(out=m2f[:, cs(tt, 8)], in0=m2f[:, cs(tt, 8)], in1=gsc[:, 0:8], op=OP.mult))
                        hsv(v.tensor_tensor(out=gsc[:, 0:8], in0=p2f[:, cs(tt, 8)], in1=m2f[:, cs(tt, 8)], op=OP.mult))
                        hsv(v.reduce_sum(out=gsc[:, 8:9], in_=gsc[:, 0:8], axis=AX.X))
                        hsv(v.reduce_sum(out=gsc[:, 9:10], in_=m2f[:, cs(tt, 8)], axis=AX.X))
                        hsv(v.tensor_scalar(out=gsc[:, 10:11], in0=gidxf2[:, 2 * tt + 1:2 * tt + 2], scalar1=512.0,
                                            scalar2=None, op0=OP.mult))
                        hsv(v.tensor_tensor(out=gsc[:, 10:11], in0=gsc[:, 10:11], in1=gsc[:, 8:9], op=OP.add))
                        hsv(v.tensor_scalar(out=gsc[:, 11:12], in0=gsc[:, 9:10], scalar1=-1e6,
                                            scalar2=1e6, op0=OP.mult, op1=OP.add))
                        hsv(v.tensor_tensor(out=q2c[:, tt:tt + 1], in0=gsc[:, 10:11], in1=gsc[:, 11:12], op=OP.add))
                        hsv(v.tensor_tensor(out=g2c[:, tt:tt + 1], in0=g2c[:, tt:tt + 1], in1=gsc[:, 9:10], op=OP.mult))
                    if "gat" in dumps:
                        ctx.w(g, "mgh", ctx.cnt["mgh"])
                        for kk in range(4):
                            ctx.inc(g.dma_start(out=bass.AP(dumps["gat"].tensor, kk * 128 * 16, [[16, 128], [1, 16]]),
                                                in_=gq[:, 16 * kk:16 * (kk + 1)]), "dmp", 16)
                    for half in range(2):
                        for k8 in range(8):
                            tt = 8 * half + k8
                            if ctx.cnt.get("dsp", 0):
                                ctx.w(v, "dsp", ctx.cnt["dsp"])
                            hsv(v.tensor_scalar(out=Dh[:, cs(k8, 4096)], in0=iotaq[:], scalar1=q1c[:, tt:tt + 1],
                                                scalar2=None, op0=OP.is_equal), "mgh")
                            ctx.inc(v.scalar_tensor_tensor(out=Dh[:, cs(k8, 4096)], in0=iotaq[:],
                                                           scalar=q2c[:, tt:tt + 1], in1=Dh[:, cs(k8, 4096)],
                                                           op0=OP.is_equal, op1=OP.add), "dbd")
                            ctx.w(v, "dbd")
                        ctx.w(t, "dbd", ctx.cnt["dbd"])
                        for qt in range(32):
                            ps = psd1 if qt % 2 == 0 else psd2
                            nev = ctx.cnt.get("dse", 0)
                            if nev >= 2:
                                ctx.w(t, "dse", nev - 1)
                            ins = None
                            for k8 in range(8):
                                for c2 in range(2):
                                    ins = t.matmul(ps[:, cs(c2, 512)],
                                                   Dh[:, 4096 * k8 + 128 * qt:4096 * k8 + 128 * (qt + 1)],
                                                   xng[:, 1024 * (8 * half + k8) + 512 * c2:1024 * (8 * half + k8) + 512 * (c2 + 1)],
                                                   start=(k8 == 0), stop=(k8 == 7))
                            ctx.inc(ins, "dsp")
                            ctx.w(v, "dsp", ctx.cnt["dsp"])
                            nst = ctx.cnt.get("eiw", 0)
                            if nst >= 32:
                                ctx.w(v, "eiw", nst - 16)
                            ctx.inc(v.tensor_copy(eist[:, cs(qt % 2, 1024)], ps[:, 0:1024]), "dse")
                            ctx.w(g, "dse", ctx.cnt["dse"])
                            if half == 0:
                                ctx.inc(g.dma_start(out=eiin[128 * qt:128 * (qt + 1), :],
                                                    in_=eist[:, cs(qt % 2, 1024)]), "eiw", 16)
                            else:
                                ctx.inc(g.dma_start(out=eiin[128 * qt:128 * (qt + 1), :],
                                                    in_=eist[:, cs(qt % 2, 1024)], accum_op=OP.add), "eiw", 16)
                    g.wait_ge(ctx.S("eiw"), ctx.cnt["eiw"])
                    ctx.inc(g.collective_compute("AllGather", OP.bypass, replica_groups=[list(range(NCORE))],
                                                 ins=[eiin], outs=[eiag]), "eic")
                    if "ei" in dumps:
                        ctx.w(g, "eic")
                        ctx.inc(g.dma_start(out=dumps["ei"], in_=eiin), "dmp", 16)
                    emit_full_barrier(ctx)
                    exDsp.close()
                exD.close()

                NHC = 43
                exExp = ctx.ex.enter_context(contextlib.ExitStack())
                XeT = exExp.enter_context(nc.sbuf_tensor("XeT", [128, 2 * 8 * 512], BF16))
                hsT = exExp.enter_context(nc.sbuf_tensor("hsT", [128, NHC * 512], BF16))
                w01 = exExp.enter_context(nc.sbuf_tensor("w01", [128, 4 * 1024], BF16))
                wos = exExp.enter_context(nc.sbuf_tensor("wos", [128, 43 * 1024], BF16))
                gsx = exExp.enter_context(nc.sbuf_tensor("gsx", [128, 5 * 512], F32))
                eos = exExp.enter_context(nc.sbuf_tensor("eos", [128, 2 * 1024], BF16))
                ph0 = exExp.enter_context(nc.psum_tensor("ph0", [128, 1024], F32))
                ph1 = exExp.enter_context(nc.psum_tensor("ph1", [128, 1024], F32))
                pso1 = exExp.enter_context(nc.psum_tensor("pso1", [128, 1024], F32))
                pso2 = exExp.enter_context(nc.psum_tensor("pso2", [128, 1024], F32))
                if True:
                    ctx.w(g, "eic")
                    with g.register("ebase") as ebase:
                        pid = g.partition_id()
                        g.reg_mov(ebase, pid)
                        g.reg_mul(ebase, ebase, 512 * 1024)
                        src = bass.AP(eiag.tensor, ebase,
                                      [[4096 * 1024, NCORE], [16384, 32], [1, 16384]])
                        dst = bass.AP(eloc.tensor, 0,
                                      [[512 * 1024, NCORE], [16384, 32], [1, 16384]])
                        ctx.inc(g.dma_start(out=dst, in_=src), "elc", 16)
                    g.wait_ge(ctx.S("elc"), ctx.cnt["elc"])
                    y.wait_ge(ctx.S("elc"), ctx.cnt["elc"])
                    for hc in range(NHC):
                        hw = 88 if hc == NHC - 1 else 128
                        ctx.inc(g.dma_start(out=wos[:, cs(hc, 1024)][0:hw, :] if hw < 128 else wos[:, cs(hc, 1024)],
                                            in_=wo_d[128 * hc:128 * hc + hw, :]), "wol2", 16)
                    if "elo" in dumps:
                        with nc.sbuf_tensor("elstg", [128, 1024], BF16) as elstg:
                            for j in range(32):
                                ctx.w(g, "dmp", ctx.cnt.get("dmp", 0))
                                ctx.inc(g.dma_start(out=elstg[:], in_=eloc[128 * j:128 * (j + 1), :]), "dmp", 16)
                                g.wait_ge(ctx.S("dmp"), ctx.cnt["dmp"])
                                ctx.inc(g.dma_start(out=dumps["elo"][128 * j:128 * (j + 1), :], in_=elstg[:]), "dmp", 16)
                            g.wait_ge(ctx.S("dmp"), ctx.cnt["dmp"])
                    for sc in range(8):
                        ssl = sc % 2
                        if sc >= 2:
                            ctx.w(y, "h1m", ctx.cnt.get("h1m", 0) - NHC)
                        for sb in range(4):
                            for mb in range(8):
                                ctx.inc(y.dma_start_transpose(
                                    XeT[:, 4096 * ssl + 512 * mb + 128 * sb:4096 * ssl + 512 * mb + 128 * (sb + 1)],
                                    eloc[512 * sc + 128 * sb:512 * sc + 128 * (sb + 1), 128 * mb:128 * (mb + 1)]),
                                    "xet", 16)
                        t.wait_ge(ctx.S("xet"), ctx.cnt["xet"])
                        for hc in range(NHC):
                            hw = 88 if hc == NHC - 1 else 128
                            wsl = hc % 2
                            nwl = ctx.cnt.get("wst2", 0)
                            if nwl >= 64:
                                ctx.w(g, "h1m", ctx.cnt.get("h1m", 0) - 1)
                            w0src = bass.AP(w0b.tensor, 128 * hc, [[HMOE, 128], [128 * HMOE, 8], [1, hw]])
                            w1src = bass.AP(w1b.tensor, 128 * hc, [[HMOE, 128], [128 * HMOE, 8], [1, hw]])
                            ctx.inc(g.dma_start(out=sb3(w01[:, cs(2 * wsl, 1024)], 8, 128)[:, :, 0:hw], in_=w0src), "wst2", 16)
                            ctx.inc(g.dma_start(out=sb3(w01[:, cs(2 * wsl + 1, 1024)], 8, 128)[:, :, 0:hw], in_=w1src), "wst2", 16)
                            t.wait_ge(ctx.S("wst2"), ctx.cnt["wst2"])
                            phA = ph0 if hc % 2 == 0 else ph1
                            nsg = ctx.cnt.get("sgx", 0)
                            if nsg >= 2:
                                ctx.w(t, "sgx", nsg - 1)
                            ins0, ins1 = None, None
                            for mp in range(8):
                                ins0 = t.matmul(phA[0:hw, 0:512],
                                                w01[:, 1024 * (2 * wsl) + 128 * mp:1024 * (2 * wsl) + 128 * mp + hw],
                                                XeT[:, 4096 * (sc % 2) + 512 * mp:4096 * (sc % 2) + 512 * (mp + 1)],
                                                start=(mp == 0), stop=(mp == 7))
                            ctx.inc(ins0, "h0m")
                            for mp in range(8):
                                ins1 = t.matmul(phA[0:hw, 512:1024],
                                                w01[:, 1024 * (2 * wsl + 1) + 128 * mp:1024 * (2 * wsl + 1) + 128 * mp + hw],
                                                XeT[:, 4096 * (sc % 2) + 512 * mp:4096 * (sc % 2) + 512 * (mp + 1)],
                                                start=(mp == 0), stop=(mp == 7))
                            ctx.inc(ins1, "h1m")
                            ctx.w(s, "h1m", ctx.cnt["h1m"])
                            ctx.inc(s.activation(out=gsx[0:hw, 0:512], in_=phA[0:hw, 0:512], func=AF.Square), "gsq")
                            ctx.w(v, "gsq")
                            hsv(v.tensor_scalar(out=gsx[0:hw, 512:1024], in0=gsx[0:hw, 0:512],
                                                scalar1=0.044715 * 0.7978845608028654, scalar2=0.7978845608028654,
                                                op0=OP.mult, op1=OP.add))
                            hsv(v.tensor_tensor(out=gsx[0:hw, 1024:1536], in0=gsx[0:hw, 512:1024],
                                                in1=phA[0:hw, 0:512], op=OP.mult))
                            ctx.inc(v.tensor_copy(ctx.barscr[:, 6:7], ctx.barscr[:, 7:8]), "gs2")
                            ctx.w(s, "gs2", ctx.cnt["gs2"])
                            ctx.inc(s.activation(out=gsx[0:hw, 1536:2048], in_=gsx[0:hw, 1024:1536], func=AF.Tanh), "gth")
                            ctx.w(v, "gth")
                            hsv(v.tensor_scalar(out=gsx[0:hw, 1536:2048], in0=gsx[0:hw, 1536:2048],
                                                scalar1=1.0, scalar2=0.5, op0=OP.add, op1=OP.mult))
                            hsv(v.tensor_tensor(out=gsx[0:hw, 2048:2560], in0=gsx[0:hw, 1536:2048],
                                                in1=phA[0:hw, 0:512], op=OP.mult))
                            ctx.inc(v.tensor_tensor(out=hsT[0:hw, cs(hc, 512)], in0=gsx[0:hw, 2048:2560],
                                                    in1=phA[0:hw, 512:1024], op=OP.mult), "sgx")
                            ctx.w(v, "sgx")
                        t.wait_ge(ctx.S("wol2"), ctx.cnt["wol2"])
                        ctx.w(t, "sgx", ctx.cnt["sgx"])
                        for st4 in range(4):
                            pso = pso1 if st4 % 2 == 0 else pso2
                            nev = ctx.cnt.get("eoe", 0)
                            if nev >= 2:
                                ctx.w(t, "eoe", nev - 1)
                            ins = None
                            for hc in range(NHC):
                                hw = 88 if hc == NHC - 1 else 128
                                for c2 in range(2):
                                    ins = t.matmul(pso[:, cs(c2, 512)],
                                                   hsT[0:hw, 512 * hc + 128 * st4:512 * hc + 128 * (st4 + 1)],
                                                   wos[0:hw, 1024 * hc + 512 * c2:1024 * hc + 512 * (c2 + 1)],
                                                   start=(hc == 0), stop=(hc == NHC - 1))
                            ctx.inc(ins, "eom")
                            ctx.w(v, "eom", ctx.cnt["eom"])
                            nei = ctx.cnt.get("eow", 0)
                            if nei >= 32:
                                ctx.w(v, "eow", nei - 16)
                            ctx.inc(v.tensor_copy(eos[:, cs(st4 % 2, 1024)], pso[:, 0:1024]), "eoe")
                            ctx.w(g, "eoe", ctx.cnt["eoe"])
                            ctx.inc(g.dma_start(out=eoin[512 * sc + 128 * st4:512 * sc + 128 * (st4 + 1), :],
                                                in_=eos[:, cs(st4 % 2, 1024)]), "eow", 16)
                    g.wait_ge(ctx.S("eow"), ctx.cnt["eow"])
                    ctx.inc(g.collective_compute("AllGather", OP.bypass, replica_groups=[list(range(NCORE))],
                                                 ins=[eoin], outs=[eoag]), "eoc")
                    if "eo" in dumps:
                        ctx.w(g, "eoc")
                        with nc.sbuf_tensor("eostg", [128, 1024], BF16) as eostg:
                            for j in range(32):
                                ctx.w(g, "dmp", ctx.cnt.get("dmp", 0))
                                ctx.inc(g.dma_start(out=eostg[:], in_=eoin[128 * j:128 * (j + 1), :]), "dmp", 16)
                                g.wait_ge(ctx.S("dmp"), ctx.cnt["dmp"])
                                ctx.inc(g.dma_start(out=dumps["eo"][128 * j:128 * (j + 1), :], in_=eostg[:]), "dmp", 16)
                            g.wait_ge(ctx.S("dmp"), ctx.cnt["dmp"])
                    emit_full_barrier(ctx)

                    exExp.close()
                exCmb = ctx.ex.enter_context(contextlib.ExitStack())
                eosb = exCmb.enter_context(nc.sbuf_tensor("eosb", [128, 32 * 1024], BF16))
                cmbr = exCmb.enter_context(nc.sbuf_tensor("cmb", [128, 4096], BF16))
                cmbT = exCmb.enter_context(nc.sbuf_tensor("cmbT", [128, 4096], BF16))
                ffst = exCmb.enter_context(nc.sbuf_tensor("ffst", [128, 2 * 1024], F32))
                psct = exCmb.enter_context(nc.psum_tensor("psct", [128, 2048], BF16))
                psc1 = exCmb.enter_context(nc.psum_tensor("psc1", [128, 1024], F32))
                psc2 = exCmb.enter_context(nc.psum_tensor("psc2", [128, 1024], F32))
                if True:
                    ctx.w(g, "eoc")
                    with g.register("obase") as obase:
                        pid = g.partition_id()
                        g.reg_mov(obase, pid)
                        g.reg_mul(obase, obase, 512 * 1024)
                        src = bass.AP(eoag.tensor, obase,
                                      [[4096 * 1024, NCORE], [16384, 32], [1, 16384]])
                        dst = bass.AP(eocd.tensor, 0,
                                      [[512 * 1024, NCORE], [16384, 32], [1, 16384]])
                        ctx.inc(g.dma_start(out=dst, in_=src), "eor", 16)
                    g.wait_ge(ctx.S("eor"), ctx.cnt["eor"])
                    for ee in range(NCORE):
                        src = bass.AP(eocd.tensor, 512 * 1024 * ee,
                                      [[1024, 128], [128 * 1024, 4], [1, 1024]])
                        ctx.inc(g.dma_start(out=sb3(eosb[:, 1024 * 4 * ee:1024 * 4 * (ee + 1)], 4, 1024),
                                            in_=src), "eor", 16)
                    t.wait_ge(ctx.S("eor"), ctx.cnt["eor"])
                    for st in range(NT):
                        nts = ctx.cnt.get("ctt", 0)
                        if nts:
                            ctx.w(v, "ctt", nts)
                        hsv(v.tensor_scalar(out=cmbr[:, 0:4096], in0=iotaq[:], scalar1=q1c[:, st:st + 1],
                                            scalar2=g1c[:, st:st + 1], op0=OP.is_equal, op1=OP.mult))
                        ctx.inc(v.tensor_scalar(out=cmbT[:, 0:4096], in0=iotaq[:], scalar1=q2c[:, st:st + 1],
                                                scalar2=g2c[:, st:st + 1], op0=OP.is_equal, op1=OP.mult), "cb2")
                        ctx.w(v, "cb2")
                        ctx.inc(v.tensor_tensor(out=cmbr[:, 0:4096], in0=cmbr[:, 0:4096], in1=cmbT[:, 0:4096],
                                                op=OP.add), "cbd")
                        ctx.w(t, "cbd", ctx.cnt["cbd"])
                        for qt in range(32):
                            jt = ctx.cnt.get("ctr", 0)
                            sl2 = (jt % 2) * 1024
                            if jt >= 2:
                                ctx.w(t, "ctt", jt - 1)
                            ctx.inc(t.transpose(psct[:, sl2:sl2 + 128], cmbr[:, cs(qt, 128)], id_bf[:]), "ctr")
                            ctx.w(v, "ctr", jt + 1)
                            ctx.inc(v.tensor_copy(cmbT[:, cs(qt, 128)], psct[:, sl2:sl2 + 128]), "ctt")
                        ctx.w(t, "ctt", ctx.cnt["ctt"])
                        ps = psc1 if st % 2 == 0 else psc2
                        nev = ctx.cnt.get("ffe", 0)
                        if nev >= 2:
                            ctx.w(t, "ffe", nev - 1)
                        ins = None
                        for qt in range(32):
                            for c2 in range(2):
                                ins = t.matmul(ps[:, cs(c2, 512)], cmbT[:, cs(qt, 128)],
                                               eosb[:, 1024 * qt + 512 * c2:1024 * qt + 512 * (c2 + 1)],
                                               start=(qt == 0), stop=(qt == 31))
                        ctx.inc(ins, "ffm")
                        ctx.w(v, "ffm", ctx.cnt["ffm"])
                        nfw = ctx.cnt.get("ffw", 0)
                        if nfw >= 32:
                            ctx.w(v, "ffw", nfw - 16)
                        ctx.inc(v.tensor_copy(ffst[:, cs(st % 2, 1024)], ps[:, 0:1024]), "ffe")
                        ctx.w(g, "ffe", ctx.cnt["ffe"])
                        ctx.inc(g.dma_start(out=ap3(ffout, 1, 1024, off_elems=1024 * 128 * st),
                                            in_=ffst[:, cs(st % 2, 1024)].unsqueeze(1)), "ffw", 16)
                    g.wait_ge(ctx.S("ffw"), ctx.cnt["ffw"])
                    ctx.inc(g.nop(), "ffc")
                    if "ffo" in dumps:
                        ctx.inc(g.dma_start(out=dumps["ffo"], in_=ffout), "dmp", 16)
                    emit_full_barrier(ctx)

                    exCmb.close()
            exMoe.close()
        with nc.sbuf_tensor("fr", [128, 3 * 1024], F32) as fr, \
             nc.sbuf_tensor("fo", [128, 2 * 1024], F32) as fo:
            ctx.w(g, "ffc")
            for tc in range(NT):
                if tc >= 1:
                    ctx.w(g, "fadd", ctx.cnt.get("fadd", 0))
                if tc >= 2:
                    ctx.w(g, "fst", ctx.cnt.get("fst", 0) - 16)
                ctx.inc(g.dma_start(out=sb3(fr[:, 0:1024], 1, 1024),
                                    in_=ap3(x1_d, 1, 1024, off_elems=1024 * 128 * tc)), "fld", 16)
                ctx.inc(g.dma_start(out=sb3(fr[:, 1024:2048], 1, 1024),
                                    in_=ap3(y1d, 1, 1024, off_elems=1024 * 128 * tc)), "fld", 16)
                ctx.inc(g.dma_start(out=sb3(fr[:, 2048:3072], 1, 1024),
                                    in_=ap3(ffout, 1, 1024, off_elems=1024 * 128 * tc)), "fld", 16)
                v.wait_ge(ctx.S("fld"), ctx.cnt["fld"])
                sl = tc % 2
                ctx.inc(v.tensor_tensor(out=fo[:, cs(sl, 1024)], in0=fr[:, 0:1024],
                                        in1=fr[:, 1024:2048], op=OP.add), "fadd")
                ctx.w(v, "fadd")
                ctx.inc(v.tensor_tensor(out=fo[:, cs(sl, 1024)], in0=fo[:, cs(sl, 1024)],
                                        in1=fr[:, 2048:3072], op=OP.add), "fadd")
                ctx.w(g, "fadd", ctx.cnt["fadd"])
                ctx.inc(g.dma_start(out=ap3(out_d, 1, 1024, off_elems=1024 * 128 * tc),
                                    in_=fo[:, cs(sl, 1024)].unsqueeze(1)), "fst", 16)
            g.wait_ge(ctx.S("fst"), ctx.cnt["fst"])
            if dumps:
                g.wait_ge(ctx.S("dmp"), ctx.cnt.get("dmp", 0))
    return nc

def build_fused():
    nc = bass.Bass("TRN2", target_bir_lowering=False, debug=False)
    with contextlib.ExitStack() as ex0:
        ctx = Ctx(nc, ex0)
        F = dict(nc=nc, ctx=ctx)
        build_l0(F=F)
        build_l1(F=F)
    return nc


def make_in_maps_fused(inputs):
    m0 = make_in_maps_l0(inputs)
    m1 = make_in_maps_l1(inputs, None)
    return [dict(m0[c], **m1[c]) for c in range(NCORE)]


def kernel(**inputs):
    maps = make_in_maps_fused(inputs)
    if "f" not in _BASS_CACHE:
        _BASS_CACHE["f"] = build_fused()
    res = run_bass_kernel_spmd(_BASS_CACHE["f"], maps, core_ids=list(range(NCORE)))
    return assemble_contig(res.results, "out")


def kernel_2launch(**inputs):
    maps0 = make_in_maps_l0(inputs)
    if "l0" not in _BASS_CACHE:
        _BASS_CACHE["l0"] = build_l0()
    res0 = run_bass_kernel_spmd(_BASS_CACHE["l0"], maps0, core_ids=list(range(NCORE)))
    x1_list = [np.asarray(r["x1"]) for r in res0.results]
    maps1 = make_in_maps_l1(inputs, x1_list)
    if "l1" not in _BASS_CACHE:
        _BASS_CACHE["l1"] = build_l1()
    res1 = run_bass_kernel_spmd(_BASS_CACHE["l1"], maps1, core_ids=list(range(NCORE)))
    return assemble_contig(res1.results, "out")
